# version 04
# baseline (speedup 1.0000x reference)
import os

import jax

jax.config.update("jax_compilation_cache_dir",
                  os.path.join(os.path.dirname(os.path.abspath(__file__)), "_jaxcache"))
jax.config.update("jax_persistent_cache_min_compile_time_secs", 0.0)
jax.config.update("jax_persistent_cache_min_entry_size_bytes", 0)

import numpy as np
from contextlib import ExitStack

import concourse.bass as bass
import concourse.mybir as mybir
from concourse.bass_utils import run_bass_kernel_spmd
from concourse.bass_core_rust import IndirectOffsetOnAxis

F32 = mybir.dt.float32
BF16 = mybir.dt.bfloat16
I32 = mybir.dt.int32
AF = mybir.ActivationFunctionType
ALU = mybir.AluOpType
AX = mybir.AxisListType

D = 1024
TL = 2048
NT = 16
NCORES = 8
S = 4096
FF = 1408
E = 16
CLOC = 448
CTOT = E * CLOC
CAP = 2560
NH = 16
HD = 64
DC = 8
FC = 11
TB = 4
QB = 512
NKT = S // 128
EXP_BIAS = -20.0

DEV_PHASES = None


class Sem:
    __slots__ = ("h", "n", "name")

    def __init__(self, h, name):
        self.h = h
        self.n = 0
        self.name = name


class Prog:
    ENGINES = ("tensor", "vector", "scalar", "gpsimd", "sync")

    def __init__(self, nc):
        self.nc = nc
        self.q = {e: [] for e in self.ENGINES}
        self.sems = []
        self.max_wait = {}

    def sem(self, name):
        s = Sem(self.nc.alloc_semaphore(name), name)
        self.sems.append(s)
        return s

    def emit(self, eng, f, waits=(), incs=()):
        ws = []
        for item in waits:
            if isinstance(item, Sem):
                s, v = item, item.n
            else:
                s, v = item
            assert v <= s.n, f"wait {s.name}>={v} but only {s.n} issued"
            self.max_wait[s.name] = max(self.max_wait.get(s.name, 0), v)
            if v > 0:
                ws.append((s.h, v))
        ics = []
        for item in incs:
            if isinstance(item, tuple):
                s, k = item
            else:
                s, k = item, 1
            s.n += k
            ics.append((s.h, k))

        def run(e, f=f, ws=ws, ics=ics):
            for (h, v) in ws:
                e.wait_ge(h, v)
            inst = f(e)
            for (h, k) in ics:
                inst.then_inc(h, k)

        self.q[eng].append(run)

    def finalize(self):
        with self.nc.Block() as block:
            for eng in self.ENGINES:
                items = self.q[eng]

                def body(e, items=items):
                    for f in items:
                        f(e)

                getattr(block, eng)(body)


def _host_constants():
    c = {}
    c["ident"] = np.eye(128, dtype=np.float32)
    c["ones_col"] = np.ones((128, 1), np.float32)
    c["ones_row"] = np.ones((1, 128), np.float32)
    c["lstrict"] = np.triu(np.ones((128, 128), np.float32), 1)
    c["lstrict16"] = np.triu(np.ones((16, 16), np.float32), 1)
    c["iota16"] = np.tile(np.arange(16, dtype=np.float32), (128, 1))
    return c


def build_program(mode="L1"):
    nc = bass.Bass("TRN2", target_bir_lowering=False)
    P = Prog(nc)
    es = ExitStack()

    t_in = {}
    def ext(name, shape, dtype=F32):
        t_in[name] = nc.dram_tensor(name, list(shape), dtype, kind="ExternalInput").ap()
        return t_in[name]

    x_fm = ext("x_fm", [D, TL])
    tilemask = ext("tilemask", [128, NT])
    if mode in ("L2", "L4"):
        i_cntall = ext("i_cntall", [NCORES * NT, E])
    if mode == "L3":
        i_agk = ext("i_agk", [NCORES * D, TL], BF16)
        i_agv = ext("i_agv", [NCORES * TL, D], BF16)
    bqv_row = ext("bqv_row", [1, D])
    ktp = ext("ktp", [D, S])
    vp = ext("vp", [S, D])
    gamma = ext("gamma", [128, DC])
    beta = ext("beta", [128, DC])
    wqkT = ext("wqkT", [D, 3 * D])
    bqkv = ext("bqkv", [128, 24])
    woutT = ext("woutT", [D, D])
    bout = ext("bout", [128, DC])
    rw = ext("rw", [D, E])
    rb = ext("rb", [1, E])
    swg = ext("swg", [D, FF])
    swu = ext("swu", [D, FF])
    swd = ext("swd", [FF, D])
    wg = ext("wg", [E, D, FF])
    wu = ext("wu", [E, D, FF])
    wd = ext("wd", [E, FF, D])

    if mode in ("L4", "D_attn_a", "D_ffn1"):
        out_tok = nc.dram_tensor("out_tok", [TL, D], F32, kind="ExternalOutput").ap()
    else:
        x_out = nc.dram_tensor("x_out", [D, TL], F32, kind="ExternalOutput").ap()
    if mode in ("L1", "L3"):
        o_cnt = nc.dram_tensor("o_cnt", [NT, E], F32, kind="ExternalOutput").ap()
    if mode == "L2":
        o_kt = nc.dram_tensor("o_kt", [D, TL], BF16, kind="ExternalOutput").ap()
        o_v = nc.dram_tensor("o_v", [TL, D], BF16, kind="ExternalOutput").ap()

    hfm_d = nc.dram_tensor("hfm_d", [D, TL], BF16, kind="Internal").ap()
    exp_in = nc.dram_tensor("exp_in", [CTOT + 128, D], BF16, kind="Internal").ap()
    y_tok = nc.dram_tensor("y_tok", [CTOT + 128, D], BF16, kind="Internal").ap()
    z_tok = nc.dram_tensor("z_tok", [TL, D], BF16, kind="Internal").ap()
    cnt_in = [nc.dram_tensor(f"cnt_in{i}", [NT, E], F32, kind="Internal").ap() for i in range(2)]
    cnt_out = [nc.dram_tensor(f"cnt_out{i}", [NCORES * NT, E], F32, kind="Internal").ap() for i in range(2)]
    bounce = nc.dram_tensor("bounce", [4, 256], F32, kind="Internal").ap()
    agk_in = nc.dram_tensor("agk_in", [D, TL], BF16, kind="Internal").ap()
    agk_out = nc.dram_tensor("agk_out", [NCORES * D, TL], BF16, kind="Internal").ap()
    agv_in = nc.dram_tensor("agv_in", [TL, D], BF16, kind="Internal").ap()
    agv_out = nc.dram_tensor("agv_out", [NCORES * TL, D], BF16, kind="Internal").ap()

    HC = _host_constants()
    ident_f = nc.inline_tensor(HC["ident"], "identf").ap()
    consts_np = {
        "identb": HC["ident"],
        "ones_colb": HC["ones_col"],
        "lstrictb": HC["lstrict"],
    }

    sb = lambda name, shape, dt: es.enter_context(nc.sbuf_tensor(name, list(shape), dt))
    X = sb("X", [128, DC * TL], F32)
    HB = sb("HB", [128, DC * TL], BF16)
    WB = sb("WB", [128, 16384], BF16)
    SROW = sb("SROW", [128, 4096], BF16)
    SCB = sb("SCB", [128, 512], F32)
    XT1 = sb("XT1", [128, 512], F32)
    PR = sb("PR", [128, 4096], BF16)
    YS = sb("YS", [128, 2048], BF16)
    OS = sb("OS", [128, 1024], F32)
    WS = [sb(f"WS{i}", [128, 3072], BF16) for i in range(2)]
    VS = [sb(f"VS{i}", [128, 1024], BF16) for i in range(2)]
    KTILE = [sb(f"KT{i}", [128, 4096], BF16) for i in range(2)]
    VTILE = [sb(f"VT{i}", [128, 2080], BF16) for i in range(2)]
    IDB = sb("IDB", [128, 128], BF16)
    IDF = sb("IDF", [128, 128], F32)
    ONEB = sb("ONEB", [128, 128], BF16)
    ONEF = sb("ONEF", [128, 128], F32)
    CONSTS = sb("CONSTS", [128, 8], F32)
    LSTR = sb("LSTR", [128, 128], F32)
    LSTR16 = sb("LSTR16", [16, 16], F32)
    IOTA = sb("IOTA", [128, 16], F32)
    TMASK = sb("TMASK", [128, NT], F32)
    GAM = sb("GAM", [128, DC], F32)
    BET = sb("BET", [128, DC], F32)
    BQ = sb("BQ", [128, 24], F32)
    BO = sb("BO", [128, DC], F32)
    RBS = sb("RBS", [1, E], BF16)
    RWS = sb("RWS", [128, DC * E], BF16)
    RT = sb("RT", [128, 16 * NT], F32)
    RTI = sb("RTI", [128, 4 * NT], I32)
    LG = sb("LG", [128, NT * E], F32)
    OH1 = sb("OH1", [128, NT * E], F32)
    OH2 = sb("OH2", [128, NT * E], F32)
    RSCR = sb("RSCR", [128, 64], F32)
    CNTL = sb("CNTL", [16, E], F32)
    CSX = sb("CSX", [128, NT * E], F32)
    CNTS = sb("CNTS", [128, E], F32)
    ZEROT = sb("ZEROT", [128, 512], BF16)

    ONEC = ONEB[:, 0:1]
    PB2 = [es.enter_context(nc.psum_tensor(f"PB2_{i}", [128, 1024], F32)) for i in range(4)]
    PB = [PB2[i // 2][:, (i % 2) * 512:(i % 2) * 512 + 512] for i in range(8)]

    pe = P.sem("pe")
    dve = P.sem("dve")
    act = P.sem("act")
    pool = P.sem("pool")
    ld0 = P.sem("ld0")
    ldg = P.sem("ldg")
    ws_s = [P.sem(f"ws{i}") for i in range(2)]
    kv_s = [P.sem(f"kv{i}") for i in range(2)]
    st_s = P.sem("st_s")
    st_g = P.sem("st_g")
    zi = P.sem("zi")
    cc = P.sem("cc")
    mg = P.sem("mg")
    tl_s = [P.sem(f"tl{i}") for i in range(2)]

    def T(f, waits=(), incs=(pe,)):
        P.emit("tensor", f, waits, incs)

    def V_(f, waits=(), incs=(dve,)):
        P.emit("vector", f, waits, incs)

    def A_(f, waits=(), incs=(act,)):
        P.emit("scalar", f, waits, incs)

    def G_(f, waits=(), incs=()):
        P.emit("gpsimd", f, waits, incs)

    def SY(f, waits=(), incs=()):
        P.emit("sync", f, waits, incs)

    def dma_g(out, in_, sem, waits=()):
        G_(lambda g, o=out, i=in_: g.dma_start(out=o, in_=i), waits, incs=[(sem, 16)])

    def dma_s(out, in_, sem, waits=()):
        SY(lambda s, o=out, i=in_: s.dma_start(out=o, in_=i), waits, incs=[(sem, 16)])

    def dma_s_tr(out, in_, sem, waits=()):
        SY(lambda s, o=out, i=in_: s.dma_start(out=o, in_=i, transpose=True), waits,
           incs=[(sem, 16)])

    def Xc(dc, lo=0, hi=TL):
        return X[:, dc * TL + lo: dc * TL + hi]

    def Hc(dc, lo=0, hi=TL):
        return HB[:, dc * TL + lo: dc * TL + hi]

    inline_aps = {}
    for nm, arr in consts_np.items():
        inline_aps[nm] = nc.inline_tensor(arr.astype(np.float32), nm).ap()
    l16 = nc.inline_tensor(HC["lstrict16"], "l16np").ap()
    iota_t = nc.inline_tensor(HC["iota16"], "iotanp").ap()

    dma_s(IDF[:, :], ident_f, ld0)
    dma_s(IOTA[:, :], iota_t, ld0)
    dma_s(GAM[:, :], gamma, ld0)
    dma_s(BET[:, :], beta, ld0)
    dma_s(BQ[:, :], bqkv, ld0)
    dma_s(BO[:, :], bout, ld0)
    dma_s(X[:, :].rearrange("p (c t) -> p c t", c=DC),
          x_fm.rearrange("(c p) t -> p c t", p=128), ld0)
    dma_g(IDB[:, :], inline_aps["identb"], ldg)
    dma_s(LSTR[:, :], inline_aps["lstrictb"], ld0)
    dma_s(LSTR16[:, :], l16, ld0)
    dma_s(TMASK[:, :], tilemask, ld0)
    dma_g(RBS[:, :], rb, ldg)
    dma_g(RWS[:, :].rearrange("p (c e) -> p c e", c=DC),
          rw.rearrange("(c p) e -> p c e", p=128), ldg)
    V_(lambda v: v.memset(ONEB[:, :], 1.0))
    V_(lambda v: v.memset(ONEF[:, :], 1.0))
    V_(lambda v: v.memset(CONSTS[:, 0:1], 1e-5))
    V_(lambda v: v.memset(CONSTS[:, 1:2], EXP_BIAS))
    V_(lambda v: v.memset(ZEROT[:, :], 0.0))
    for i in range(2):
        V_(lambda v, i=i: v.memset(VTILE[i][:, :], 1.0))
    init_dve = dve.n
    for r in range(0, CTOT, 128):
        for cc2 in range(2):
            dma_g(exp_in[r:r + 128, cc2 * 512:(cc2 + 1) * 512], ZEROT[:, :], zi,
                  waits=[(dve, init_dve)])
    for cc2 in range(2):
        dma_g(y_tok[CTOT:CTOT + 128, cc2 * 512:(cc2 + 1) * 512], ZEROT[:, :], zi,
              waits=[(dve, init_dve)])
    init_stg = zi.n

    def emit_ln():
        for tb in range(TB):
            o = tb * 512
            ps = PB[6][0:1, 0:512]
            ps2 = PB[7][0:1, 0:512]
            pe_gate = pe.n
            dve_gate = dve.n
            act_gate = act.n
            for dc in range(DC):
                V_(lambda v, dc=dc, o=o: v.tensor_copy(
                    PR[:, dc * 512:(dc + 1) * 512], Xc(dc, o, o + 512)),
                   waits=[(pe, pe_gate), (dve, dve_gate), (act, act_gate),
                          (ld0, ld0.n), (ldg, ldg.n)])
            cast_done = dve.n
            for dc in range(DC):
                V_(lambda v, dc=dc: v.tensor_mul(
                    WB[:, dc * 512:(dc + 1) * 512],
                    PR[:, dc * 512:(dc + 1) * 512], PR[:, dc * 512:(dc + 1) * 512]),
                   waits=[(dve, cast_done)])
            mul_done = dve.n
            for dc in range(DC):
                T(lambda t, dc=dc, ps=ps: t.matmul(
                    ps, ONEC[:, :], PR[:, dc * 512:(dc + 1) * 512],
                    start=(dc == 0), stop=(dc == DC - 1)),
                  waits=([(dve, cast_done), (ldg, ldg.n)] if dc == 0 else []),
                  incs=[pe] if dc == DC - 1 else [])
            for dc in range(DC):
                T(lambda t, dc=dc, ps2=ps2: t.matmul(
                    ps2, ONEC[:, :], WB[:, dc * 512:(dc + 1) * 512],
                    start=(dc == 0), stop=(dc == DC - 1)),
                  waits=([(dve, mul_done)] if dc == 0 else []),
                  incs=[pe] if dc == DC - 1 else [])
            mm_done = pe.n
            s1 = XT1[0:1, 0:512]; s2 = SCB[0:1, 0:512]; s3 = OS[0:1, 0:512]
            V_(lambda v, ps=ps, s1=s1: v.tensor_scalar_mul(s1, ps, 1.0 / D),
               waits=[(pe, mm_done), (dve, dve.n), (act, act.n)])
            V_(lambda v, s1=s1, s2=s2: v.tensor_mul(s2, s1, s1),
               waits=[(dve, dve.n)])
            V_(lambda v, ps2=ps2, s3=s3: v.tensor_scalar_mul(s3, ps2, 1.0 / D),
               waits=[(pe, mm_done), (dve, dve.n)])
            V_(lambda v, s2=s2, s3=s3: v.tensor_sub(s3, s3, s2),
               waits=[(dve, dve.n)])
            A_(lambda s, s2=s2, s3=s3: s.activation(s2, s3, AF.Sqrt, bias=CONSTS[0:1, 0:1]),
               waits=[(dve, dve.n)])
            V_(lambda v, s2=s2, s3=s3: v.reciprocal(s3, s2),
               waits=[(act, act.n), (dve, dve.n)])
            V_(lambda v, s3=s3, o=o: v.tensor_copy(SROW[0:1, o:o + 512], s3),
               waits=[(dve, dve.n)])
            V_(lambda v, s1=s1, s2=s2, s3=s3: v.tensor_mul(s2, s1, s3),
               waits=[(dve, dve.n)])
            V_(lambda v, s2=s2, o=o: v.tensor_scalar(
                SROW[0:1, 2048 + o:2048 + o + 512], s2, -1.0, None, ALU.mult),
               waits=[(dve, dve.n)])

    def emit_h(extra_gamma=False):
        for tb in range(TB):
            o = tb * 512
            pa = PB[6][:, 0:512]
            pb_ = PB[7][:, 0:512]
            T(lambda t, o=o, pa=pa: t.matmul(pa, ONEB[0:1, :], SROW[0:1, o:o + 512],
                                             start=True, stop=True),
              waits=[(dve, dve.n), (pe, pe.n)], incs=[pe])
            T(lambda t, o=o, pb_=pb_: t.matmul(pb_, ONEB[0:1, :], SROW[0:1, 2048 + o:2048 + o + 512],
                                               start=True, stop=True),
              waits=[], incs=[pe])
            bc_done = pe.n
            for dc in range(DC):
                V_(lambda v, dc=dc, o=o, pa=pa: v.tensor_tensor(
                    XT1[:, 0:512], Xc(dc, o, o + 512), pa, ALU.mult),
                   waits=[(pe, bc_done), (dve, dve.n)])
                if not extra_gamma:
                    V_(lambda v, dc=dc, o=o, pb_=pb_: v.tensor_tensor(
                        Hc(dc, o, o + 512), XT1[:, 0:512], pb_, ALU.add),
                       waits=[(dve, dve.n)])
                else:
                    V_(lambda v, dc=dc, o=o, pb_=pb_: v.tensor_tensor(
                        XT1[:, 0:512], XT1[:, 0:512], pb_, ALU.add),
                       waits=[(dve, dve.n)])
                    V_(lambda v, dc=dc, o=o: v.tensor_scalar(
                        Hc(dc, o, o + 512), XT1[:, 0:512],
                        GAM[:, dc:dc + 1], BET[:, dc:dc + 1], ALU.mult, ALU.add),
                       waits=[(dve, dve.n)])

    def emit_attn_a():
        act_a_gate = {}
        k_done = {}
        g2_done = {}
        for tb in range(TB):
            o = tb * 512
            pss = PB[2][0:1, 0:512]
            g2_dve = {}
            for sc in range(NKT + 2):
              if sc < NKT:
                wsb = WS[sc % 2]
                wsem = ws_s[sc % 2]
                dma_g(wsb[:, 0:1024].rearrange("p (c f) -> p c f", c=DC),
                      ktp[:, sc * 128:(sc + 1) * 128].rearrange("(c p) f -> p c f", p=128),
                      wsem, waits=[(pe, k_done.get((tb, sc - 2),
                                                  k_done.get((tb - 1, NKT - 2 + (sc % 2)), 0))),
                                   (wsem, max(0, wsem.n - 16))])
                ldw = wsem.n
                pscore = PB[sc % 2][:, 0:512]
                for dc in range(DC):
                    T(lambda t, dc=dc, wsb=wsb, pscore=pscore, o=o: t.matmul(
                        pscore, wsb[:, dc * 128:(dc + 1) * 128], Hc(dc, o, o + 512),
                        start=(dc == 0), stop=(dc == DC - 1)),
                      waits=([(ws_s[sc % 2], ldw)] +
                             ([(act, act_a_gate[sc - 2])] if sc >= 2 else
                              [(dve, dve.n), (act, act.n)]) if dc == 0 else []),
                      incs=[pe] if dc == DC - 1 else [])
                k_done[(tb, sc)] = pe.n
              sd = sc - 2
              if sd >= 0:
                T(lambda t, sd=sd, pss=pss: t.matmul(
                    pss, ONEC[:, :], PR[:, (sd % 2) * 512:(sd % 2) * 512 + 512],
                    start=(sd == 0), stop=(sd == NKT - 1)),
                  waits=[(dve, g2_dve[sd])], incs=[pe])
                g2_done[sd] = pe.n
              if sc < NKT:
                A_(lambda s, sc=sc, pscore=pscore: s.activation(
                    WB[:, sc * 512:(sc + 1) * 512], pscore, AF.Gelu),
                   waits=[(pe, k_done[(tb, sc)])])
                act_a_gate[sc] = act.n
                V_(lambda v, sc=sc: v.tensor_mul(
                    PR[:, (sc % 2) * 512:(sc % 2) * 512 + 512],
                    WB[:, sc * 512:(sc + 1) * 512], WB[:, sc * 512:(sc + 1) * 512]),
                   waits=[(act, act.n)] + ([(pe, g2_done[sc - 2])] if sc >= 2 else [(pe, pe.n)]))
                g2_dve[sc] = dve.n
            A_(lambda s, pss=pss: s.activation(XT1[0:1, 0:512], pss, AF.Sqrt),
               waits=[(pe, pe.n), (dve, dve.n)])
            V_(lambda v: v.reciprocal(OS[0:1, 0:512], XT1[0:1, 0:512]),
               waits=[(act, act.n), (dve, dve.n)])
            V_(lambda v: v.tensor_scalar_mul(XT1[0:1, 0:512], OS[0:1, 0:512], 64.0),
               waits=[(dve, dve.n)])
            T(lambda t: t.matmul(PB[2][:, 0:512], ONEF[0:1, :], XT1[0:1, 0:512],
                                 start=True, stop=True),
              waits=[(dve, dve.n)], incs=[pe])
            V_(lambda v: v.tensor_copy(SCB[:, :], PB[2][:, 0:512]),
               waits=[(pe, pe.n)])
            scb_done = dve.n
            ctx_sc = {}
            for sc in range(NKT):
                vsb = VS[sc % 2]
                vsem = kv_s[sc % 2]
                dma_g(vsb[:, 0:1024],
                      vp[sc * 128:(sc + 1) * 128, :], vsem,
                      waits=[(pe, ctx_sc.get((tb, sc - 2), ctx_sc.get((tb - 1, NKT - 2 + (sc % 2)), 0))),
                             (vsem, max(0, vsem.n - 16))])
                ldw = vsem.n
                for vc in range(DC):
                    T(lambda t, sc=sc, vc=vc, vsb=vsb, o=o: t.matmul(
                        PB[vc][:, 0:512], vsb[:, vc * 128:(vc + 1) * 128],
                        WB[:, sc * 512:(sc + 1) * 512],
                        start=(sc == 0), stop=(sc == NKT - 1)),
                      waits=([(kv_s[sc % 2], ldw)] if vc == 0 else []) +
                            ([(dve, scb_done)] if sc == 0 and vc == 0 else []),
                      incs=[pe] if vc == DC - 1 else [])
                ctx_sc[(tb, sc)] = pe.n
            ctx_done = pe.n
            for vc in range(DC):
                V_(lambda v, vc=vc: v.tensor_tensor(XT1[:, 0:512], PB[vc][:, 0:512],
                                                    SCB[:, :], ALU.mult),
                   waits=[(pe, ctx_done), (dve, dve.n), (act, act.n)])
                V_(lambda v, vc=vc, o=o: v.tensor_tensor(
                    Xc(vc, o, o + 512), Xc(vc, o, o + 512), XT1[:, 0:512], ALU.add),
                   waits=[(dve, dve.n)])

    moe_gather_done = [0, 0]

    def emit_ffn(app, stop_after_counts=False):
        for ti in range(NT):
            pl = PB[4 + ti % 2][:, 0:E]
            for dc in range(DC):
                T(lambda t, ti=ti, dc=dc, pl=pl: t.matmul(
                    pl, Hc(dc, ti * 128, (ti + 1) * 128), RWS[:, dc * E:(dc + 1) * E],
                    start=(dc == 0), stop=False),
                  waits=([(dve, dve.n), (ldg, ldg.n), (act, act.n)] if dc == 0 else []))
            T(lambda t, pl=pl: t.matmul(pl, ONEB[0:1, :], RBS[:, :],
                                        start=False, stop=True), incs=[pe])
            V_(lambda v, ti=ti, pl=pl: v.tensor_copy(LG[:, ti * E:(ti + 1) * E], pl),
               waits=[(pe, pe.n), (dve, dve.n)])
        qI1, qI2, qW1, qW2, qPL1, qPL2, qPG1, qPG2, qG1, qG2 = range(10)

        def rt(q):
            return RT[:, q * NT:(q + 1) * NT]

        for ti in range(NT):
            Lsl = LG[:, ti * E:(ti + 1) * E]
            t0 = RSCR[:, 0:E]; t1 = RSCR[:, E:2 * E]; t2 = RSCR[:, 2 * E:3 * E]
            c0 = RSCR[:, 3 * E:3 * E + 1]; c1 = RSCR[:, 3 * E + 1:3 * E + 2]
            V_(lambda v, Lsl=Lsl, c0=c0: v.reduce_max(c0, Lsl, axis=AX.X),
               waits=[(dve, dve.n)])
            V_(lambda v, c0=c0, c1=c1: v.tensor_scalar_mul(c1, c0, -1.0),
               waits=[(dve, dve.n)])
            A_(lambda s, Lsl=Lsl, t0=t0, c1=c1: s.activation(t0, Lsl, AF.Exp, bias=c1),
               waits=[(dve, dve.n)])
            V_(lambda v, t0=t0, c0=c0: v.reduce_sum(c0, t0, axis=AX.X),
               waits=[(act, act.n), (dve, dve.n)])
            V_(lambda v, c0=c0, c1=c1: v.reciprocal(c1, c0), waits=[(dve, dve.n)])
            V_(lambda v, t0=t0, c1=c1: v.tensor_scalar_mul(t0, t0, c1),
               waits=[(dve, dve.n)])
            V_(lambda v, t0=t0, c0=c0: v.reduce_max(c0, t0, axis=AX.X),
               waits=[(dve, dve.n)])
            V_(lambda v, ti=ti, t0=t0, c0=c0: v.tensor_scalar(
                OH1[:, ti * E:(ti + 1) * E], t0, c0, None, ALU.is_ge),
               waits=[(dve, dve.n)])
            V_(lambda v, ti=ti, t1=t1: v.tensor_mul(t1, IOTA[:, :], OH1[:, ti * E:(ti + 1) * E]),
               waits=[(dve, dve.n), (ld0, ld0.n)])
            V_(lambda v, ti=ti, t1=t1: v.reduce_max(rt(qI1)[:, ti:ti + 1], t1, axis=AX.X),
               waits=[(dve, dve.n)])
            V_(lambda v, ti=ti, c0=c0: v.tensor_copy(rt(qW1)[:, ti:ti + 1], c0),
               waits=[(dve, dve.n)])
            V_(lambda v, ti=ti, t1=t1: v.tensor_scalar_mul(t1, OH1[:, ti * E:(ti + 1) * E], 1e9),
               waits=[(dve, dve.n)])
            V_(lambda v, t0=t0, t1=t1: v.tensor_sub(t0, t0, t1), waits=[(dve, dve.n)])
            V_(lambda v, t0=t0, c0=c0: v.reduce_max(c0, t0, axis=AX.X),
               waits=[(dve, dve.n)])
            V_(lambda v, ti=ti, t0=t0, c0=c0: v.tensor_scalar(
                OH2[:, ti * E:(ti + 1) * E], t0, c0, None, ALU.is_ge),
               waits=[(dve, dve.n)])
            V_(lambda v, ti=ti, t1=t1: v.tensor_mul(t1, IOTA[:, :], OH2[:, ti * E:(ti + 1) * E]),
               waits=[(dve, dve.n)])
            V_(lambda v, ti=ti, t1=t1: v.reduce_max(rt(qI2)[:, ti:ti + 1], t1, axis=AX.X),
               waits=[(dve, dve.n)])
            V_(lambda v, ti=ti, c0=c0: v.tensor_copy(rt(qW2)[:, ti:ti + 1], c0),
               waits=[(dve, dve.n)])
        V_(lambda v: v.tensor_add(RSCR[:, 0:NT], rt(qW1), rt(qW2)), waits=[(dve, dve.n)])
        V_(lambda v: v.tensor_scalar_add(RSCR[:, 0:NT], RSCR[:, 0:NT], 1e-9),
           waits=[(dve, dve.n)])
        V_(lambda v: v.reciprocal(RSCR[:, NT:2 * NT], RSCR[:, 0:NT]), waits=[(dve, dve.n)])
        V_(lambda v: v.tensor_mul(rt(qW1), rt(qW1), RSCR[:, NT:2 * NT]), waits=[(dve, dve.n)])
        V_(lambda v: v.tensor_mul(rt(qW2), rt(qW2), RSCR[:, NT:2 * NT]), waits=[(dve, dve.n)])
        topk_dve = dve.n
        for ti in range(NT):
            V_(lambda v, ti=ti: v.tensor_add(CSX[:, ti * E:(ti + 1) * E],
                                             OH1[:, ti * E:(ti + 1) * E],
                                             OH2[:, ti * E:(ti + 1) * E]),
               waits=[(dve, dve.n)])
        cnt_dve = dve.n
        for ti in range(NT):
            T(lambda t, ti=ti: t.matmul(PB[6][0:1, ti * E:(ti + 1) * E], ONEF[:, 0:1],
                                        CSX[:, ti * E:(ti + 1) * E],
                                        start=True, stop=True),
              waits=([(dve, cnt_dve), (ld0, ld0.n)] if ti == 0 else []),
              incs=[pe] if ti == NT - 1 else [])
        V_(lambda v: v.tensor_copy(XT1[0:1, 0:NT * E], PB[6][0:1, 0:NT * E]),
           waits=[(pe, pe.n), (dve, dve.n)])
        dma_g(cnt_in[app].rearrange("t e -> (t e)").rearrange("(o x) -> o x", o=1),
              XT1[0:1, 0:NT * E], st_g, waits=[(dve, dve.n)])
        if stop_after_counts:
            dma_s(o_cnt.rearrange("t e -> (t e)").rearrange("(o x) -> o x", o=1),
                  XT1[0:1, 0:NT * E], st_s, waits=[(dve, dve.n)])
            return
        for dc in range(DC):
            dma_s(hfm_d[dc * 128:(dc + 1) * 128, :], Hc(dc), st_s,
                  waits=[(act, act.n), (dve, dve.n)])
        hfm_done = st_s.n
        def prod_sl(fc, lo, hi):
            base = fc * 2048
            bufs = [(WB, 16384), (KTILE[0], 4096), (KTILE[1], 4096)]
            off = base + lo
            for b, size in bufs:
                if off < size:
                    assert off + (hi - lo) <= size, (fc, lo, hi)
                    return b[:, off:off + (hi - lo)]
                off -= size
            raise AssertionError("prod overflow")

        sh_gate = pe.n
        sh_mm = {}
        sh_sil = {}
        sh_mul = {}
        for fc in range(FC):
            wsb = WS[fc % 2]
            wsem = ws_s[fc % 2]
            G_(lambda g, fc=fc, wsb=wsb: g.dma_start(
                out=wsb[:, 0:1024].rearrange("p (c f) -> p c f", c=DC),
                in_=swg[:, fc * 128:(fc + 1) * 128].rearrange("(c p) f -> p c f", p=128)),
               waits=[(pe, sh_mm.get(fc - 2, sh_gate)), (wsem, wsem.n)], incs=[(wsem, 16)])
            G_(lambda g, fc=fc, wsb=wsb: g.dma_start(
                out=wsb[:, 1024:2048].rearrange("p (c f) -> p c f", c=DC),
                in_=swu[:, fc * 128:(fc + 1) * 128].rearrange("(c p) f -> p c f", p=128)),
               incs=[(wsem, 16)])
            ldw = wsem.n
            for th in range(2):
                st = 2 * fc + th
                p = st % 2
                pg2 = PB2[p]
                pu2 = PB2[2 + p]
                for half in range(2):
                    dst = pg2 if half == 0 else pu2
                    for dc in range(DC):
                        for q in range(2):
                            T(lambda t, fc=fc, th=th, half=half, dc=dc, q=q, wsb=wsb, dst=dst: t.matmul(
                                dst[:, q * 512:(q + 1) * 512],
                                wsb[:, half * 1024 + dc * 128:half * 1024 + (dc + 1) * 128],
                                Hc(dc, th * 1024 + q * 512, th * 1024 + (q + 1) * 512),
                                start=(dc == 0), stop=(dc == DC - 1)),
                              waits=([(ws_s[fc % 2], ldw),
                                      (dve, sh_mul.get(st - 2, dve.n if st == 0 else 0)),
                                      (act, sh_sil.get(st - 2, act.n if st == 0 else 0))]
                                     if half == 0 and dc == 0 and q == 0 else []),
                              incs=[pe] if (half == 1 and dc == DC - 1 and q == 1) else [])
                mm_done = pe.n
                if th == 1:
                    sh_mm[fc] = pe.n
                A_(lambda s, th=th, pg2=pg2: s.activation(
                    PR[:, th * 1024:(th + 1) * 1024], pg2[:, 0:1024], AF.Silu),
                   waits=[(pe, mm_done)] +
                         ([(dve, sh_mul[st - 2])] if st >= 2 else []))
                sh_sil[st] = act.n
                V_(lambda v, fc=fc, th=th, pu2=pu2: v.tensor_tensor(
                    prod_sl(fc, th * 1024, (th + 1) * 1024),
                    PR[:, th * 1024:(th + 1) * 1024], pu2[:, 0:1024], ALU.mult),
                   waits=[(act, sh_sil[st]), (dve, dve.n)],
                   incs=[dve])
                sh_mul[st] = dve.n
        prod_dve = dve.n
        dn_mm = {}
        sh_bankfree = {}
        for pair in range(4):
            for fc in range(FC):
                wsb = WS[fc % 2]
                wsem = ws_s[fc % 2]
                G_(lambda g, pair=pair, fc=fc, wsb=wsb: g.dma_start(
                    out=wsb[:, 2048:2304].rearrange("p (c f) -> p c f", c=2),
                    in_=swd[fc * 128:(fc + 1) * 128,
                            pair * 256:(pair + 1) * 256].rearrange("p (c f) -> p c f", c=2)),
                   waits=[(pe, dn_mm.get((pair, fc - 2),
                                         dn_mm.get((pair - 1, FC - 1 - (fc % 2)), sh_mm[FC - 1]))),
                          (wsem, wsem.n)],
                   incs=[(wsem, 16)])
                ldw = wsem.n
                for d2 in range(2):
                    for tq in range(4):
                        T(lambda t, pair=pair, fc=fc, d2=d2, tq=tq, wsb=wsb: t.matmul(
                            PB[d2 * 4 + tq][:, 0:512],
                            wsb[:, 2048 + d2 * 128:2048 + (d2 + 1) * 128],
                            prod_sl(fc, tq * 512, (tq + 1) * 512),
                            start=(fc == 0), stop=(fc == FC - 1)),
                          waits=([(ws_s[fc % 2], ldw)] if d2 == 0 and tq == 0 else []) +
                                ([(dve, sh_bankfree.get(pair - 1, prod_dve))]
                                 if fc == 0 and d2 == 0 and tq == 0 else []),
                          incs=[pe] if (d2 == 1 and tq == 3) else [])
                dn_mm[(pair, fc)] = pe.n
            down_done = pe.n
            for d2 in range(2):
                dc = pair * 2 + d2
                for tq in range(4):
                    o = tq * 512
                    pa = PB[d2 * 4 + tq][:, 0:512]
                    V_(lambda v, dc=dc, o=o, pa=pa: v.tensor_tensor(
                        Xc(dc, o, o + 512), Xc(dc, o, o + 512), pa, ALU.add),
                       waits=[(pe, down_done), (dve, dve.n), (st_g, st_g.n),
                              (st_s, st_s.n)])
            dn_mm[(pair, FC - 1)] = max(dn_mm[(pair, FC - 1)], 0)
            bank_free = dve.n
            for d2 in range(2):
                dc = pair * 2 + d2
                for tq in range(4):
                    o = tq * 512
                    V_(lambda v, dc=dc, o=o: v.tensor_tensor(
                        Xc(dc, o, o + 512), Xc(dc, o, o + 512), Hc(dc, o, o + 512), ALU.add),
                       waits=[(dve, dve.n)])
            sh_bankfree[pair] = bank_free
        shared_done = dve.n
        dma_s(CNTS[:, :], i_cntall, ld0)
        dma_s(CNTL[:, :], cnt_in[app], ld0, waits=[(st_g, st_g.n)])
        ag_ld = ld0.n
        T(lambda t: t.matmul(PB[6][0:16, 0:E], LSTR16[:, :], CNTL[:, :],
                             start=True, stop=True),
          waits=[(ld0, ag_ld), (dve, shared_done)], incs=[pe])
        T(lambda t: t.matmul(PB[7][0:16, 0:E], TMASK[:, :], CNTS[:, :],
                             start=True, stop=True),
          waits=[(ld0, ag_ld)], incs=[pe])
        off_pe = pe.n
        V_(lambda v: v.tensor_copy(OS[0:16, 0:E], PB[6][0:16, 0:E]),
           waits=[(pe, off_pe), (dve, dve.n)])
        V_(lambda v: v.tensor_copy(OS[0:16, E:2 * E], PB[7][0:16, 0:E]),
           waits=[(dve, dve.n)])
        dma_s(bounce[0:1, :].rearrange("o (t e) -> (o t) e", e=E), OS[0:16, 0:E],
              st_s, waits=[(dve, dve.n)])
        dma_s(bounce[1:2, :].rearrange("o (t e) -> (o t) e", e=E), OS[0:16, E:2 * E],
              st_s, waits=[(dve, dve.n)])
        dma_s(SCB[0:1, 0:256], bounce[0:1, :], ld0, waits=[(st_s, st_s.n)])
        dma_s(OS[0:1, 0:256], bounce[1:2, :], ld0, waits=[(st_s, st_s.n)])
        offrow_ld = ld0.n
        for ti in range(NT):
            pcs = PB[4][:, 0:E]
            plo = PB[5][:, 0:E]
            pgo = PB[5][:, E:2 * E]
            T(lambda t, ti=ti, pcs=pcs: t.matmul(pcs, LSTR[:, :], OH1[:, ti * E:(ti + 1) * E],
                                                 start=True, stop=False),
              waits=[(dve, dve.n), (ld0, offrow_ld)] if ti == 0 else [(dve, dve.n)])
            T(lambda t, ti=ti, pcs=pcs: t.matmul(pcs, LSTR[:, :], OH2[:, ti * E:(ti + 1) * E],
                                                 start=False, stop=True), incs=[pe])
            T(lambda t, ti=ti, plo=plo: t.matmul(plo, ONEF[0:1, :], SCB[0:1, ti * E:(ti + 1) * E],
                                                 start=True, stop=True), incs=[pe])
            T(lambda t, ti=ti, pgo=pgo: t.matmul(pgo, ONEF[0:1, :], OS[0:1, ti * E:(ti + 1) * E],
                                                 start=True, stop=True), incs=[pe])
            pos_pe = pe.n
            V_(lambda v, ti=ti, pcs=pcs: v.tensor_copy(CSX[:, ti * E:(ti + 1) * E], pcs),
               waits=[(pe, pos_pe), (dve, dve.n)])
            t0 = RSCR[:, 0:E]; t1 = RSCR[:, E:2 * E]
            V_(lambda v, ti=ti, plo=plo, t0=t0: v.tensor_tensor(
                t0, CSX[:, ti * E:(ti + 1) * E], plo, ALU.add), waits=[(dve, dve.n)])
            V_(lambda v, ti=ti, pgo=pgo, t1=t1: v.tensor_tensor(
                t1, CSX[:, ti * E:(ti + 1) * E], pgo, ALU.add), waits=[(dve, dve.n)])
            for (k, q_pl, q_pg) in ((0, qPL1, qPG1), (1, qPL2, qPG2)):
                OHk = (OH1 if k == 0 else OH2)[:, ti * E:(ti + 1) * E]
                t2 = RSCR[:, 2 * E:3 * E]
                V_(lambda v, t0=t0, t2=t2, OHk=OHk: v.tensor_mul(t2, t0, OHk),
                   waits=[(dve, dve.n)])
                V_(lambda v, ti=ti, t2=t2, q=q_pl: v.reduce_sum(rt(q)[:, ti:ti + 1], t2, axis=AX.X),
                   waits=[(dve, dve.n)])
                V_(lambda v, t1=t1, t2=t2, OHk=OHk: v.tensor_mul(t2, t1, OHk),
                   waits=[(dve, dve.n)])
                V_(lambda v, ti=ti, t2=t2, q=q_pg: v.reduce_sum(rt(q)[:, ti:ti + 1], t2, axis=AX.X),
                   waits=[(dve, dve.n)])
        W1s = RSCR[:, 0:NT]; W2s = RSCR[:, NT:2 * NT]
        V_(lambda v: v.tensor_scalar(W1s, rt(qPG1), CAP * 1.0, None, ALU.is_lt),
           waits=[(dve, dve.n)])
        V_(lambda v: v.tensor_scalar(W2s, rt(qPG2), CAP * 1.0, None, ALU.is_lt),
           waits=[(dve, dve.n)])
        V_(lambda v: v.tensor_mul(rt(qG1), rt(qW1), W1s), waits=[(dve, dve.n)])
        V_(lambda v: v.tensor_mul(rt(qG2), rt(qW2), W2s), waits=[(dve, dve.n)])
        for (qi, qpl, ws_, col) in ((qI1, qPL1, W1s, 0), (qI2, qPL2, W2s, 1)):
            f0 = RSCR[:, 2 * NT:3 * NT]
            V_(lambda v, qi=qi, f0=f0: v.tensor_scalar(f0, rt(qi), CLOC * 1.0, None, ALU.mult),
               waits=[(dve, dve.n)])
            V_(lambda v, qpl=qpl, f0=f0: v.tensor_add(f0, f0, rt(qpl)), waits=[(dve, dve.n)])
            f1 = RSCR[:, 3 * NT:4 * NT]
            V_(lambda v, ws_=ws_, f1=f1: v.tensor_scalar(f1, ws_, -1e9, 1e9, ALU.mult, ALU.add),
               waits=[(dve, dve.n)])
            V_(lambda v, f0=f0, f1=f1: v.tensor_add(f1, f0, f1), waits=[(dve, dve.n)])
            V_(lambda v, f1=f1: v.tensor_scalar_min(f1, f1, CTOT * 1.0), waits=[(dve, dve.n)])
            V_(lambda v, f1=f1, col=col: v.tensor_copy(RTI[:, col * NT:(col + 1) * NT], f1),
               waits=[(dve, dve.n)])
            V_(lambda v, f1=f1, col=col: v.tensor_copy(RTI[:, (2 + col) * NT:(3 + col) * NT], f1),
               waits=[(dve, dve.n)])
        flat_dve = dve.n
        prev_gather = moe_gather_done[0] if app == 1 else 0
        for ti in range(NT):
            par = ti % 2
            dma_s_tr(YS[:, par * 1024:(par + 1) * 1024],
                     hfm_d[:, ti * 128:(ti + 1) * 128], tl_s[par],
                     waits=[(st_s, st_s.n), (mg, mg.n)])
            tl_done = tl_s[par].n
            for k in range(2):
                G_(lambda g, ti=ti, k=k, par=par: g.indirect_dma_start(
                    out=exp_in, out_offset=IndirectOffsetOnAxis(RTI[:, k * NT + ti:k * NT + ti + 1], 0),
                    in_=YS[:, par * 1024:(par + 1) * 1024], in_offset=None),
                   waits=[(tl_s[par], tl_done), (dve, flat_dve),
                          (zi, init_stg), (mg, mg.n)],
                   incs=[(mg, 16)])
        scatter_done = mg.n
        gu_mm = {}
        dnx_mm = {}
        ex_act = {}
        ex_dve = {}
        for e in range(E):
            ebuf = KTILE[e % 2]
            esem = tl_s[e % 2]
            ein_gate = gu_mm.get((e - 2, FC - 1), dn_mm[(3, FC - 1)])
            for dc in range(DC):
                dma_s_tr(ebuf[:, dc * CLOC:dc * CLOC + CLOC],
                         exp_in[e * CLOC:(e + 1) * CLOC, dc * 128:(dc + 1) * 128],
                         esem,
                         waits=([(mg, scatter_done), (pe, ein_gate)] if dc == 0 else []))
            ein_ld = esem.n
            for fc in range(FC):
                wsb = WS[fc % 2]
                wsem = ws_s[fc % 2]
                gu_gate = gu_mm.get((e, fc - 2),
                                    gu_mm.get((e - 1, FC - 1 - (fc % 2)), dn_mm[(3, FC - 1)]))
                G_(lambda g, e=e, fc=fc, wsb=wsb: g.dma_start(
                    out=wsb[:, 0:1024].rearrange("p (c f) -> p c f", c=DC),
                    in_=wg[e, :, fc * 128:(fc + 1) * 128].rearrange("(c p) f -> p c f", p=128)),
                   waits=[(pe, gu_gate), (wsem, wsem.n)], incs=[(wsem, 16)])
                G_(lambda g, e=e, fc=fc, wsb=wsb: g.dma_start(
                    out=wsb[:, 1024:2048].rearrange("p (c f) -> p c f", c=DC),
                    in_=wu[e, :, fc * 128:(fc + 1) * 128].rearrange("(c p) f -> p c f", p=128)),
                   incs=[(wsem, 16)])
                ldw = wsem.n
                pg = PB[fc % 2][:, 0:CLOC]
                pu = PB[2 + fc % 2][:, 0:CLOC]
                for dc in range(DC):
                    T(lambda t, dc=dc, wsb=wsb, pg=pg, ebuf=ebuf: t.matmul(
                        pg, wsb[:, dc * 128:(dc + 1) * 128],
                        ebuf[:, dc * CLOC:dc * CLOC + CLOC],
                        start=(dc == 0), stop=(dc == DC - 1)),
                      waits=([(ws_s[fc % 2], ldw), (esem, ein_ld),
                              (act, ex_act.get((e, fc - 2), ex_act.get((e, fc - 1), act.n))),
                              (dve, ex_dve.get((e, fc - 2), ex_dve.get((e, fc - 1), dve.n)))]
                             if dc == 0 else []),
                      incs=[pe] if dc == DC - 1 else [])
                for dc in range(DC):
                    T(lambda t, dc=dc, wsb=wsb, pu=pu, ebuf=ebuf: t.matmul(
                        pu, wsb[:, 1024 + dc * 128:1024 + (dc + 1) * 128],
                        ebuf[:, dc * CLOC:dc * CLOC + CLOC],
                        start=(dc == 0), stop=(dc == DC - 1)),
                      incs=[pe] if dc == DC - 1 else [])
                gu_done = pe.n
                gu_mm[(e, fc)] = pe.n
                A_(lambda s, fc=fc, pg=pg: s.activation(
                    PR[:, (fc % 2) * 512:(fc % 2) * 512 + CLOC], pg, AF.Silu),
                   waits=[(pe, gu_done)] +
                         ([(dve, ex_dve[(e, fc - 2)])] if (e, fc - 2) in ex_dve else []))
                ex_act[(e, fc)] = act.n
                V_(lambda v, fc=fc, pu=pu: v.tensor_tensor(
                    WB[:, fc * CLOC:(fc + 1) * CLOC],
                    PR[:, (fc % 2) * 512:(fc % 2) * 512 + CLOC], pu, ALU.mult),
                   waits=[(act, act.n), (pe, gu_done), (dve, dve.n)], incs=[dve])
                ex_dve[(e, fc)] = dve.n
            eprod_dve = dve.n
            for fc in range(FC):
                wsb = WS[fc % 2]
                wsem = ws_s[fc % 2]
                dn_gate = dnx_mm.get((e, fc - 2),
                                     dnx_mm.get((e - 1, FC - 1 - (fc % 2)), dn_mm[(3, FC - 1)]))
                G_(lambda g, e=e, fc=fc, wsb=wsb: g.dma_start(
                    out=wsb[:, 2048:3072],
                    in_=wd[e, fc * 128:(fc + 1) * 128, :]),
                   waits=[(pe, dn_gate), (wsem, wsem.n)], incs=[(wsem, 16)])
                ldw = wsem.n
                for sc in range(4):
                    rows = 128 if sc < 3 else CLOC - 384
                    for dh in range(2):
                        T(lambda t, fc=fc, sc=sc, dh=dh, rows=rows, wsb=wsb: t.matmul(
                            PB[sc * 2 + dh][0:rows, 0:512],
                            WB[:, fc * CLOC + sc * 128:fc * CLOC + sc * 128 + rows],
                            wsb[:, 2048 + dh * 512:2048 + (dh + 1) * 512],
                            start=(fc == 0), stop=(fc == FC - 1)),
                          waits=([(ws_s[fc % 2], ldw)] if sc == 0 and dh == 0 else []) +
                                ([(dve, eprod_dve)] if fc == 0 and sc == 0 and dh == 0 else []),
                          incs=[pe] if (sc == 3 and dh == 1) else [])
                dnx_mm[(e, fc)] = pe.n
            y_done = pe.n
            for sc in range(4):
                rows = 128 if sc < 3 else CLOC - 384
                V_(lambda v, sc=sc, rows=rows: v.tensor_copy(
                    YS[0:rows, 0:1024], PB[sc * 2][0:rows, 0:512]),
                   waits=[(pe, y_done), (dve, dve.n), (tl_s[0], tl_s[0].n),
                          (tl_s[1], tl_s[1].n)]) if False else None
                V_(lambda v, sc=sc, rows=rows: v.tensor_copy(
                    YS[0:rows, 0:512], PB[sc * 2][0:rows, 0:512]),
                   waits=[(pe, y_done), (dve, dve.n), (st_g, st_g.n)])
                V_(lambda v, sc=sc, rows=rows: v.tensor_copy(
                    YS[0:rows, 512:1024], PB[sc * 2 + 1][0:rows, 0:512]),
                   waits=[(dve, dve.n)])
                dma_g(y_tok[e * CLOC + sc * 128:e * CLOC + sc * 128 + rows, :],
                      YS[0:rows, 0:1024], st_g, waits=[(dve, dve.n)])
        for ti in range(NT):
            G_(lambda g, ti=ti: g.indirect_dma_start(
                out=PR[:, 0:1024], out_offset=None,
                in_=y_tok, in_offset=IndirectOffsetOnAxis(RTI[:, 2 * NT + ti:2 * NT + ti + 1], 0)),
               waits=[(st_g, st_g.n), (dve, dve.n), (act, act.n)], incs=[(mg, 16)])
            G_(lambda g, ti=ti: g.indirect_dma_start(
                out=PR[:, 1024:2048], out_offset=None,
                in_=y_tok, in_offset=IndirectOffsetOnAxis(RTI[:, 3 * NT + ti:3 * NT + ti + 1], 0)),
               incs=[(mg, 16)])
            g_done = mg.n
            V_(lambda v, ti=ti: v.tensor_scalar(YS[:, 0:1024], PR[:, 0:1024],
                                                rt(qG1)[:, ti:ti + 1], None, ALU.mult),
               waits=[(mg, g_done), (dve, dve.n)])
            V_(lambda v, ti=ti: v.scalar_tensor_tensor(
                YS[:, 1024:2048], PR[:, 1024:2048], rt(qG2)[:, ti:ti + 1], YS[:, 0:1024],
                ALU.mult, ALU.add),
               waits=[(dve, dve.n)], incs=[dve])
            dma_g(z_tok[ti * 128:(ti + 1) * 128, :], YS[:, 1024:2048], st_g,
                  waits=[(dve, dve.n)])
        zt_done = st_g.n
        moe_gather_done[app] = mg.n
        for dc in range(DC):
            par = dc % 2
            dma_s_tr(YS[:, 0:2048] if False else YS[:, par * 1024:par * 1024 + 1024],
                     z_tok[:, dc * 128:(dc + 1) * 128], tl_s[par],
                     waits=[(st_g, zt_done), (dve, dve.n)]) if False else None
        for dc in range(DC):
            par = dc % 2
            dma_s_tr(WB[:, par * 2048:par * 2048 + 2048],
                     z_tok[:, dc * 128:(dc + 1) * 128], tl_s[par],
                     waits=[(st_g, zt_done), (dve, dve.n), (pe, pe.n)])
            tl_done = tl_s[par].n
            for tq in range(4):
                V_(lambda v, dc=dc, par=par, tq=tq: v.tensor_tensor(
                    Xc(dc, tq * 512, (tq + 1) * 512), Xc(dc, tq * 512, (tq + 1) * 512),
                    WB[:, par * 2048 + tq * 512:par * 2048 + (tq + 1) * 512], ALU.add),
                   waits=[(tl_s[par], tl_done), (dve, dve.n)])

    def emit_qk_proj(feat_lo, bias_col0, dest):
        qk_mm = {}
        qk_act = {}
        base_pe = pe.n
        for oc8 in range(8):
            wsb = WS[oc8 % 2]
            wsem = ws_s[oc8 % 2]
            G_(lambda g, oc8=oc8, wsb=wsb: g.dma_start(
                out=wsb[:, 0:1024].rearrange("p (c f) -> p c f", c=DC),
                in_=wqkT[:, feat_lo + oc8 * 128:feat_lo + (oc8 + 1) * 128]
                    .rearrange("(c p) f -> p c f", p=128)),
               waits=[(pe, qk_mm.get(oc8 - 2, base_pe)), (wsem, wsem.n),
                      (act, act.n if oc8 == 0 else 0)], incs=[(wsem, 16)])
            ldw = wsem.n
            for tb in range(TB):
                pq = PB[(oc8 % 2) * 4 + tb][:, 0:512]
                for dc in range(DC):
                    T(lambda t, oc8=oc8, tb=tb, dc=dc, wsb=wsb, pq=pq: t.matmul(
                        pq, wsb[:, dc * 128:(dc + 1) * 128],
                        Hc(dc, tb * 512, (tb + 1) * 512),
                        start=(dc == 0), stop=(dc == DC - 1)),
                      waits=([(ws_s[oc8 % 2], ldw), (act, qk_act.get(oc8 - 2, act.n)),
                              (dve, dve.n if oc8 == 0 else 0)]
                             if tb == 0 and dc == 0 else []),
                      incs=[pe] if dc == DC - 1 else [])
            mm_done = pe.n
            qk_mm[oc8] = pe.n
            for tb in range(TB):
                A_(lambda s, oc8=oc8, tb=tb: s.activation(
                    dest[:, oc8 * 2048 + tb * 512:oc8 * 2048 + (tb + 1) * 512],
                    PB[(oc8 % 2) * 4 + tb][:, 0:512], AF.Identity,
                    bias=BQ[:, bias_col0 + oc8:bias_col0 + oc8 + 1]),
                   waits=([(pe, mm_done)] if tb == 0 else []))
            qk_act[oc8] = act.n

    def emit_attnb_kv():
        emit_qk_proj(1024, 8, WB)
        kt_act = act.n
        for oc8 in range(8):
            dma_s(o_kt[oc8 * 128:(oc8 + 1) * 128, :],
                  WB[:, oc8 * 2048:(oc8 + 1) * 2048], st_s, waits=[(act, kt_act)])
        for q in range(2):
            G_(lambda g, q=q: g.dma_start(
                out=KTILE[q][:, :].rearrange("p (c f) -> p c f", c=DC),
                in_=wqkT[:, 2048 + q * 512:2048 + (q + 1) * 512]
                    .rearrange("(c p) f -> p c f", p=128)),
               waits=[(pe, pe.n), (act, act.n), (dve, dve.n)], incs=[(kv_s[q], 16)])
        G_(lambda g: g.dma_start(out=PR[0:1, 0:1024], in_=bqv_row),
           waits=[(dve, dve.n)], incs=[(ldg, 16)])
        T(lambda t: t.matmul(PB[6][:, 0:512], ONEB[0:1, :], PR[0:1, 0:512],
                             start=True, stop=True),
          waits=[(ldg, ldg.n), (act, act.n), (dve, dve.n)], incs=[pe])
        T(lambda t: t.matmul(PB[7][:, 0:512], ONEB[0:1, :], PR[0:1, 512:1024],
                             start=True, stop=True), incs=[pe])
        bv_pe = pe.n
        V_(lambda v: v.tensor_copy(PR[:, 1024:1536], PB[6][:, 0:512]),
           waits=[(pe, bv_pe), (dve, dve.n)])
        V_(lambda v: v.tensor_copy(PR[:, 1536:2048], PB[7][:, 0:512]),
           waits=[(dve, dve.n)])
        bv_dve = dve.n
        vdve = {}
        for tt in range(NT):
            b0 = PB[(tt % 2) * 2][:, 0:512]
            b1 = PB[(tt % 2) * 2 + 1][:, 0:512]
            for dc in range(DC):
                T(lambda t, tt=tt, dc=dc, b0=b0: t.matmul(
                    b0, Hc(dc, tt * 128, (tt + 1) * 128),
                    KTILE[0][:, dc * 512:(dc + 1) * 512],
                    start=(dc == 0), stop=(dc == DC - 1)),
                  waits=(([(kv_s[0], kv_s[0].n), (kv_s[1], kv_s[1].n),
                           (dve, dve.n), (act, act.n)] if tt == 0 else
                          [(dve, vdve.get(tt - 2, 0))]) if dc == 0 else []),
                  incs=[pe] if dc == DC - 1 else [])
                T(lambda t, tt=tt, dc=dc, b1=b1: t.matmul(
                    b1, Hc(dc, tt * 128, (tt + 1) * 128),
                    KTILE[1][:, dc * 512:(dc + 1) * 512],
                    start=(dc == 0), stop=(dc == DC - 1)),
                  incs=[pe] if dc == DC - 1 else [])
            vp_done = pe.n
            V_(lambda v, tt=tt, b0=b0: v.tensor_tensor(YS[:, (tt % 2) * 1024:(tt % 2) * 1024 + 512], b0,
                                                PR[:, 1024:1536], ALU.add),
               waits=[(pe, vp_done), (dve, max(bv_dve, dve.n)), (st_s, st_s.n)])
            V_(lambda v, tt=tt, b1=b1: v.tensor_tensor(YS[:, (tt % 2) * 1024 + 512:(tt % 2) * 1024 + 1024], b1,
                                                PR[:, 1536:2048], ALU.add),
               waits=[(dve, dve.n)])
            vdve[tt] = dve.n
            dma_s(o_v[tt * 128:(tt + 1) * 128, :], YS[:, (tt % 2) * 1024:(tt % 2) * 1024 + 1024], st_s,
                  waits=[(dve, dve.n)])

    def emit_attnb_attn():
        emit_qk_proj(0, 0, WB)
        agk = i_agk
        agv = i_agv
        sc_last = {}
        ctx_last = {}
        wout_mm = {}
        for qb in range(4):
            for hp in range(8):
                par = hp % 2
                ksem = kv_s[par]
                gi = qb * 8 + hp
                kt_gate = sc_last.get(gi - 2, pe.n if gi < 2 else 0)
                v_gate = ctx_last.get(gi - 1, pe.n if gi < 1 else 0)
                for rank in range(8):
                    SY(lambda sy, qb=qb, hp=hp, rank=rank, par=par: sy.dma_start(
                        out=KTILE[par].rearrange("p (j r c) -> p j r c", j=4, r=8)[:, :, rank, :],
                        in_=agk[1024 * rank + 128 * hp:1024 * rank + 128 * (hp + 1),
                                512 * qb:512 * (qb + 1)].rearrange("p (j c) -> p j c", j=4)),
                       waits=([(pe, kt_gate)] if rank == 0 else []),
                       incs=[(ksem, 16)])
                for hh in range(2):
                    h = 2 * hp + hh
                    for rank in range(8):
                        SY(lambda sy, qb=qb, h=h, hh=hh, rank=rank: sy.dma_start(
                            out=VTILE[hh].rearrange("p (j r c) -> p j r c", j=4, r=8, c=65)[:, :, rank, 0:64],
                            in_=agv[2048 * rank + 512 * qb:2048 * rank + 512 * (qb + 1),
                                    64 * h:64 * (h + 1)].rearrange("(j p) c -> p j c", p=128)),
                           waits=(([(pe, v_gate)] +
                                   ([(dve, dve.n)] if gi == 0 else []))
                                  if hh == 0 and rank == 0 else []),
                           incs=[(ksem, 16)])
                ld_done = ksem.n
                act_kc = {}
                pe_ctx = {}
                sc_pe_k = {}
                for kc in range(NKT + 2):
                    if kc < NKT:
                        pA = PB2[kc % 2][:, 0:512]
                        pB_ = PB2[kc % 2][:, 512:1024]
                        wA = [(ksem, ld_done)] if kc == 0 else []
                        if kc >= 2:
                            wA = wA + [(act, act_kc[kc - 2])]
                        if kc <= 1:
                            wA = wA + [(act, act.n), (dve, dve.n)]
                        T(lambda t, qb=qb, hp=hp, kc=kc, par=par, pA=pA: t.matmul(
                            pA, KTILE[par][0:64, kc * 128:(kc + 1) * 128],
                            WB[0:64, hp * 2048 + qb * 512:hp * 2048 + (qb + 1) * 512],
                            start=True, stop=True, tile_position=(0, 0)),
                          waits=wA, incs=[pe])
                        T(lambda t, qb=qb, hp=hp, kc=kc, par=par, pB_=pB_: t.matmul(
                            pB_, KTILE[par][64:128, kc * 128:(kc + 1) * 128],
                            WB[64:128, hp * 2048 + qb * 512:hp * 2048 + (qb + 1) * 512],
                            start=True, stop=True, tile_position=(64, 0)),
                          incs=[pe])
                        sc_pe_k[kc] = pe.n
                    kd = kc - 2
                    if kd >= 0:
                        T(lambda t, kd=kd: t.matmul(
                            PB[4][0:65, 0:512], VTILE[0][:, kd * 65:(kd + 1) * 65],
                            PR[:, (kd % 2) * 1024:(kd % 2) * 1024 + 512],
                            start=(kd == 0), stop=(kd == NKT - 1)),
                          waits=[(act, act_kc[kd])] + ([(dve, dve.n)] if kd == 0 else []))
                        T(lambda t, kd=kd: t.matmul(
                            PB[5][0:65, 0:512], VTILE[1][:, kd * 65:(kd + 1) * 65],
                            PR[:, (kd % 2) * 1024 + 512:(kd % 2) * 1024 + 1024],
                            start=(kd == 0), stop=(kd == NKT - 1)),
                          incs=[pe])
                        pe_ctx[kd] = pe.n
                    if kc < NKT:
                        A_(lambda s, kc=kc: s.activation(
                            PR[:, (kc % 2) * 1024:(kc % 2) * 1024 + 1024],
                            PB2[kc % 2][:, 0:1024], AF.Exp,
                            bias=CONSTS[:, 1:2], scale=0.125),
                           waits=[(pe, sc_pe_k[kc])] +
                                 ([(pe, pe_ctx[kc - 2])] if kc >= 2 else []))
                        act_kc[kc] = act.n
                sc_pe = sc_pe_k[NKT - 1]
                ctx_pe = pe.n
                sc_last[qb * 8 + hp] = sc_pe
                ctx_last[qb * 8 + hp] = pe.n
                V_(lambda v: v.reciprocal(XT1[0:1, 0:512], PB[4][64:65, 0:512]),
                   waits=[(pe, ctx_pe), (dve, dve.n)])
                V_(lambda v: v.reciprocal(OS[0:1, 0:512], PB[5][64:65, 0:512]),
                   waits=[(dve, dve.n)])
                rc_dve = dve.n
                T(lambda t: t.matmul(PB[6][0:64, 0:512], ONEF[0:1, 0:64], XT1[0:1, 0:512],
                                     start=True, stop=True),
                  waits=[(dve, rc_dve)], incs=[pe])
                T(lambda t: t.matmul(PB[6][64:128, 0:512], ONEF[0:1, 0:64], OS[0:1, 0:512],
                                     start=True, stop=True), incs=[pe])
                bc_pe = pe.n
                V_(lambda v: v.tensor_copy(SCB[:, 0:512], PB[6][:, 0:512]),
                   waits=[(pe, bc_pe), (dve, dve.n)])
                V_(lambda v, qb=qb, hp=hp: v.tensor_tensor(
                    HB[0:64, hp * 2048 + qb * 512:hp * 2048 + (qb + 1) * 512],
                    PB[4][0:64, 0:512], SCB[0:64, 0:512], ALU.mult),
                   waits=[(dve, dve.n)])
                V_(lambda v, qb=qb, hp=hp: v.tensor_tensor(
                    HB[64:128, hp * 2048 + qb * 512:hp * 2048 + (qb + 1) * 512],
                    PB[5][0:64, 0:512], SCB[64:128, 0:512], ALU.mult),
                   waits=[(dve, dve.n)], incs=[dve])
            o = qb * 512
            T(lambda t, o=o: t.matmul(PB[6][:, 0:512], ONEB[0:1, :], SROW[0:1, o:o + 512],
                                      start=True, stop=True),
              waits=[(dve, dve.n), (act, act.n)], incs=[pe])
            T(lambda t, o=o: t.matmul(PB[7][:, 0:512], ONEB[0:1, :],
                                      SROW[0:1, 2048 + o:2048 + o + 512],
                                      start=True, stop=True), incs=[pe])
            ab_pe = pe.n
            for oc in range(8):
                wsb = WS[oc % 2]
                wsem = ws_s[oc % 2]
                G_(lambda g, oc=oc, wsb=wsb: g.dma_start(
                    out=wsb[:, 0:1024].rearrange("p (c f) -> p c f", c=DC),
                    in_=woutT[:, oc * 128:(oc + 1) * 128].rearrange("(c p) f -> p c f", p=128)),
                   waits=[(pe, wout_mm.get((qb, oc - 2),
                                           wout_mm.get((qb - 1, 6 + (oc % 2)), pe.n if qb == 0 and oc < 2 else 0))),
                          (wsem, wsem.n)], incs=[(wsem, 16)])
                ldw = wsem.n
                po = PB[oc % 4][:, 0:512]
                for dc in range(DC):
                    T(lambda t, oc=oc, dc=dc, o=o, wsb=wsb, po=po: t.matmul(
                        po, wsb[:, dc * 128:(dc + 1) * 128],
                        HB[:, dc * 2048 + o:dc * 2048 + o + 512],
                        start=(dc == 0), stop=(dc == DC - 1)),
                      waits=([(ws_s[oc % 2], ldw), (act, act.n)] if dc == 0 else []) +
                            ([(dve, dve.n)] if oc == 0 and dc == 0 else []),
                      incs=[pe] if dc == DC - 1 else [])
                mm_done = pe.n
                wout_mm[(qb, oc)] = pe.n
                A_(lambda s, oc=oc, po=po: s.activation(XT1[:, 0:512], po, AF.Identity,
                                                        bias=BO[:, oc:oc + 1]),
                   waits=[(pe, mm_done), (dve, dve.n)])
                V_(lambda v, oc=oc, o=o: v.tensor_tensor(
                    OS[:, 0:512], Xc(oc, o, o + 512), PB[6][:, 0:512], ALU.mult),
                   waits=[(pe, ab_pe), (dve, dve.n), (act, act.n)])
                V_(lambda v: v.tensor_tensor(OS[:, 0:512], OS[:, 0:512],
                                             PB[7][:, 0:512], ALU.add),
                   waits=[(dve, dve.n)])
                V_(lambda v: v.tensor_tensor(OS[:, 0:512], OS[:, 0:512],
                                             XT1[:, 0:512], ALU.add),
                   waits=[(dve, dve.n)])
                V_(lambda v, oc=oc, o=o: v.tensor_tensor(
                    Xc(oc, o, o + 512), Xc(oc, o, o + 512), OS[:, 0:512], ALU.add),
                   waits=[(dve, dve.n)], incs=[dve])

    def emit_x_out():
        for dc in range(DC):
            dma_s(x_out[dc * 128:(dc + 1) * 128, :], Xc(dc), st_s,
                  waits=[(dve, dve.n), (pe, pe.n), (act, act.n)])
        SY(lambda s: s.nop(), waits=[(st_s, st_s.n), (st_g, st_g.n)])

    def emit_out():
        for ti in range(NT):
            for dc in range(DC):
                T(lambda t, ti=ti, dc=dc: t.transpose(
                    PB[dc][:, 0:128], Xc(dc, ti * 128, ti * 128 + 128), IDF[:, :]),
                  waits=[(ld0, ld0.n), (dve, dve.n)], incs=[pe])
            tp_done = pe.n
            for dc in range(DC):
                V_(lambda v, dc=dc: v.tensor_copy(OS[:, dc * 128:(dc + 1) * 128],
                                                  PB[dc][:, 0:128]),
                   waits=[(pe, tp_done), (st_s, st_s.n)], incs=[dve])
            dma_s(out_tok[ti * 128:(ti + 1) * 128, :], OS[:, :], st_s,
                  waits=[(dve, dve.n)])
        SY(lambda s: s.nop(), waits=[(st_s, st_s.n)])

    if mode in ("L1", "D_attn_a"):
        emit_ln()
        emit_h()
        emit_attn_a()
        if mode == "L1":
            emit_ln()
            emit_h()
            emit_ffn(0, stop_after_counts=True)
            emit_x_out()
        else:
            emit_out()
    elif mode == "L2":
        emit_ln()
        emit_h()
        emit_ffn(0)
        emit_ln()
        emit_h(extra_gamma=True)
        emit_attnb_kv()
        emit_x_out()
    elif mode == "L3":
        emit_ln()
        emit_h(extra_gamma=True)
        emit_attnb_attn()
        emit_ln()
        emit_h()
        emit_ffn(1, stop_after_counts=True)
        emit_x_out()
    elif mode == "L4":
        emit_ln()
        emit_h()
        emit_ffn(1)
        emit_out()
    else:
        raise ValueError(mode)

    P.finalize()
    es.close()
    return nc


def _prep_inputs(inputs):
    x = np.ascontiguousarray(inputs["L0_attn_a_inputs"].reshape(16384, 1024))
    ktp = np.ascontiguousarray(inputs["L0_attn_a_key_param_tokens"].T)
    vp = np.ascontiguousarray(inputs["L0_attn_a_value_param_tokens"])
    wqkT = np.ascontiguousarray(inputs["L0_attn_b_W_qkv"].T)
    woutT = np.ascontiguousarray(inputs["L0_attn_b_W_out"].T)
    shared = dict(
        ktp=ktp, vp=vp,
        gamma=inputs["L0_attn_b_ln_gamma"].reshape(8, 128).T,
        beta=inputs["L0_attn_b_ln_beta"].reshape(8, 128).T,
        wqkT=wqkT, bqkv=inputs["L0_attn_b_b_qkv"].reshape(24, 128).T,
        woutT=woutT, bout=inputs["L0_attn_b_b_out"].reshape(8, 128).T,
        rw=inputs["L0_ffn_router_w"], rb=inputs["L0_ffn_router_b"].reshape(1, 16),
        bqv_row=inputs["L0_attn_b_b_qkv"][2048:].reshape(1, 1024),
        swg=inputs["L0_ffn_sw_gate"], swu=inputs["L0_ffn_sw_up"],
        swd=inputs["L0_ffn_sw_down"],
        wg=inputs["L0_ffn_w_gate"], wu=inputs["L0_ffn_w_up"],
        wd=inputs["L0_ffn_w_down"],
    )
    shared = {k: np.ascontiguousarray(v, dtype=np.float32) for k, v in shared.items()}
    xt = x.reshape(128, 128, 1024)
    in_maps = []
    for c in range(NCORES):
        xc = np.ascontiguousarray(xt[c::8].reshape(2048, 1024).T)
        r = np.arange(128)
        gtile_of_row = (r % 16) * 8 + (r // 16)
        gi = 8 * np.arange(16) + c
        tm = (gtile_of_row[:, None] < gi[None, :]).astype(np.float32)
        m = dict(shared)
        m["x_fm"] = xc
        m["tilemask"] = tm
        in_maps.append(m)
    return in_maps


_PROG_CACHE = {}


def _prog(mode):
    if mode not in _PROG_CACHE:
        _PROG_CACHE[mode] = build_program(mode)
    return _PROG_CACHE[mode]


def kernel(**inputs):
    base_maps = _prep_inputs(inputs)
    cores = list(range(NCORES))

    r1 = run_bass_kernel_spmd(_prog("L1"), base_maps, core_ids=cores)
    cnt_all1 = np.concatenate([r1.results[c]["o_cnt"] for c in cores], axis=0)
    maps2 = []
    for c in cores:
        m = dict(base_maps[c])
        m["x_fm"] = r1.results[c]["x_out"]
        m["i_cntall"] = cnt_all1
        maps2.append(m)
    r2 = run_bass_kernel_spmd(_prog("L2"), maps2, core_ids=cores)
    agk = np.concatenate([r2.results[c]["o_kt"] for c in cores], axis=0)
    agv = np.concatenate([r2.results[c]["o_v"] for c in cores], axis=0)
    maps3 = []
    for c in cores:
        m = dict(base_maps[c])
        m["x_fm"] = r2.results[c]["x_out"]
        m["i_agk"] = agk
        m["i_agv"] = agv
        maps3.append(m)
    r3 = run_bass_kernel_spmd(_prog("L3"), maps3, core_ids=cores)
    cnt_all2 = np.concatenate([r3.results[c]["o_cnt"] for c in cores], axis=0)
    maps4 = []
    for c in cores:
        m = dict(base_maps[c])
        m["x_fm"] = r3.results[c]["x_out"]
        m["i_cntall"] = cnt_all2
        maps4.append(m)
    r4 = run_bass_kernel_spmd(_prog("L4"), maps4, core_ids=cores)
    out = np.empty((128, 128, 1024), np.float32)
    for c in cores:
        out[c::8] = r4.results[c]["out_tok"].reshape(16, 128, 1024)
    return out.reshape(4, 4096, 1024)


# version 05
# speedup vs baseline: 1.0344x; 1.0344x over previous
import os

import jax

jax.config.update("jax_compilation_cache_dir",
                  os.path.join(os.path.dirname(os.path.abspath(__file__)), "_jaxcache"))
jax.config.update("jax_persistent_cache_min_compile_time_secs", 0.0)
jax.config.update("jax_persistent_cache_min_entry_size_bytes", 0)

import numpy as np
from contextlib import ExitStack

import concourse.bass as bass
import concourse.mybir as mybir
from concourse.bass_utils import run_bass_kernel_spmd
from concourse.bass_core_rust import IndirectOffsetOnAxis

F32 = mybir.dt.float32
BF16 = mybir.dt.bfloat16
I32 = mybir.dt.int32
AF = mybir.ActivationFunctionType
ALU = mybir.AluOpType
AX = mybir.AxisListType

D = 1024
TL = 2048
NT = 16
NCORES = 8
S = 4096
FF = 1408
E = 16
CLOC = 448
CTOT = E * CLOC
CAP = 2560
NH = 16
HD = 64
DC = 8
FC = 11
TB = 4
QB = 512
NKT = S // 128
EXP_BIAS = -20.0

DEV_PHASES = None


class Sem:
    __slots__ = ("h", "n", "name")

    def __init__(self, h, name):
        self.h = h
        self.n = 0
        self.name = name


class Prog:
    ENGINES = ("tensor", "vector", "scalar", "gpsimd", "sync")

    def __init__(self, nc):
        self.nc = nc
        self.q = {e: [] for e in self.ENGINES}
        self.sems = []
        self.max_wait = {}

    def sem(self, name):
        s = Sem(self.nc.alloc_semaphore(name), name)
        self.sems.append(s)
        return s

    def emit(self, eng, f, waits=(), incs=()):
        ws = []
        for item in waits:
            if isinstance(item, Sem):
                s, v = item, item.n
            else:
                s, v = item
            assert v <= s.n, f"wait {s.name}>={v} but only {s.n} issued"
            self.max_wait[s.name] = max(self.max_wait.get(s.name, 0), v)
            if v > 0:
                ws.append((s.h, v))
        ics = []
        for item in incs:
            if isinstance(item, tuple):
                s, k = item
            else:
                s, k = item, 1
            s.n += k
            ics.append((s.h, k))

        def run(e, f=f, ws=ws, ics=ics):
            for (h, v) in ws:
                e.wait_ge(h, v)
            inst = f(e)
            for (h, k) in ics:
                inst.then_inc(h, k)

        self.q[eng].append(run)

    def finalize(self):
        with self.nc.Block() as block:
            for eng in self.ENGINES:
                items = self.q[eng]

                def body(e, items=items):
                    for f in items:
                        f(e)

                getattr(block, eng)(body)


def _host_constants():
    c = {}
    c["ident"] = np.eye(128, dtype=np.float32)
    c["ones_col"] = np.ones((128, 1), np.float32)
    c["ones_row"] = np.ones((1, 128), np.float32)
    c["lstrict"] = np.triu(np.ones((128, 128), np.float32), 1)
    c["lstrict16"] = np.triu(np.ones((16, 16), np.float32), 1)
    c["iota16"] = np.tile(np.arange(16, dtype=np.float32), (128, 1))
    return c


def build_program(mode="L1"):
    nc = bass.Bass("TRN2", target_bir_lowering=False)
    P = Prog(nc)
    es = ExitStack()

    t_in = {}
    def ext(name, shape, dtype=F32):
        t_in[name] = nc.dram_tensor(name, list(shape), dtype, kind="ExternalInput").ap()
        return t_in[name]

    x_fm = ext("x_fm", [D, TL])
    tilemask = ext("tilemask", [128, NT])
    if mode in ("L2", "L4"):
        i_cntall = ext("i_cntall", [NCORES * NT, E])
    if mode == "L3":
        i_agk = ext("i_agk", [NCORES * D, TL], BF16)
        i_agv = ext("i_agv", [NCORES * TL, D], BF16)
    bqv_row = ext("bqv_row", [1, D])
    ktp = ext("ktp", [D, S])
    vp = ext("vp", [S, D])
    gamma = ext("gamma", [128, DC])
    beta = ext("beta", [128, DC])
    wqkT = ext("wqkT", [D, 3 * D])
    bqkv = ext("bqkv", [128, 24])
    woutT = ext("woutT", [D, D])
    bout = ext("bout", [128, DC])
    rw = ext("rw", [D, E])
    rb = ext("rb", [1, E])
    swg = ext("swg", [D, FF])
    swu = ext("swu", [D, FF])
    swd = ext("swd", [FF, D])
    wg = ext("wg", [E, D, FF])
    wu = ext("wu", [E, D, FF])
    wd = ext("wd", [E, FF, D])

    if mode in ("L4", "D_attn_a", "D_ffn1"):
        out_tok = nc.dram_tensor("out_tok", [TL, D], F32, kind="ExternalOutput").ap()
    else:
        x_out = nc.dram_tensor("x_out", [D, TL], F32, kind="ExternalOutput").ap()
    if mode in ("L1", "L3"):
        o_cnt = nc.dram_tensor("o_cnt", [NT, E], F32, kind="ExternalOutput").ap()
    if mode == "L2":
        o_kt = nc.dram_tensor("o_kt", [D, TL], BF16, kind="ExternalOutput").ap()
        o_v = nc.dram_tensor("o_v", [TL, D], BF16, kind="ExternalOutput").ap()

    hfm_d = nc.dram_tensor("hfm_d", [D, TL], BF16, kind="Internal").ap()
    exp_in = nc.dram_tensor("exp_in", [CTOT + 128, D], BF16, kind="Internal").ap()
    y_tok = nc.dram_tensor("y_tok", [CTOT + 128, D], BF16, kind="Internal").ap()
    z_tok = nc.dram_tensor("z_tok", [TL, D], BF16, kind="Internal").ap()
    cnt_in = [nc.dram_tensor(f"cnt_in{i}", [NT, E], F32, kind="Internal").ap() for i in range(2)]
    cnt_out = [nc.dram_tensor(f"cnt_out{i}", [NCORES * NT, E], F32, kind="Internal").ap() for i in range(2)]
    bounce = nc.dram_tensor("bounce", [4, 256], F32, kind="Internal").ap()
    agk_in = nc.dram_tensor("agk_in", [D, TL], BF16, kind="Internal").ap()
    agk_out = nc.dram_tensor("agk_out", [NCORES * D, TL], BF16, kind="Internal").ap()
    agv_in = nc.dram_tensor("agv_in", [TL, D], BF16, kind="Internal").ap()
    agv_out = nc.dram_tensor("agv_out", [NCORES * TL, D], BF16, kind="Internal").ap()

    HC = _host_constants()
    ident_f = nc.inline_tensor(HC["ident"], "identf").ap()
    consts_np = {
        "identb": HC["ident"],
        "ones_colb": HC["ones_col"],
        "lstrictb": HC["lstrict"],
    }

    sb = lambda name, shape, dt: es.enter_context(nc.sbuf_tensor(name, list(shape), dt))
    X = sb("X", [128, DC * TL], F32)
    HB = sb("HB", [128, DC * TL], BF16)
    WB = sb("WB", [128, 16384], BF16)
    SROW = sb("SROW", [128, 4096], BF16)
    SCB = sb("SCB", [128, 512], F32)
    XT1 = sb("XT1", [128, 512], F32)
    PR = sb("PR", [128, 4096], BF16)
    YS = sb("YS", [128, 2048], BF16)
    OS = sb("OS", [128, 1024], F32)
    WS = [sb(f"WS{i}", [128, 3072], BF16) for i in range(2)]
    VS = [sb(f"VS{i}", [128, 1024], BF16) for i in range(2)]
    KTILE = [sb(f"KT{i}", [128, 4096], BF16) for i in range(2)]
    VTILE = [sb(f"VT{i}", [128, 2080], BF16) for i in range(2)]
    IDB = sb("IDB", [128, 128], BF16)
    IDF = sb("IDF", [128, 128], F32)
    ONEB = sb("ONEB", [128, 128], BF16)
    ONEF = sb("ONEF", [128, 128], F32)
    CONSTS = sb("CONSTS", [128, 8], F32)
    LSTR = sb("LSTR", [128, 128], F32)
    LSTR16 = sb("LSTR16", [16, 16], F32)
    IOTA = sb("IOTA", [128, 16], F32)
    TMASK = sb("TMASK", [128, NT], F32)
    GAM = sb("GAM", [128, DC], F32)
    BET = sb("BET", [128, DC], F32)
    BQ = sb("BQ", [128, 24], F32)
    BO = sb("BO", [128, DC], F32)
    RBS = sb("RBS", [1, E], BF16)
    RWS = sb("RWS", [128, DC * E], BF16)
    RT = sb("RT", [128, 16 * NT], F32)
    RTI = sb("RTI", [128, 4 * NT], I32)
    LG = sb("LG", [128, NT * E], F32)
    OH1 = sb("OH1", [128, NT * E], F32)
    OH2 = sb("OH2", [128, NT * E], F32)
    RSCR = sb("RSCR", [128, 64], F32)
    CNTL = sb("CNTL", [16, E], F32)
    CSX = sb("CSX", [128, NT * E], F32)
    CNTS = sb("CNTS", [128, E], F32)
    ZEROT = sb("ZEROT", [128, 512], BF16)

    ONEC = ONEB[:, 0:1]
    PB2 = [es.enter_context(nc.psum_tensor(f"PB2_{i}", [128, 1024], F32)) for i in range(4)]
    PB = [PB2[i // 2][:, (i % 2) * 512:(i % 2) * 512 + 512] for i in range(8)]

    pe = P.sem("pe")
    dve = P.sem("dve")
    act = P.sem("act")
    pool = P.sem("pool")
    ld0 = P.sem("ld0")
    ldg = P.sem("ldg")
    ws_s = [P.sem(f"ws{i}") for i in range(2)]
    kv_s = [P.sem(f"kv{i}") for i in range(2)]
    st_s = P.sem("st_s")
    st_g = P.sem("st_g")
    zi = P.sem("zi")
    cc = P.sem("cc")
    mg = P.sem("mg")
    tl_s = [P.sem(f"tl{i}") for i in range(2)]

    def T(f, waits=(), incs=(pe,)):
        P.emit("tensor", f, waits, incs)

    def V_(f, waits=(), incs=(dve,)):
        P.emit("vector", f, waits, incs)

    def A_(f, waits=(), incs=(act,)):
        P.emit("scalar", f, waits, incs)

    def G_(f, waits=(), incs=()):
        P.emit("gpsimd", f, waits, incs)

    def SY(f, waits=(), incs=()):
        P.emit("sync", f, waits, incs)

    def dma_g(out, in_, sem, waits=()):
        G_(lambda g, o=out, i=in_: g.dma_start(out=o, in_=i), waits, incs=[(sem, 16)])

    def dma_s(out, in_, sem, waits=()):
        SY(lambda s, o=out, i=in_: s.dma_start(out=o, in_=i), waits, incs=[(sem, 16)])

    def dma_s_tr(out, in_, sem, waits=()):
        SY(lambda s, o=out, i=in_: s.dma_start(out=o, in_=i, transpose=True), waits,
           incs=[(sem, 16)])

    def Xc(dc, lo=0, hi=TL):
        return X[:, dc * TL + lo: dc * TL + hi]

    def Hc(dc, lo=0, hi=TL):
        return HB[:, dc * TL + lo: dc * TL + hi]

    inline_aps = {}
    for nm, arr in consts_np.items():
        inline_aps[nm] = nc.inline_tensor(arr.astype(np.float32), nm).ap()
    l16 = nc.inline_tensor(HC["lstrict16"], "l16np").ap()
    iota_t = nc.inline_tensor(HC["iota16"], "iotanp").ap()

    dma_s(IDF[:, :], ident_f, ld0)
    dma_s(IOTA[:, :], iota_t, ld0)
    dma_s(GAM[:, :], gamma, ld0)
    dma_s(BET[:, :], beta, ld0)
    dma_s(BQ[:, :], bqkv, ld0)
    dma_s(BO[:, :], bout, ld0)
    dma_s(X[:, :].rearrange("p (c t) -> p c t", c=DC),
          x_fm.rearrange("(c p) t -> p c t", p=128), ld0)
    dma_g(IDB[:, :], inline_aps["identb"], ldg)
    dma_s(LSTR[:, :], inline_aps["lstrictb"], ld0)
    dma_s(LSTR16[:, :], l16, ld0)
    dma_s(TMASK[:, :], tilemask, ld0)
    dma_g(RBS[:, :], rb, ldg)
    dma_g(RWS[:, :].rearrange("p (c e) -> p c e", c=DC),
          rw.rearrange("(c p) e -> p c e", p=128), ldg)
    V_(lambda v: v.memset(ONEB[:, :], 1.0))
    V_(lambda v: v.memset(ONEF[:, :], 1.0))
    V_(lambda v: v.memset(CONSTS[:, 0:1], 1e-5))
    V_(lambda v: v.memset(CONSTS[:, 1:2], EXP_BIAS))
    V_(lambda v: v.memset(ZEROT[:, :], 0.0))
    for i in range(2):
        V_(lambda v, i=i: v.memset(VTILE[i][:, :], 1.0))
    init_dve = dve.n
    for r in range(0, CTOT, 128):
        for cc2 in range(2):
            dma_g(exp_in[r:r + 128, cc2 * 512:(cc2 + 1) * 512], ZEROT[:, :], zi,
                  waits=[(dve, init_dve)])
    for cc2 in range(2):
        dma_g(y_tok[CTOT:CTOT + 128, cc2 * 512:(cc2 + 1) * 512], ZEROT[:, :], zi,
              waits=[(dve, init_dve)])
    init_stg = zi.n

    def emit_ln():
        for tb in range(TB):
            o = tb * 512
            ps = PB[6][0:1, 0:512]
            ps2 = PB[7][0:1, 0:512]
            pe_gate = pe.n
            dve_gate = dve.n
            act_gate = act.n
            for dc in range(DC):
                V_(lambda v, dc=dc, o=o: v.tensor_copy(
                    PR[:, dc * 512:(dc + 1) * 512], Xc(dc, o, o + 512)),
                   waits=[(pe, pe_gate), (dve, dve_gate), (act, act_gate),
                          (ld0, ld0.n), (ldg, ldg.n)])
            cast_done = dve.n
            for dc in range(DC):
                V_(lambda v, dc=dc: v.tensor_mul(
                    WB[:, dc * 512:(dc + 1) * 512],
                    PR[:, dc * 512:(dc + 1) * 512], PR[:, dc * 512:(dc + 1) * 512]),
                   waits=[(dve, cast_done)])
            mul_done = dve.n
            for dc in range(DC):
                T(lambda t, dc=dc, ps=ps: t.matmul(
                    ps, ONEC[:, :], PR[:, dc * 512:(dc + 1) * 512],
                    start=(dc == 0), stop=(dc == DC - 1)),
                  waits=([(dve, cast_done), (ldg, ldg.n)] if dc == 0 else []),
                  incs=[pe] if dc == DC - 1 else [])
            for dc in range(DC):
                T(lambda t, dc=dc, ps2=ps2: t.matmul(
                    ps2, ONEC[:, :], WB[:, dc * 512:(dc + 1) * 512],
                    start=(dc == 0), stop=(dc == DC - 1)),
                  waits=([(dve, mul_done)] if dc == 0 else []),
                  incs=[pe] if dc == DC - 1 else [])
            mm_done = pe.n
            s1 = XT1[0:1, 0:512]; s2 = SCB[0:1, 0:512]; s3 = OS[0:1, 0:512]
            V_(lambda v, ps=ps, s1=s1: v.tensor_scalar_mul(s1, ps, 1.0 / D),
               waits=[(pe, mm_done), (dve, dve.n), (act, act.n)])
            V_(lambda v, s1=s1, s2=s2: v.tensor_mul(s2, s1, s1),
               waits=[(dve, dve.n)])
            V_(lambda v, ps2=ps2, s3=s3: v.tensor_scalar_mul(s3, ps2, 1.0 / D),
               waits=[(pe, mm_done), (dve, dve.n)])
            V_(lambda v, s2=s2, s3=s3: v.tensor_sub(s3, s3, s2),
               waits=[(dve, dve.n)])
            A_(lambda s, s2=s2, s3=s3: s.activation(s2, s3, AF.Sqrt, bias=CONSTS[0:1, 0:1]),
               waits=[(dve, dve.n)])
            V_(lambda v, s2=s2, s3=s3: v.reciprocal(s3, s2),
               waits=[(act, act.n), (dve, dve.n)])
            V_(lambda v, s3=s3, o=o: v.tensor_copy(SROW[0:1, o:o + 512], s3),
               waits=[(dve, dve.n)])
            V_(lambda v, s1=s1, s2=s2, s3=s3: v.tensor_mul(s2, s1, s3),
               waits=[(dve, dve.n)])
            V_(lambda v, s2=s2, o=o: v.tensor_scalar(
                SROW[0:1, 2048 + o:2048 + o + 512], s2, -1.0, None, ALU.mult),
               waits=[(dve, dve.n)])

    def emit_h(extra_gamma=False):
        for tb in range(TB):
            o = tb * 512
            pa = PB[6][:, 0:512]
            pb_ = PB[7][:, 0:512]
            T(lambda t, o=o, pa=pa: t.matmul(pa, ONEB[0:1, :], SROW[0:1, o:o + 512],
                                             start=True, stop=True),
              waits=[(dve, dve.n), (pe, pe.n)], incs=[pe])
            T(lambda t, o=o, pb_=pb_: t.matmul(pb_, ONEB[0:1, :], SROW[0:1, 2048 + o:2048 + o + 512],
                                               start=True, stop=True),
              waits=[], incs=[pe])
            bc_done = pe.n
            for dc in range(DC):
                V_(lambda v, dc=dc, o=o, pa=pa: v.tensor_tensor(
                    XT1[:, 0:512], Xc(dc, o, o + 512), pa, ALU.mult),
                   waits=[(pe, bc_done), (dve, dve.n)])
                if not extra_gamma:
                    V_(lambda v, dc=dc, o=o, pb_=pb_: v.tensor_tensor(
                        Hc(dc, o, o + 512), XT1[:, 0:512], pb_, ALU.add),
                       waits=[(dve, dve.n)])
                else:
                    V_(lambda v, dc=dc, o=o, pb_=pb_: v.tensor_tensor(
                        XT1[:, 0:512], XT1[:, 0:512], pb_, ALU.add),
                       waits=[(dve, dve.n)])
                    V_(lambda v, dc=dc, o=o: v.tensor_scalar(
                        Hc(dc, o, o + 512), XT1[:, 0:512],
                        GAM[:, dc:dc + 1], BET[:, dc:dc + 1], ALU.mult, ALU.add),
                       waits=[(dve, dve.n)])

    def emit_attn_a():
        act_a_gate = {}
        k_done = {}
        g2_done = {}
        for tb in range(TB):
            o = tb * 512
            pss = PB[2][0:1, 0:512]
            g2_dve = {}
            for sc in range(NKT + 2):
              if sc < NKT:
                wsb = WS[sc % 2]
                wsem = ws_s[sc % 2]
                dma_g(wsb[:, 0:1024].rearrange("p (c f) -> p c f", c=DC),
                      ktp[:, sc * 128:(sc + 1) * 128].rearrange("(c p) f -> p c f", p=128),
                      wsem, waits=[(pe, k_done.get((tb, sc - 2),
                                                  k_done.get((tb - 1, NKT - 2 + (sc % 2)), 0))),
                                   (wsem, max(0, wsem.n - 16))])
                ldw = wsem.n
                pscore = PB[sc % 2][:, 0:512]
                for dc in range(DC):
                    T(lambda t, dc=dc, wsb=wsb, pscore=pscore, o=o: t.matmul(
                        pscore, wsb[:, dc * 128:(dc + 1) * 128], Hc(dc, o, o + 512),
                        start=(dc == 0), stop=(dc == DC - 1)),
                      waits=([(ws_s[sc % 2], ldw)] +
                             ([(act, act_a_gate[sc - 2])] if sc >= 2 else
                              [(dve, dve.n), (act, act.n)]) if dc == 0 else []),
                      incs=[pe] if dc == DC - 1 else [])
                k_done[(tb, sc)] = pe.n
              sd = sc - 2
              if sd >= 0:
                T(lambda t, sd=sd, pss=pss: t.matmul(
                    pss, ONEC[:, :], PR[:, (sd % 2) * 512:(sd % 2) * 512 + 512],
                    start=(sd == 0), stop=(sd == NKT - 1)),
                  waits=[(dve, g2_dve[sd])], incs=[pe])
                g2_done[sd] = pe.n
              if sc < NKT:
                A_(lambda s, sc=sc, pscore=pscore: s.activation(
                    WB[:, sc * 512:(sc + 1) * 512], pscore, AF.Gelu),
                   waits=[(pe, k_done[(tb, sc)])])
                act_a_gate[sc] = act.n
                V_(lambda v, sc=sc: v.tensor_mul(
                    PR[:, (sc % 2) * 512:(sc % 2) * 512 + 512],
                    WB[:, sc * 512:(sc + 1) * 512], WB[:, sc * 512:(sc + 1) * 512]),
                   waits=[(act, act.n)] + ([(pe, g2_done[sc - 2])] if sc >= 2 else [(pe, pe.n)]))
                g2_dve[sc] = dve.n
            A_(lambda s, pss=pss: s.activation(XT1[0:1, 0:512], pss, AF.Sqrt),
               waits=[(pe, pe.n), (dve, dve.n)])
            V_(lambda v: v.reciprocal(OS[0:1, 0:512], XT1[0:1, 0:512]),
               waits=[(act, act.n), (dve, dve.n)])
            V_(lambda v: v.tensor_scalar_mul(XT1[0:1, 0:512], OS[0:1, 0:512], 64.0),
               waits=[(dve, dve.n)])
            T(lambda t: t.matmul(PB[2][:, 0:512], ONEF[0:1, :], XT1[0:1, 0:512],
                                 start=True, stop=True),
              waits=[(dve, dve.n)], incs=[pe])
            V_(lambda v: v.tensor_copy(SCB[:, :], PB[2][:, 0:512]),
               waits=[(pe, pe.n)])
            scb_done = dve.n
            ctx_sc = {}
            for sc in range(NKT):
                vsb = VS[sc % 2]
                vsem = kv_s[sc % 2]
                dma_g(vsb[:, 0:1024],
                      vp[sc * 128:(sc + 1) * 128, :], vsem,
                      waits=[(pe, ctx_sc.get((tb, sc - 2), ctx_sc.get((tb - 1, NKT - 2 + (sc % 2)), 0))),
                             (vsem, max(0, vsem.n - 16))])
                ldw = vsem.n
                for vc in range(DC):
                    T(lambda t, sc=sc, vc=vc, vsb=vsb, o=o: t.matmul(
                        PB[vc][:, 0:512], vsb[:, vc * 128:(vc + 1) * 128],
                        WB[:, sc * 512:(sc + 1) * 512],
                        start=(sc == 0), stop=(sc == NKT - 1)),
                      waits=([(kv_s[sc % 2], ldw)] if vc == 0 else []) +
                            ([(dve, scb_done)] if sc == 0 and vc == 0 else []),
                      incs=[pe] if vc == DC - 1 else [])
                ctx_sc[(tb, sc)] = pe.n
            ctx_done = pe.n
            for vc in range(DC):
                V_(lambda v, vc=vc: v.tensor_tensor(XT1[:, 0:512], PB[vc][:, 0:512],
                                                    SCB[:, :], ALU.mult),
                   waits=[(pe, ctx_done), (dve, dve.n), (act, act.n)])
                V_(lambda v, vc=vc, o=o: v.tensor_tensor(
                    Xc(vc, o, o + 512), Xc(vc, o, o + 512), XT1[:, 0:512], ALU.add),
                   waits=[(dve, dve.n)])

    moe_gather_done = [0, 0]

    def emit_ffn(app, stop_after_counts=False):
        h_act = act.n
        h_dve = dve.n
        for ti in range(NT):
            pl = PB[4 + ti % 2][:, 0:E]
            for dc in range(DC):
                T(lambda t, ti=ti, dc=dc, pl=pl: t.matmul(
                    pl, Hc(dc, ti * 128, (ti + 1) * 128), RWS[:, dc * E:(dc + 1) * E],
                    start=(dc == 0), stop=False),
                  waits=([(dve, dve.n), (ldg, ldg.n), (act, act.n)] if dc == 0 else []))
            T(lambda t, pl=pl: t.matmul(pl, ONEB[0:1, :], RBS[:, :],
                                        start=False, stop=True), incs=[pe])
            V_(lambda v, ti=ti, pl=pl: v.tensor_copy(LG[:, ti * E:(ti + 1) * E], pl),
               waits=[(pe, pe.n), (dve, dve.n)])
        qI1, qI2, qW1, qW2, qPL1, qPL2, qPG1, qPG2, qG1, qG2 = range(10)

        def rt(q):
            return RT[:, q * NT:(q + 1) * NT]

        for ti in range(NT):
            Lsl = LG[:, ti * E:(ti + 1) * E]
            t0 = RSCR[:, 0:E]; t1 = RSCR[:, E:2 * E]; t2 = RSCR[:, 2 * E:3 * E]
            c0 = RSCR[:, 3 * E:3 * E + 1]; c1 = RSCR[:, 3 * E + 1:3 * E + 2]
            V_(lambda v, Lsl=Lsl, c0=c0: v.reduce_max(c0, Lsl, axis=AX.X),
               waits=[(dve, dve.n)])
            V_(lambda v, c0=c0, c1=c1: v.tensor_scalar_mul(c1, c0, -1.0),
               waits=[(dve, dve.n)])
            A_(lambda s, Lsl=Lsl, t0=t0, c1=c1: s.activation(t0, Lsl, AF.Exp, bias=c1),
               waits=[(dve, dve.n)])
            V_(lambda v, t0=t0, c0=c0: v.reduce_sum(c0, t0, axis=AX.X),
               waits=[(act, act.n), (dve, dve.n)])
            V_(lambda v, c0=c0, c1=c1: v.reciprocal(c1, c0), waits=[(dve, dve.n)])
            V_(lambda v, t0=t0, c1=c1: v.tensor_scalar_mul(t0, t0, c1),
               waits=[(dve, dve.n)])
            V_(lambda v, t0=t0, c0=c0: v.reduce_max(c0, t0, axis=AX.X),
               waits=[(dve, dve.n)])
            V_(lambda v, ti=ti, t0=t0, c0=c0: v.tensor_scalar(
                OH1[:, ti * E:(ti + 1) * E], t0, c0, None, ALU.is_ge),
               waits=[(dve, dve.n)])
            V_(lambda v, ti=ti, t1=t1: v.tensor_mul(t1, IOTA[:, :], OH1[:, ti * E:(ti + 1) * E]),
               waits=[(dve, dve.n), (ld0, ld0.n)])
            V_(lambda v, ti=ti, t1=t1: v.reduce_max(rt(qI1)[:, ti:ti + 1], t1, axis=AX.X),
               waits=[(dve, dve.n)])
            V_(lambda v, ti=ti, c0=c0: v.tensor_copy(rt(qW1)[:, ti:ti + 1], c0),
               waits=[(dve, dve.n)])
            V_(lambda v, ti=ti, t1=t1: v.tensor_scalar_mul(t1, OH1[:, ti * E:(ti + 1) * E], 1e9),
               waits=[(dve, dve.n)])
            V_(lambda v, t0=t0, t1=t1: v.tensor_sub(t0, t0, t1), waits=[(dve, dve.n)])
            V_(lambda v, t0=t0, c0=c0: v.reduce_max(c0, t0, axis=AX.X),
               waits=[(dve, dve.n)])
            V_(lambda v, ti=ti, t0=t0, c0=c0: v.tensor_scalar(
                OH2[:, ti * E:(ti + 1) * E], t0, c0, None, ALU.is_ge),
               waits=[(dve, dve.n)])
            V_(lambda v, ti=ti, t1=t1: v.tensor_mul(t1, IOTA[:, :], OH2[:, ti * E:(ti + 1) * E]),
               waits=[(dve, dve.n)])
            V_(lambda v, ti=ti, t1=t1: v.reduce_max(rt(qI2)[:, ti:ti + 1], t1, axis=AX.X),
               waits=[(dve, dve.n)])
            V_(lambda v, ti=ti, c0=c0: v.tensor_copy(rt(qW2)[:, ti:ti + 1], c0),
               waits=[(dve, dve.n)])
        V_(lambda v: v.tensor_add(RSCR[:, 0:NT], rt(qW1), rt(qW2)), waits=[(dve, dve.n)])
        V_(lambda v: v.tensor_scalar_add(RSCR[:, 0:NT], RSCR[:, 0:NT], 1e-9),
           waits=[(dve, dve.n)])
        V_(lambda v: v.reciprocal(RSCR[:, NT:2 * NT], RSCR[:, 0:NT]), waits=[(dve, dve.n)])
        V_(lambda v: v.tensor_mul(rt(qW1), rt(qW1), RSCR[:, NT:2 * NT]), waits=[(dve, dve.n)])
        V_(lambda v: v.tensor_mul(rt(qW2), rt(qW2), RSCR[:, NT:2 * NT]), waits=[(dve, dve.n)])
        topk_dve = dve.n
        for ti in range(NT):
            V_(lambda v, ti=ti: v.tensor_add(CSX[:, ti * E:(ti + 1) * E],
                                             OH1[:, ti * E:(ti + 1) * E],
                                             OH2[:, ti * E:(ti + 1) * E]),
               waits=[(dve, dve.n)])
        cnt_dve = dve.n
        for ti in range(NT):
            T(lambda t, ti=ti: t.matmul(PB[6][0:1, ti * E:(ti + 1) * E], ONEF[:, 0:1],
                                        CSX[:, ti * E:(ti + 1) * E],
                                        start=True, stop=True),
              waits=([(dve, cnt_dve), (ld0, ld0.n)] if ti == 0 else []),
              incs=[pe] if ti == NT - 1 else [])
        V_(lambda v: v.tensor_copy(XT1[0:1, 0:NT * E], PB[6][0:1, 0:NT * E]),
           waits=[(pe, pe.n), (dve, dve.n)])
        dma_g(cnt_in[app].rearrange("t e -> (t e)").rearrange("(o x) -> o x", o=1),
              XT1[0:1, 0:NT * E], st_g, waits=[(dve, dve.n)])
        if stop_after_counts:
            dma_s(o_cnt.rearrange("t e -> (t e)").rearrange("(o x) -> o x", o=1),
                  XT1[0:1, 0:NT * E], st_s, waits=[(dve, dve.n)])
            return
        for dc in range(DC):
            dma_s(hfm_d[dc * 128:(dc + 1) * 128, :], Hc(dc), st_s,
                  waits=[(act, h_act), (dve, h_dve)])
        dma_s(CNTS[:, :], i_cntall, ld0)
        dma_s(CNTL[:, :], cnt_in[app], ld0, waits=[(st_g, st_g.n)])
        ag_ld = ld0.n
        T(lambda t: t.matmul(PB[6][0:16, 0:E], LSTR16[:, :], CNTL[:, :],
                             start=True, stop=True),
          waits=[(ld0, ag_ld), (dve, dve.n)], incs=[pe])
        T(lambda t: t.matmul(PB[7][0:16, 0:E], TMASK[:, :], CNTS[:, :],
                             start=True, stop=True),
          waits=[(ld0, ag_ld)], incs=[pe])
        off_pe = pe.n
        V_(lambda v: v.tensor_copy(OS[0:16, 0:E], PB[6][0:16, 0:E]),
           waits=[(pe, off_pe), (dve, dve.n)])
        V_(lambda v: v.tensor_copy(OS[0:16, E:2 * E], PB[7][0:16, 0:E]),
           waits=[(dve, dve.n)])
        dma_s(bounce[0:1, :].rearrange("o (t e) -> (o t) e", e=E), OS[0:16, 0:E],
              st_s, waits=[(dve, dve.n)])
        dma_s(bounce[1:2, :].rearrange("o (t e) -> (o t) e", e=E), OS[0:16, E:2 * E],
              st_s, waits=[(dve, dve.n)])
        dma_s(SCB[0:1, 0:256], bounce[0:1, :], ld0, waits=[(st_s, st_s.n)])
        dma_s(OS[0:1, 0:256], bounce[1:2, :], ld0, waits=[(st_s, st_s.n)])
        offrow_ld = ld0.n
        for ti in range(NT):
            pcs = PB[4][:, 0:E]
            plo = PB[5][:, 0:E]
            pgo = PB[5][:, E:2 * E]
            T(lambda t, ti=ti, pcs=pcs: t.matmul(pcs, LSTR[:, :], OH1[:, ti * E:(ti + 1) * E],
                                                 start=True, stop=False),
              waits=[(dve, dve.n), (ld0, offrow_ld)] if ti == 0 else [(dve, dve.n)])
            T(lambda t, ti=ti, pcs=pcs: t.matmul(pcs, LSTR[:, :], OH2[:, ti * E:(ti + 1) * E],
                                                 start=False, stop=True), incs=[pe])
            T(lambda t, ti=ti, plo=plo: t.matmul(plo, ONEF[0:1, :], SCB[0:1, ti * E:(ti + 1) * E],
                                                 start=True, stop=True), incs=[pe])
            T(lambda t, ti=ti, pgo=pgo: t.matmul(pgo, ONEF[0:1, :], OS[0:1, ti * E:(ti + 1) * E],
                                                 start=True, stop=True), incs=[pe])
            pos_pe = pe.n
            V_(lambda v, ti=ti, pcs=pcs: v.tensor_copy(CSX[:, ti * E:(ti + 1) * E], pcs),
               waits=[(pe, pos_pe), (dve, dve.n)])
            t0 = RSCR[:, 0:E]; t1 = RSCR[:, E:2 * E]
            V_(lambda v, ti=ti, plo=plo, t0=t0: v.tensor_tensor(
                t0, CSX[:, ti * E:(ti + 1) * E], plo, ALU.add), waits=[(dve, dve.n)])
            V_(lambda v, ti=ti, pgo=pgo, t1=t1: v.tensor_tensor(
                t1, CSX[:, ti * E:(ti + 1) * E], pgo, ALU.add), waits=[(dve, dve.n)])
            for (k, q_pl, q_pg) in ((0, qPL1, qPG1), (1, qPL2, qPG2)):
                OHk = (OH1 if k == 0 else OH2)[:, ti * E:(ti + 1) * E]
                t2 = RSCR[:, 2 * E:3 * E]
                V_(lambda v, t0=t0, t2=t2, OHk=OHk: v.tensor_mul(t2, t0, OHk),
                   waits=[(dve, dve.n)])
                V_(lambda v, ti=ti, t2=t2, q=q_pl: v.reduce_sum(rt(q)[:, ti:ti + 1], t2, axis=AX.X),
                   waits=[(dve, dve.n)])
                V_(lambda v, t1=t1, t2=t2, OHk=OHk: v.tensor_mul(t2, t1, OHk),
                   waits=[(dve, dve.n)])
                V_(lambda v, ti=ti, t2=t2, q=q_pg: v.reduce_sum(rt(q)[:, ti:ti + 1], t2, axis=AX.X),
                   waits=[(dve, dve.n)])
        W1s = RSCR[:, 0:NT]; W2s = RSCR[:, NT:2 * NT]
        V_(lambda v: v.tensor_scalar(W1s, rt(qPG1), CAP * 1.0, None, ALU.is_lt),
           waits=[(dve, dve.n)])
        V_(lambda v: v.tensor_scalar(W2s, rt(qPG2), CAP * 1.0, None, ALU.is_lt),
           waits=[(dve, dve.n)])
        V_(lambda v: v.tensor_mul(rt(qG1), rt(qW1), W1s), waits=[(dve, dve.n)])
        V_(lambda v: v.tensor_mul(rt(qG2), rt(qW2), W2s), waits=[(dve, dve.n)])
        for (qi, qpl, ws_, col) in ((qI1, qPL1, W1s, 0), (qI2, qPL2, W2s, 1)):
            f0 = RSCR[:, 2 * NT:3 * NT]
            V_(lambda v, qi=qi, f0=f0: v.tensor_scalar(f0, rt(qi), CLOC * 1.0, None, ALU.mult),
               waits=[(dve, dve.n)])
            V_(lambda v, qpl=qpl, f0=f0: v.tensor_add(f0, f0, rt(qpl)), waits=[(dve, dve.n)])
            f1 = RSCR[:, 3 * NT:4 * NT]
            V_(lambda v, ws_=ws_, f1=f1: v.tensor_scalar(f1, ws_, -1e9, 1e9, ALU.mult, ALU.add),
               waits=[(dve, dve.n)])
            V_(lambda v, f0=f0, f1=f1: v.tensor_add(f1, f0, f1), waits=[(dve, dve.n)])
            V_(lambda v, f1=f1: v.tensor_scalar_min(f1, f1, CTOT * 1.0), waits=[(dve, dve.n)])
            V_(lambda v, f1=f1, col=col: v.tensor_copy(RTI[:, col * NT:(col + 1) * NT], f1),
               waits=[(dve, dve.n)])
            V_(lambda v, f1=f1, col=col: v.tensor_copy(RTI[:, (2 + col) * NT:(3 + col) * NT], f1),
               waits=[(dve, dve.n)])
        flat_dve = dve.n
        for ti in range(NT):
            par = ti % 2
            dma_s_tr(YS[:, par * 1024:(par + 1) * 1024],
                     hfm_d[:, ti * 128:(ti + 1) * 128], tl_s[par],
                     waits=[(st_s, st_s.n), (mg, mg.n)])
            tl_done = tl_s[par].n
            for k in range(2):
                G_(lambda g, ti=ti, k=k, par=par: g.indirect_dma_start(
                    out=exp_in, out_offset=IndirectOffsetOnAxis(RTI[:, k * NT + ti:k * NT + ti + 1], 0),
                    in_=YS[:, par * 1024:(par + 1) * 1024], in_offset=None),
                   waits=[(tl_s[par], tl_done), (dve, flat_dve),
                          (zi, init_stg), (mg, mg.n)],
                   incs=[(mg, 16)])
        scatter_done = mg.n
        def prod_sl(fc, lo, hi):
            base = fc * 2048
            bufs = [(WB, 16384), (VTILE[0], 2048), (VTILE[1], 2048),
                    (VS[0], 1024), (VS[1], 1024)]
            off = base + lo
            for b, size in bufs:
                if off < size:
                    assert off + (hi - lo) <= size, (fc, lo, hi)
                    return b[:, off:off + (hi - lo)]
                off -= size
            raise AssertionError("prod overflow")

        sh_gate = pe.n
        sh_mm = {}
        sh_sil = {}
        sh_mul = {}
        for fc in range(FC):
            wsb = WS[fc % 2]
            wsem = ws_s[fc % 2]
            G_(lambda g, fc=fc, wsb=wsb: g.dma_start(
                out=wsb[:, 0:1024].rearrange("p (c f) -> p c f", c=DC),
                in_=swg[:, fc * 128:(fc + 1) * 128].rearrange("(c p) f -> p c f", p=128)),
               waits=[(pe, sh_mm.get(fc - 2, sh_gate)), (wsem, wsem.n)], incs=[(wsem, 16)])
            G_(lambda g, fc=fc, wsb=wsb: g.dma_start(
                out=wsb[:, 1024:2048].rearrange("p (c f) -> p c f", c=DC),
                in_=swu[:, fc * 128:(fc + 1) * 128].rearrange("(c p) f -> p c f", p=128)),
               incs=[(wsem, 16)])
            ldw = wsem.n
            for th in range(2):
                st = 2 * fc + th
                p = st % 2
                pg2 = PB2[p]
                pu2 = PB2[2 + p]
                for half in range(2):
                    dst = pg2 if half == 0 else pu2
                    for dc in range(DC):
                        for q in range(2):
                            T(lambda t, fc=fc, th=th, half=half, dc=dc, q=q, wsb=wsb, dst=dst: t.matmul(
                                dst[:, q * 512:(q + 1) * 512],
                                wsb[:, half * 1024 + dc * 128:half * 1024 + (dc + 1) * 128],
                                Hc(dc, th * 1024 + q * 512, th * 1024 + (q + 1) * 512),
                                start=(dc == 0), stop=(dc == DC - 1)),
                              waits=([(ws_s[fc % 2], ldw),
                                      (dve, sh_mul.get(st - 2, dve.n if st == 0 else 0)),
                                      (act, sh_sil.get(st - 2, act.n if st == 0 else 0))]
                                     if half == 0 and dc == 0 and q == 0 else []),
                              incs=[pe] if (half == 1 and dc == DC - 1 and q == 1) else [])
                mm_done = pe.n
                if th == 1:
                    sh_mm[fc] = pe.n
                A_(lambda s, th=th, pg2=pg2: s.activation(
                    PR[:, th * 1024:(th + 1) * 1024], pg2[:, 0:1024], AF.Silu),
                   waits=[(pe, mm_done)] +
                         ([(dve, sh_mul[st - 2])] if st >= 2 else []))
                sh_sil[st] = act.n
                V_(lambda v, fc=fc, th=th, pu2=pu2: v.tensor_tensor(
                    prod_sl(fc, th * 1024, (th + 1) * 1024),
                    PR[:, th * 1024:(th + 1) * 1024], pu2[:, 0:1024], ALU.mult),
                   waits=[(act, sh_sil[st]), (dve, dve.n)],
                   incs=[dve])
                sh_mul[st] = dve.n
        prod_dve = dve.n
        dn_mm = {}
        sh_bankfree = {}
        for pair in range(4):
            for fc in range(FC):
                wsb = WS[fc % 2]
                wsem = ws_s[fc % 2]
                G_(lambda g, pair=pair, fc=fc, wsb=wsb: g.dma_start(
                    out=wsb[:, 2048:2304].rearrange("p (c f) -> p c f", c=2),
                    in_=swd[fc * 128:(fc + 1) * 128,
                            pair * 256:(pair + 1) * 256].rearrange("p (c f) -> p c f", c=2)),
                   waits=[(pe, dn_mm.get((pair, fc - 2),
                                         dn_mm.get((pair - 1, FC - 1 - (fc % 2)), sh_mm[FC - 1]))),
                          (wsem, wsem.n)],
                   incs=[(wsem, 16)])
                ldw = wsem.n
                for d2 in range(2):
                    for tq in range(4):
                        T(lambda t, pair=pair, fc=fc, d2=d2, tq=tq, wsb=wsb: t.matmul(
                            PB[d2 * 4 + tq][:, 0:512],
                            wsb[:, 2048 + d2 * 128:2048 + (d2 + 1) * 128],
                            prod_sl(fc, tq * 512, (tq + 1) * 512),
                            start=(fc == 0), stop=(fc == FC - 1)),
                          waits=([(ws_s[fc % 2], ldw)] if d2 == 0 and tq == 0 else []) +
                                ([(dve, sh_bankfree.get(pair - 1, prod_dve))]
                                 if fc == 0 and d2 == 0 and tq == 0 else []),
                          incs=[pe] if (d2 == 1 and tq == 3) else [])
                dn_mm[(pair, fc)] = pe.n
            down_done = pe.n
            for d2 in range(2):
                dc = pair * 2 + d2
                for tq in range(4):
                    o = tq * 512
                    pa = PB[d2 * 4 + tq][:, 0:512]
                    V_(lambda v, dc=dc, o=o, pa=pa: v.tensor_tensor(
                        Xc(dc, o, o + 512), Xc(dc, o, o + 512), pa, ALU.add),
                       waits=[(pe, down_done), (dve, dve.n), (st_g, st_g.n),
                              (st_s, st_s.n)])
            dn_mm[(pair, FC - 1)] = max(dn_mm[(pair, FC - 1)], 0)
            bank_free = dve.n
            for d2 in range(2):
                dc = pair * 2 + d2
                for tq in range(4):
                    o = tq * 512
                    V_(lambda v, dc=dc, o=o: v.tensor_tensor(
                        Xc(dc, o, o + 512), Xc(dc, o, o + 512), Hc(dc, o, o + 512), ALU.add),
                       waits=[(dve, dve.n)])
            sh_bankfree[pair] = bank_free
        gu_mm = {}
        dnx_mm = {}
        ex_act = {}
        ex_dve = {}
        ex_yhalf = {}
        for e in range(E):
            ebuf = KTILE[e % 2]
            esem = tl_s[e % 2]
            ein_gate = gu_mm.get((e - 2, FC - 1), 0)
            for dc in range(DC):
                dma_s_tr(ebuf[:, dc * CLOC:dc * CLOC + CLOC],
                         exp_in[e * CLOC:(e + 1) * CLOC, dc * 128:(dc + 1) * 128],
                         esem,
                         waits=([(mg, scatter_done), (pe, ein_gate)] if dc == 0 else []))
            ein_ld = esem.n
            for fc in range(FC):
                wsb = WS[fc % 2]
                wsem = ws_s[fc % 2]
                gu_gate = gu_mm.get((e, fc - 2),
                                    gu_mm.get((e - 1, FC - 1 - (fc % 2)), dn_mm[(3, FC - 1)]))
                G_(lambda g, e=e, fc=fc, wsb=wsb: g.dma_start(
                    out=wsb[:, 0:1024].rearrange("p (c f) -> p c f", c=DC),
                    in_=wg[e, :, fc * 128:(fc + 1) * 128].rearrange("(c p) f -> p c f", p=128)),
                   waits=[(pe, gu_gate), (wsem, wsem.n)], incs=[(wsem, 16)])
                G_(lambda g, e=e, fc=fc, wsb=wsb: g.dma_start(
                    out=wsb[:, 1024:2048].rearrange("p (c f) -> p c f", c=DC),
                    in_=wu[e, :, fc * 128:(fc + 1) * 128].rearrange("(c p) f -> p c f", p=128)),
                   incs=[(wsem, 16)])
                ldw = wsem.n
                pg = PB[fc % 2][:, 0:CLOC]
                pu = PB[2 + fc % 2][:, 0:CLOC]
                for dc in range(DC):
                    T(lambda t, dc=dc, wsb=wsb, pg=pg, ebuf=ebuf: t.matmul(
                        pg, wsb[:, dc * 128:(dc + 1) * 128],
                        ebuf[:, dc * CLOC:dc * CLOC + CLOC],
                        start=(dc == 0), stop=(dc == DC - 1)),
                      waits=([(ws_s[fc % 2], ldw), (esem, ein_ld),
                              (act, ex_act.get((e, fc - 2), ex_act.get((e, fc - 1), act.n))),
                              (dve, ex_dve.get((e, fc - 2), ex_dve.get(
                                  (e, fc - 1), ex_yhalf.get(e - 1, dve.n))))]
                             if dc == 0 else []),
                      incs=[pe] if dc == DC - 1 else [])
                for dc in range(DC):
                    T(lambda t, dc=dc, wsb=wsb, pu=pu, ebuf=ebuf: t.matmul(
                        pu, wsb[:, 1024 + dc * 128:1024 + (dc + 1) * 128],
                        ebuf[:, dc * CLOC:dc * CLOC + CLOC],
                        start=(dc == 0), stop=(dc == DC - 1)),
                      incs=[pe] if dc == DC - 1 else [])
                gu_done = pe.n
                gu_mm[(e, fc)] = pe.n
                A_(lambda s, fc=fc, pg=pg: s.activation(
                    PR[:, (fc % 2) * 512:(fc % 2) * 512 + CLOC], pg, AF.Silu),
                   waits=[(pe, gu_done)] +
                         ([(dve, ex_dve[(e, fc - 2)])] if (e, fc - 2) in ex_dve else []))
                ex_act[(e, fc)] = act.n
                V_(lambda v, fc=fc, pu=pu: v.tensor_tensor(
                    WB[:, fc * CLOC:(fc + 1) * CLOC],
                    PR[:, (fc % 2) * 512:(fc % 2) * 512 + CLOC], pu, ALU.mult),
                   waits=[(act, act.n), (pe, gu_done), (dve, dve.n)], incs=[dve])
                ex_dve[(e, fc)] = dve.n
            eprod_dve = dve.n
            for fc in range(FC):
                wsb = WS[fc % 2]
                wsem = ws_s[fc % 2]
                dn_gate = dnx_mm.get((e, fc - 2),
                                     dnx_mm.get((e - 1, FC - 1 - (fc % 2)), dn_mm[(3, FC - 1)]))
                G_(lambda g, e=e, fc=fc, wsb=wsb: g.dma_start(
                    out=wsb[:, 2048:3072],
                    in_=wd[e, fc * 128:(fc + 1) * 128, :]),
                   waits=[(pe, dn_gate), (wsem, wsem.n)], incs=[(wsem, 16)])
                ldw = wsem.n
                for sc in range(4):
                    rows = 128 if sc < 3 else CLOC - 384
                    for dh in range(2):
                        T(lambda t, fc=fc, sc=sc, dh=dh, rows=rows, wsb=wsb: t.matmul(
                            PB[sc * 2 + dh][0:rows, 0:512],
                            WB[:, fc * CLOC + sc * 128:fc * CLOC + sc * 128 + rows],
                            wsb[:, 2048 + dh * 512:2048 + (dh + 1) * 512],
                            start=(fc == 0), stop=(fc == FC - 1)),
                          waits=([(ws_s[fc % 2], ldw)] if sc == 0 and dh == 0 else []) +
                                ([(dve, eprod_dve)] if fc == 0 and sc == 0 and dh == 0 else []),
                          incs=[pe] if (sc == 3 and dh == 1) else [])
                dnx_mm[(e, fc)] = pe.n
            y_done = pe.n
            for sc in range(4):
                rows = 128 if sc < 3 else CLOC - 384
                V_(lambda v, sc=sc, rows=rows: v.tensor_copy(
                    YS[0:rows, 0:1024], PB[sc * 2][0:rows, 0:512]),
                   waits=[(pe, y_done), (dve, dve.n), (tl_s[0], tl_s[0].n),
                          (tl_s[1], tl_s[1].n)]) if False else None
                V_(lambda v, sc=sc, rows=rows: v.tensor_copy(
                    YS[0:rows, 0:512], PB[sc * 2][0:rows, 0:512]),
                   waits=[(pe, y_done), (dve, dve.n), (st_g, st_g.n)])
                V_(lambda v, sc=sc, rows=rows: v.tensor_copy(
                    YS[0:rows, 512:1024], PB[sc * 2 + 1][0:rows, 0:512]),
                   waits=[(dve, dve.n)])
                dma_g(y_tok[e * CLOC + sc * 128:e * CLOC + sc * 128 + rows, :],
                      YS[0:rows, 0:1024], st_g, waits=[(dve, dve.n)])
                if sc == 1:
                    ex_yhalf[e] = dve.n
        prev_tp = 0
        for ti in range(NT):
            G_(lambda g, ti=ti: g.indirect_dma_start(
                out=PR[:, 0:1024], out_offset=None,
                in_=y_tok, in_offset=IndirectOffsetOnAxis(RTI[:, 2 * NT + ti:2 * NT + ti + 1], 0)),
               waits=[(st_g, st_g.n), (dve, dve.n), (act, act.n)], incs=[(mg, 16)])
            G_(lambda g, ti=ti: g.indirect_dma_start(
                out=PR[:, 1024:2048], out_offset=None,
                in_=y_tok, in_offset=IndirectOffsetOnAxis(RTI[:, 3 * NT + ti:3 * NT + ti + 1], 0)),
               incs=[(mg, 16)])
            g_done = mg.n
            V_(lambda v, ti=ti: v.tensor_scalar(OS[:, :], PR[:, 0:1024],
                                                rt(qG1)[:, ti:ti + 1], None, ALU.mult),
               waits=[(mg, g_done), (dve, dve.n), (pe, prev_tp)])
            V_(lambda v, ti=ti: v.scalar_tensor_tensor(
                OS[:, :], PR[:, 1024:2048], rt(qG2)[:, ti:ti + 1], OS[:, :],
                ALU.mult, ALU.add),
               waits=[(dve, dve.n)], incs=[dve])
            os_dve = dve.n
            for dc in range(DC):
                T(lambda t, ti=ti, dc=dc: t.transpose(
                    PB[dc][:, 0:128], OS[:, dc * 128:(dc + 1) * 128], IDF[:, :]),
                  waits=([(dve, os_dve)] if dc == 0 else []),
                  incs=[pe] if dc == DC - 1 else [])
            tp_done = pe.n
            prev_tp = tp_done
            for dc in range(DC):
                V_(lambda v, ti=ti, dc=dc: v.tensor_tensor(
                    Xc(dc, ti * 128, (ti + 1) * 128), Xc(dc, ti * 128, (ti + 1) * 128),
                    PB[dc][:, 0:128], ALU.add),
                   waits=([(pe, tp_done)] if dc == 0 else []) + [(dve, dve.n)])
        moe_gather_done[app] = mg.n

    def emit_qk_proj(feat_lo, bias_col0, dest):
        qk_mm = {}
        qk_act = {}
        base_pe = pe.n
        for oc8 in range(8):
            wsb = WS[oc8 % 2]
            wsem = ws_s[oc8 % 2]
            G_(lambda g, oc8=oc8, wsb=wsb: g.dma_start(
                out=wsb[:, 0:1024].rearrange("p (c f) -> p c f", c=DC),
                in_=wqkT[:, feat_lo + oc8 * 128:feat_lo + (oc8 + 1) * 128]
                    .rearrange("(c p) f -> p c f", p=128)),
               waits=[(pe, qk_mm.get(oc8 - 2, base_pe)), (wsem, wsem.n),
                      (act, act.n if oc8 == 0 else 0)], incs=[(wsem, 16)])
            ldw = wsem.n
            for tb in range(TB):
                pq = PB[(oc8 % 2) * 4 + tb][:, 0:512]
                for dc in range(DC):
                    T(lambda t, oc8=oc8, tb=tb, dc=dc, wsb=wsb, pq=pq: t.matmul(
                        pq, wsb[:, dc * 128:(dc + 1) * 128],
                        Hc(dc, tb * 512, (tb + 1) * 512),
                        start=(dc == 0), stop=(dc == DC - 1)),
                      waits=([(ws_s[oc8 % 2], ldw), (act, qk_act.get(oc8 - 2, act.n)),
                              (dve, dve.n if oc8 == 0 else 0)]
                             if tb == 0 and dc == 0 else []),
                      incs=[pe] if dc == DC - 1 else [])
            mm_done = pe.n
            qk_mm[oc8] = pe.n
            for tb in range(TB):
                A_(lambda s, oc8=oc8, tb=tb: s.activation(
                    dest[:, oc8 * 2048 + tb * 512:oc8 * 2048 + (tb + 1) * 512],
                    PB[(oc8 % 2) * 4 + tb][:, 0:512], AF.Identity,
                    bias=BQ[:, bias_col0 + oc8:bias_col0 + oc8 + 1]),
                   waits=([(pe, mm_done)] if tb == 0 else []))
            qk_act[oc8] = act.n

    def emit_attnb_kv():
        emit_qk_proj(1024, 8, WB)
        kt_act = act.n
        for oc8 in range(8):
            dma_s(o_kt[oc8 * 128:(oc8 + 1) * 128, :],
                  WB[:, oc8 * 2048:(oc8 + 1) * 2048], st_s, waits=[(act, kt_act)])
        for q in range(2):
            G_(lambda g, q=q: g.dma_start(
                out=KTILE[q][:, :].rearrange("p (c f) -> p c f", c=DC),
                in_=wqkT[:, 2048 + q * 512:2048 + (q + 1) * 512]
                    .rearrange("(c p) f -> p c f", p=128)),
               waits=[(pe, pe.n), (act, act.n), (dve, dve.n)], incs=[(kv_s[q], 16)])
        G_(lambda g: g.dma_start(out=PR[0:1, 0:1024], in_=bqv_row),
           waits=[(dve, dve.n)], incs=[(ldg, 16)])
        T(lambda t: t.matmul(PB[6][:, 0:512], ONEB[0:1, :], PR[0:1, 0:512],
                             start=True, stop=True),
          waits=[(ldg, ldg.n), (act, act.n), (dve, dve.n)], incs=[pe])
        T(lambda t: t.matmul(PB[7][:, 0:512], ONEB[0:1, :], PR[0:1, 512:1024],
                             start=True, stop=True), incs=[pe])
        bv_pe = pe.n
        V_(lambda v: v.tensor_copy(PR[:, 1024:1536], PB[6][:, 0:512]),
           waits=[(pe, bv_pe), (dve, dve.n)])
        V_(lambda v: v.tensor_copy(PR[:, 1536:2048], PB[7][:, 0:512]),
           waits=[(dve, dve.n)])
        bv_dve = dve.n
        vdve = {}
        for tt in range(NT):
            b0 = PB[(tt % 2) * 2][:, 0:512]
            b1 = PB[(tt % 2) * 2 + 1][:, 0:512]
            for dc in range(DC):
                T(lambda t, tt=tt, dc=dc, b0=b0: t.matmul(
                    b0, Hc(dc, tt * 128, (tt + 1) * 128),
                    KTILE[0][:, dc * 512:(dc + 1) * 512],
                    start=(dc == 0), stop=(dc == DC - 1)),
                  waits=(([(kv_s[0], kv_s[0].n), (kv_s[1], kv_s[1].n),
                           (dve, dve.n), (act, act.n)] if tt == 0 else
                          [(dve, vdve.get(tt - 2, 0))]) if dc == 0 else []),
                  incs=[pe] if dc == DC - 1 else [])
                T(lambda t, tt=tt, dc=dc, b1=b1: t.matmul(
                    b1, Hc(dc, tt * 128, (tt + 1) * 128),
                    KTILE[1][:, dc * 512:(dc + 1) * 512],
                    start=(dc == 0), stop=(dc == DC - 1)),
                  incs=[pe] if dc == DC - 1 else [])
            vp_done = pe.n
            V_(lambda v, tt=tt, b0=b0: v.tensor_tensor(YS[:, (tt % 2) * 1024:(tt % 2) * 1024 + 512], b0,
                                                PR[:, 1024:1536], ALU.add),
               waits=[(pe, vp_done), (dve, max(bv_dve, dve.n)), (st_s, st_s.n)])
            V_(lambda v, tt=tt, b1=b1: v.tensor_tensor(YS[:, (tt % 2) * 1024 + 512:(tt % 2) * 1024 + 1024], b1,
                                                PR[:, 1536:2048], ALU.add),
               waits=[(dve, dve.n)])
            vdve[tt] = dve.n
            dma_s(o_v[tt * 128:(tt + 1) * 128, :], YS[:, (tt % 2) * 1024:(tt % 2) * 1024 + 1024], st_s,
                  waits=[(dve, dve.n)])

    def emit_attnb_attn():
        emit_qk_proj(0, 0, WB)
        agk = i_agk
        agv = i_agv
        sc_last = {}
        ctx_last = {}
        wout_mm = {}
        for qb in range(4):
            for hp in range(8):
                par = hp % 2
                ksem = kv_s[par]
                gi = qb * 8 + hp
                kt_gate = sc_last.get(gi - 2, pe.n if gi < 2 else 0)
                v_gate = ctx_last.get(gi - 1, pe.n if gi < 1 else 0)
                for rank in range(8):
                    SY(lambda sy, qb=qb, hp=hp, rank=rank, par=par: sy.dma_start(
                        out=KTILE[par].rearrange("p (j r c) -> p j r c", j=4, r=8)[:, :, rank, :],
                        in_=agk[1024 * rank + 128 * hp:1024 * rank + 128 * (hp + 1),
                                512 * qb:512 * (qb + 1)].rearrange("p (j c) -> p j c", j=4)),
                       waits=([(pe, kt_gate)] if rank == 0 else []),
                       incs=[(ksem, 16)])
                for hh in range(2):
                    h = 2 * hp + hh
                    for rank in range(8):
                        SY(lambda sy, qb=qb, h=h, hh=hh, rank=rank: sy.dma_start(
                            out=VTILE[hh].rearrange("p (j r c) -> p j r c", j=4, r=8, c=65)[:, :, rank, 0:64],
                            in_=agv[2048 * rank + 512 * qb:2048 * rank + 512 * (qb + 1),
                                    64 * h:64 * (h + 1)].rearrange("(j p) c -> p j c", p=128)),
                           waits=(([(pe, v_gate)] +
                                   ([(dve, dve.n)] if gi == 0 else []))
                                  if hh == 0 and rank == 0 else []),
                           incs=[(ksem, 16)])
                ld_done = ksem.n
                act_kc = {}
                pe_ctx = {}
                sc_pe_k = {}
                for kc in range(NKT + 2):
                    if kc < NKT:
                        pA = PB2[kc % 2][:, 0:512]
                        pB_ = PB2[kc % 2][:, 512:1024]
                        wA = [(ksem, ld_done)] if kc == 0 else []
                        if kc >= 2:
                            wA = wA + [(act, act_kc[kc - 2])]
                        if kc <= 1:
                            wA = wA + [(act, act.n), (dve, dve.n)]
                        T(lambda t, qb=qb, hp=hp, kc=kc, par=par, pA=pA: t.matmul(
                            pA, KTILE[par][0:64, kc * 128:(kc + 1) * 128],
                            WB[0:64, hp * 2048 + qb * 512:hp * 2048 + (qb + 1) * 512],
                            start=True, stop=True, tile_position=(0, 0)),
                          waits=wA, incs=[pe])
                        T(lambda t, qb=qb, hp=hp, kc=kc, par=par, pB_=pB_: t.matmul(
                            pB_, KTILE[par][64:128, kc * 128:(kc + 1) * 128],
                            WB[64:128, hp * 2048 + qb * 512:hp * 2048 + (qb + 1) * 512],
                            start=True, stop=True, tile_position=(64, 0)),
                          incs=[pe])
                        sc_pe_k[kc] = pe.n
                    kd = kc - 2
                    if kd >= 0:
                        T(lambda t, kd=kd: t.matmul(
                            PB[4][0:65, 0:512], VTILE[0][:, kd * 65:(kd + 1) * 65],
                            PR[:, (kd % 2) * 1024:(kd % 2) * 1024 + 512],
                            start=(kd == 0), stop=(kd == NKT - 1)),
                          waits=[(act, act_kc[kd])] + ([(dve, dve.n)] if kd == 0 else []))
                        T(lambda t, kd=kd: t.matmul(
                            PB[5][0:65, 0:512], VTILE[1][:, kd * 65:(kd + 1) * 65],
                            PR[:, (kd % 2) * 1024 + 512:(kd % 2) * 1024 + 1024],
                            start=(kd == 0), stop=(kd == NKT - 1)),
                          incs=[pe])
                        pe_ctx[kd] = pe.n
                    if kc < NKT:
                        A_(lambda s, kc=kc: s.activation(
                            PR[:, (kc % 2) * 1024:(kc % 2) * 1024 + 1024],
                            PB2[kc % 2][:, 0:1024], AF.Exp,
                            bias=CONSTS[:, 1:2], scale=0.125),
                           waits=[(pe, sc_pe_k[kc])] +
                                 ([(pe, pe_ctx[kc - 2])] if kc >= 2 else []))
                        act_kc[kc] = act.n
                sc_pe = sc_pe_k[NKT - 1]
                ctx_pe = pe.n
                sc_last[qb * 8 + hp] = sc_pe
                ctx_last[qb * 8 + hp] = pe.n
                V_(lambda v: v.reciprocal(XT1[0:1, 0:512], PB[4][64:65, 0:512]),
                   waits=[(pe, ctx_pe), (dve, dve.n)])
                V_(lambda v: v.reciprocal(OS[0:1, 0:512], PB[5][64:65, 0:512]),
                   waits=[(dve, dve.n)])
                rc_dve = dve.n
                T(lambda t: t.matmul(PB[6][0:64, 0:512], ONEF[0:1, 0:64], XT1[0:1, 0:512],
                                     start=True, stop=True),
                  waits=[(dve, rc_dve)], incs=[pe])
                T(lambda t: t.matmul(PB[6][64:128, 0:512], ONEF[0:1, 0:64], OS[0:1, 0:512],
                                     start=True, stop=True), incs=[pe])
                bc_pe = pe.n
                V_(lambda v: v.tensor_copy(SCB[:, 0:512], PB[6][:, 0:512]),
                   waits=[(pe, bc_pe), (dve, dve.n)])
                V_(lambda v, qb=qb, hp=hp: v.tensor_tensor(
                    HB[0:64, hp * 2048 + qb * 512:hp * 2048 + (qb + 1) * 512],
                    PB[4][0:64, 0:512], SCB[0:64, 0:512], ALU.mult),
                   waits=[(dve, dve.n)])
                V_(lambda v, qb=qb, hp=hp: v.tensor_tensor(
                    HB[64:128, hp * 2048 + qb * 512:hp * 2048 + (qb + 1) * 512],
                    PB[5][0:64, 0:512], SCB[64:128, 0:512], ALU.mult),
                   waits=[(dve, dve.n)], incs=[dve])
            o = qb * 512
            T(lambda t, o=o: t.matmul(PB[6][:, 0:512], ONEB[0:1, :], SROW[0:1, o:o + 512],
                                      start=True, stop=True),
              waits=[(dve, dve.n), (act, act.n)], incs=[pe])
            T(lambda t, o=o: t.matmul(PB[7][:, 0:512], ONEB[0:1, :],
                                      SROW[0:1, 2048 + o:2048 + o + 512],
                                      start=True, stop=True), incs=[pe])
            ab_pe = pe.n
            for oc in range(8):
                wsb = WS[oc % 2]
                wsem = ws_s[oc % 2]
                G_(lambda g, oc=oc, wsb=wsb: g.dma_start(
                    out=wsb[:, 0:1024].rearrange("p (c f) -> p c f", c=DC),
                    in_=woutT[:, oc * 128:(oc + 1) * 128].rearrange("(c p) f -> p c f", p=128)),
                   waits=[(pe, wout_mm.get((qb, oc - 2),
                                           wout_mm.get((qb - 1, 6 + (oc % 2)), pe.n if qb == 0 and oc < 2 else 0))),
                          (wsem, wsem.n)], incs=[(wsem, 16)])
                ldw = wsem.n
                po = PB[oc % 4][:, 0:512]
                for dc in range(DC):
                    T(lambda t, oc=oc, dc=dc, o=o, wsb=wsb, po=po: t.matmul(
                        po, wsb[:, dc * 128:(dc + 1) * 128],
                        HB[:, dc * 2048 + o:dc * 2048 + o + 512],
                        start=(dc == 0), stop=(dc == DC - 1)),
                      waits=([(ws_s[oc % 2], ldw), (act, act.n)] if dc == 0 else []) +
                            ([(dve, dve.n)] if oc == 0 and dc == 0 else []),
                      incs=[pe] if dc == DC - 1 else [])
                mm_done = pe.n
                wout_mm[(qb, oc)] = pe.n
                A_(lambda s, oc=oc, po=po: s.activation(XT1[:, 0:512], po, AF.Identity,
                                                        bias=BO[:, oc:oc + 1]),
                   waits=[(pe, mm_done), (dve, dve.n)])
                V_(lambda v, oc=oc, o=o: v.tensor_tensor(
                    OS[:, 0:512], Xc(oc, o, o + 512), PB[6][:, 0:512], ALU.mult),
                   waits=[(pe, ab_pe), (dve, dve.n), (act, act.n)])
                V_(lambda v: v.tensor_tensor(OS[:, 0:512], OS[:, 0:512],
                                             PB[7][:, 0:512], ALU.add),
                   waits=[(dve, dve.n)])
                V_(lambda v: v.tensor_tensor(OS[:, 0:512], OS[:, 0:512],
                                             XT1[:, 0:512], ALU.add),
                   waits=[(dve, dve.n)])
                V_(lambda v, oc=oc, o=o: v.tensor_tensor(
                    Xc(oc, o, o + 512), Xc(oc, o, o + 512), OS[:, 0:512], ALU.add),
                   waits=[(dve, dve.n)], incs=[dve])

    def emit_x_out():
        for dc in range(DC):
            dma_s(x_out[dc * 128:(dc + 1) * 128, :], Xc(dc), st_s,
                  waits=[(dve, dve.n), (pe, pe.n), (act, act.n)])
        SY(lambda s: s.nop(), waits=[(st_s, st_s.n), (st_g, st_g.n)])

    def emit_out():
        for ti in range(NT):
            for dc in range(DC):
                T(lambda t, ti=ti, dc=dc: t.transpose(
                    PB[dc][:, 0:128], Xc(dc, ti * 128, ti * 128 + 128), IDF[:, :]),
                  waits=[(ld0, ld0.n), (dve, dve.n)], incs=[pe])
            tp_done = pe.n
            for dc in range(DC):
                V_(lambda v, dc=dc: v.tensor_copy(OS[:, dc * 128:(dc + 1) * 128],
                                                  PB[dc][:, 0:128]),
                   waits=[(pe, tp_done), (st_s, st_s.n)], incs=[dve])
            dma_s(out_tok[ti * 128:(ti + 1) * 128, :], OS[:, :], st_s,
                  waits=[(dve, dve.n)])
        SY(lambda s: s.nop(), waits=[(st_s, st_s.n)])

    if mode in ("L1", "D_attn_a"):
        emit_ln()
        emit_h()
        emit_attn_a()
        if mode == "L1":
            emit_ln()
            emit_h()
            emit_ffn(0, stop_after_counts=True)
            emit_x_out()
        else:
            emit_out()
    elif mode == "L2":
        emit_ln()
        emit_h()
        emit_ffn(0)
        emit_ln()
        emit_h(extra_gamma=True)
        emit_attnb_kv()
        emit_x_out()
    elif mode == "L3":
        emit_ln()
        emit_h(extra_gamma=True)
        emit_attnb_attn()
        emit_ln()
        emit_h()
        emit_ffn(1, stop_after_counts=True)
        emit_x_out()
    elif mode == "L4":
        emit_ln()
        emit_h()
        emit_ffn(1)
        emit_out()
    else:
        raise ValueError(mode)

    P.finalize()
    es.close()
    return nc


def _prep_inputs(inputs):
    x = np.ascontiguousarray(inputs["L0_attn_a_inputs"].reshape(16384, 1024))
    ktp = np.ascontiguousarray(inputs["L0_attn_a_key_param_tokens"].T)
    vp = np.ascontiguousarray(inputs["L0_attn_a_value_param_tokens"])
    wqkT = np.ascontiguousarray(inputs["L0_attn_b_W_qkv"].T)
    woutT = np.ascontiguousarray(inputs["L0_attn_b_W_out"].T)
    shared = dict(
        ktp=ktp, vp=vp,
        gamma=inputs["L0_attn_b_ln_gamma"].reshape(8, 128).T,
        beta=inputs["L0_attn_b_ln_beta"].reshape(8, 128).T,
        wqkT=wqkT, bqkv=inputs["L0_attn_b_b_qkv"].reshape(24, 128).T,
        woutT=woutT, bout=inputs["L0_attn_b_b_out"].reshape(8, 128).T,
        rw=inputs["L0_ffn_router_w"], rb=inputs["L0_ffn_router_b"].reshape(1, 16),
        bqv_row=inputs["L0_attn_b_b_qkv"][2048:].reshape(1, 1024),
        swg=inputs["L0_ffn_sw_gate"], swu=inputs["L0_ffn_sw_up"],
        swd=inputs["L0_ffn_sw_down"],
        wg=inputs["L0_ffn_w_gate"], wu=inputs["L0_ffn_w_up"],
        wd=inputs["L0_ffn_w_down"],
    )
    shared = {k: np.ascontiguousarray(v, dtype=np.float32) for k, v in shared.items()}
    xt = x.reshape(128, 128, 1024)
    in_maps = []
    for c in range(NCORES):
        xc = np.ascontiguousarray(xt[c::8].reshape(2048, 1024).T)
        r = np.arange(128)
        gtile_of_row = (r % 16) * 8 + (r // 16)
        gi = 8 * np.arange(16) + c
        tm = (gtile_of_row[:, None] < gi[None, :]).astype(np.float32)
        m = dict(shared)
        m["x_fm"] = xc
        m["tilemask"] = tm
        in_maps.append(m)
    return in_maps


_PROG_CACHE = {}


def _prog(mode):
    if mode not in _PROG_CACHE:
        _PROG_CACHE[mode] = build_program(mode)
    return _PROG_CACHE[mode]


def kernel(**inputs):
    base_maps = _prep_inputs(inputs)
    cores = list(range(NCORES))

    r1 = run_bass_kernel_spmd(_prog("L1"), base_maps, core_ids=cores)
    cnt_all1 = np.concatenate([r1.results[c]["o_cnt"] for c in cores], axis=0)
    maps2 = []
    for c in cores:
        m = dict(base_maps[c])
        m["x_fm"] = r1.results[c]["x_out"]
        m["i_cntall"] = cnt_all1
        maps2.append(m)
    r2 = run_bass_kernel_spmd(_prog("L2"), maps2, core_ids=cores)
    agk = np.concatenate([r2.results[c]["o_kt"] for c in cores], axis=0)
    agv = np.concatenate([r2.results[c]["o_v"] for c in cores], axis=0)
    maps3 = []
    for c in cores:
        m = dict(base_maps[c])
        m["x_fm"] = r2.results[c]["x_out"]
        m["i_agk"] = agk
        m["i_agv"] = agv
        maps3.append(m)
    r3 = run_bass_kernel_spmd(_prog("L3"), maps3, core_ids=cores)
    cnt_all2 = np.concatenate([r3.results[c]["o_cnt"] for c in cores], axis=0)
    maps4 = []
    for c in cores:
        m = dict(base_maps[c])
        m["x_fm"] = r3.results[c]["x_out"]
        m["i_cntall"] = cnt_all2
        maps4.append(m)
    r4 = run_bass_kernel_spmd(_prog("L4"), maps4, core_ids=cores)
    out = np.empty((128, 128, 1024), np.float32)
    for c in cores:
        out[c::8] = r4.results[c]["out_tok"].reshape(16, 128, 1024)
    return out.reshape(4, 4096, 1024)


# version 07 (final)
# speedup vs baseline: 1.1432x; 1.1051x over previous
import os

import jax

jax.config.update("jax_compilation_cache_dir",
                  os.path.join(os.path.dirname(os.path.abspath(__file__)), "_jaxcache"))
jax.config.update("jax_persistent_cache_min_compile_time_secs", 0.0)
jax.config.update("jax_persistent_cache_min_entry_size_bytes", 0)

import numpy as np
from contextlib import ExitStack

import concourse.bass as bass
import concourse.mybir as mybir
from concourse.bass_utils import run_bass_kernel_spmd
from concourse.bass_core_rust import IndirectOffsetOnAxis

F32 = mybir.dt.float32
BF16 = mybir.dt.bfloat16
I32 = mybir.dt.int32
AF = mybir.ActivationFunctionType
ALU = mybir.AluOpType
AX = mybir.AxisListType

D = 1024
TL = 2048
NT = 16
NCORES = 8
S = 4096
FF = 1408
E = 16
CLOC = 448
CTOT = E * CLOC
CAP = 2560
NH = 16
HD = 64
DC = 8
FC = 11
TB = 4
QB = 512
NKT = S // 128
EXP_BIAS = -20.0

DEV_PHASES = None


class Sem:
    __slots__ = ("h", "n", "name")

    def __init__(self, h, name):
        self.h = h
        self.n = 0
        self.name = name


class Prog:
    ENGINES = ("tensor", "vector", "scalar", "gpsimd", "sync")

    def __init__(self, nc):
        self.nc = nc
        self.q = {e: [] for e in self.ENGINES}
        self.sems = []
        self.max_wait = {}

    def sem(self, name):
        s = Sem(self.nc.alloc_semaphore(name), name)
        self.sems.append(s)
        return s

    def emit(self, eng, f, waits=(), incs=()):
        ws = []
        for item in waits:
            if isinstance(item, Sem):
                s, v = item, item.n
            else:
                s, v = item
            assert v <= s.n, f"wait {s.name}>={v} but only {s.n} issued"
            self.max_wait[s.name] = max(self.max_wait.get(s.name, 0), v)
            if v > 0:
                ws.append((s.h, v))
        ics = []
        for item in incs:
            if isinstance(item, tuple):
                s, k = item
            else:
                s, k = item, 1
            s.n += k
            ics.append((s.h, k))

        def run(e, f=f, ws=ws, ics=ics):
            for (h, v) in ws:
                e.wait_ge(h, v)
            inst = f(e)
            for (h, k) in ics:
                inst.then_inc(h, k)

        self.q[eng].append(run)

    def finalize(self):
        with self.nc.Block() as block:
            for eng in self.ENGINES:
                items = self.q[eng]

                def body(e, items=items):
                    for f in items:
                        f(e)

                getattr(block, eng)(body)


def _host_constants():
    c = {}
    c["ident"] = np.eye(128, dtype=np.float32)
    c["ones_col"] = np.ones((128, 1), np.float32)
    c["ones_row"] = np.ones((1, 128), np.float32)
    c["lstrict"] = np.triu(np.ones((128, 128), np.float32), 1)
    c["lstrict16"] = np.triu(np.ones((16, 16), np.float32), 1)
    c["iota16"] = np.tile(np.arange(16, dtype=np.float32), (128, 1))
    return c


def build_program(mode="L1"):
    nc = bass.Bass("TRN2", target_bir_lowering=False)
    P = Prog(nc)
    es = ExitStack()

    t_in = {}
    def ext(name, shape, dtype=F32):
        t_in[name] = nc.dram_tensor(name, list(shape), dtype, kind="ExternalInput").ap()
        return t_in[name]

    x_fm = ext("x_fm", [D, TL])
    tilemask = ext("tilemask", [128, NT])
    if mode in ("L2", "L4"):
        i_cntall = ext("i_cntall", [NCORES * NT, E])
    if mode == "L3":
        i_agk = ext("i_agk", [NCORES * D, TL], BF16)
        i_agv = ext("i_agv", [NCORES * TL, D], BF16)
    bqv_row = ext("bqv_row", [1, D])
    ktp = ext("ktp", [D, S])
    vp = ext("vp", [S, D])
    gamma = ext("gamma", [128, DC])
    beta = ext("beta", [128, DC])
    wqkT = ext("wqkT", [D, 3 * D])
    bqkv = ext("bqkv", [128, 24])
    woutT = ext("woutT", [D, D])
    bout = ext("bout", [128, DC])
    rw = ext("rw", [D, E])
    rb = ext("rb", [1, E])
    swg = ext("swg", [D, FF])
    swu = ext("swu", [D, FF])
    swd = ext("swd", [FF, D])
    wg = ext("wg", [E, D, FF])
    wu = ext("wu", [E, D, FF])
    wd = ext("wd", [E, FF, D])

    if mode in ("L4", "D_attn_a", "D_ffn1"):
        out_tok = nc.dram_tensor("out_tok", [TL, D], F32, kind="ExternalOutput").ap()
    else:
        x_out = nc.dram_tensor("x_out", [D, TL], F32, kind="ExternalOutput").ap()
    if mode in ("L1", "L3"):
        o_cnt = nc.dram_tensor("o_cnt", [NT, E], F32, kind="ExternalOutput").ap()
    if mode == "L2":
        o_kt = nc.dram_tensor("o_kt", [D, TL], BF16, kind="ExternalOutput").ap()
        o_v = nc.dram_tensor("o_v", [TL, D], BF16, kind="ExternalOutput").ap()

    hfm_d = nc.dram_tensor("hfm_d", [D, TL], BF16, kind="Internal").ap()
    exp_in = nc.dram_tensor("exp_in", [CTOT + 128, D], BF16, kind="Internal").ap()
    y_tok = nc.dram_tensor("y_tok", [CTOT + 128, D], BF16, kind="Internal").ap()
    z_tok = nc.dram_tensor("z_tok", [TL, D], BF16, kind="Internal").ap()
    cnt_in = [nc.dram_tensor(f"cnt_in{i}", [NT, E], F32, kind="Internal").ap() for i in range(2)]
    cnt_out = [nc.dram_tensor(f"cnt_out{i}", [NCORES * NT, E], F32, kind="Internal").ap() for i in range(2)]
    bounce = nc.dram_tensor("bounce", [4, 256], F32, kind="Internal").ap()
    agk_in = nc.dram_tensor("agk_in", [D, TL], BF16, kind="Internal").ap()
    agk_out = nc.dram_tensor("agk_out", [NCORES * D, TL], BF16, kind="Internal").ap()
    agv_in = nc.dram_tensor("agv_in", [TL, D], BF16, kind="Internal").ap()
    agv_out = nc.dram_tensor("agv_out", [NCORES * TL, D], BF16, kind="Internal").ap()

    HC = _host_constants()
    ident_f = nc.inline_tensor(HC["ident"], "identf").ap()
    consts_np = {
        "identb": HC["ident"],
        "ones_colb": HC["ones_col"],
        "lstrictb": HC["lstrict"],
    }

    sb = lambda name, shape, dt: es.enter_context(nc.sbuf_tensor(name, list(shape), dt))
    X = sb("X", [128, DC * TL], F32)
    HB = sb("HB", [128, DC * TL], BF16)
    WB = sb("WB", [128, 16384], BF16)
    SROW = sb("SROW", [128, 4096], BF16)
    SCB = sb("SCB", [128, 512], F32)
    XT1 = sb("XT1", [128, 512], F32)
    PR = sb("PR", [128, 4096], BF16)
    YS = sb("YS", [128, 2048], BF16)
    OS = sb("OS", [128, 1024], F32)
    WS = [sb(f"WS{i}", [128, 3072], BF16) for i in range(2)]
    VS = [sb(f"VS{i}", [128, 1024], BF16) for i in range(2)]
    KTILE = [sb(f"KT{i}", [128, 4096], BF16) for i in range(2)]
    VTILE = [sb(f"VT{i}", [128, 2080], BF16) for i in range(2)]
    IDB = sb("IDB", [128, 128], BF16)
    IDF = sb("IDF", [128, 128], F32)
    ONEB = sb("ONEB", [128, 128], BF16)
    ONEF = sb("ONEF", [128, 128], F32)
    CONSTS = sb("CONSTS", [128, 8], F32)
    LSTR = sb("LSTR", [128, 128], F32)
    LSTR16 = sb("LSTR16", [16, 16], F32)
    IOTA = sb("IOTA", [128, 16], F32)
    TMASK = sb("TMASK", [128, NT], F32)
    GAM = sb("GAM", [128, DC], F32)
    BET = sb("BET", [128, DC], F32)
    BQ = sb("BQ", [128, 24], F32)
    BO = sb("BO", [128, DC], F32)
    RBS = sb("RBS", [1, E], BF16)
    RWS = sb("RWS", [128, DC * E], BF16)
    RT = sb("RT", [128, 16 * NT], F32)
    RTI = sb("RTI", [128, 4 * NT], I32)
    LG = sb("LG", [128, NT * E], F32)
    OH1 = sb("OH1", [128, NT * E], F32)
    OH2 = sb("OH2", [128, NT * E], F32)
    RSCR = sb("RSCR", [128, 64], F32)
    CNTL = sb("CNTL", [16, E], F32)
    CSX = sb("CSX", [128, NT * E], F32)
    CNTS = sb("CNTS", [128, E], F32)
    ZEROT = sb("ZEROT", [128, 512], BF16)

    ONEC = ONEB[:, 0:1]
    PB2 = [es.enter_context(nc.psum_tensor(f"PB2_{i}", [128, 1024], F32)) for i in range(4)]
    PB = [PB2[i // 2][:, (i % 2) * 512:(i % 2) * 512 + 512] for i in range(8)]

    pe = P.sem("pe")
    dve = P.sem("dve")
    act = P.sem("act")
    pool = P.sem("pool")
    ld0 = P.sem("ld0")
    ldg = P.sem("ldg")
    ws_s = [P.sem(f"ws{i}") for i in range(2)]
    ws3 = P.sem("ws3")
    kv_s = [P.sem(f"kv{i}") for i in range(2)]
    st_s = P.sem("st_s")
    st_g = P.sem("st_g")
    zi = P.sem("zi")
    cc = P.sem("cc")
    mg = P.sem("mg")
    tl_s = [P.sem(f"tl{i}") for i in range(2)]

    def T(f, waits=(), incs=(pe,)):
        P.emit("tensor", f, waits, incs)

    def V_(f, waits=(), incs=(dve,)):
        P.emit("vector", f, waits, incs)

    def A_(f, waits=(), incs=(act,)):
        P.emit("scalar", f, waits, incs)

    def G_(f, waits=(), incs=()):
        P.emit("gpsimd", f, waits, incs)

    def SY(f, waits=(), incs=()):
        P.emit("sync", f, waits, incs)

    def dma_g(out, in_, sem, waits=()):
        G_(lambda g, o=out, i=in_: g.dma_start(out=o, in_=i), waits, incs=[(sem, 16)])

    def dma_s(out, in_, sem, waits=()):
        SY(lambda s, o=out, i=in_: s.dma_start(out=o, in_=i), waits, incs=[(sem, 16)])

    def dma_s_tr(out, in_, sem, waits=()):
        SY(lambda s, o=out, i=in_: s.dma_start(out=o, in_=i, transpose=True), waits,
           incs=[(sem, 16)])

    def Xc(dc, lo=0, hi=TL):
        return X[:, dc * TL + lo: dc * TL + hi]

    def Hc(dc, lo=0, hi=TL):
        return HB[:, dc * TL + lo: dc * TL + hi]

    inline_aps = {}
    for nm, arr in consts_np.items():
        inline_aps[nm] = nc.inline_tensor(arr.astype(np.float32), nm).ap()
    l16 = nc.inline_tensor(HC["lstrict16"], "l16np").ap()
    iota_t = nc.inline_tensor(HC["iota16"], "iotanp").ap()

    dma_s(IDF[:, :], ident_f, ld0)
    dma_s(IOTA[:, :], iota_t, ld0)
    dma_s(GAM[:, :], gamma, ld0)
    dma_s(BET[:, :], beta, ld0)
    dma_s(BQ[:, :], bqkv, ld0)
    dma_s(BO[:, :], bout, ld0)
    dma_s(X[:, :].rearrange("p (c t) -> p c t", c=DC),
          x_fm.rearrange("(c p) t -> p c t", p=128), ld0)
    dma_g(IDB[:, :], inline_aps["identb"], ldg)
    dma_s(LSTR[:, :], inline_aps["lstrictb"], ld0)
    dma_s(LSTR16[:, :], l16, ld0)
    dma_s(TMASK[:, :], tilemask, ld0)
    dma_g(RBS[:, :], rb, ldg)
    dma_g(RWS[:, :].rearrange("p (c e) -> p c e", c=DC),
          rw.rearrange("(c p) e -> p c e", p=128), ldg)
    V_(lambda v: v.memset(ONEB[:, :], 1.0))
    V_(lambda v: v.memset(ONEF[:, :], 1.0))
    V_(lambda v: v.memset(CONSTS[:, 0:1], 1e-5))
    V_(lambda v: v.memset(CONSTS[:, 1:2], EXP_BIAS))
    V_(lambda v: v.memset(ZEROT[:, :], 0.0))
    for i in range(2):
        V_(lambda v, i=i: v.memset(VTILE[i][:, :], 1.0))
    init_dve = dve.n
    for r in range(0, CTOT, 128):
        for cc2 in range(2):
            dma_g(exp_in[r:r + 128, cc2 * 512:(cc2 + 1) * 512], ZEROT[:, :], zi,
                  waits=[(dve, init_dve)])
    for cc2 in range(2):
        dma_g(y_tok[CTOT:CTOT + 128, cc2 * 512:(cc2 + 1) * 512], ZEROT[:, :], zi,
              waits=[(dve, init_dve)])
    init_stg = zi.n

    def emit_ln():
        for tb in range(TB):
            o = tb * 512
            ps = PB[6][0:1, 0:512]
            ps2 = PB[7][0:1, 0:512]
            pe_gate = pe.n
            dve_gate = dve.n
            act_gate = act.n
            for dc in range(DC):
                V_(lambda v, dc=dc, o=o: v.tensor_copy(
                    PR[:, dc * 512:(dc + 1) * 512], Xc(dc, o, o + 512)),
                   waits=[(pe, pe_gate), (dve, dve_gate), (act, act_gate),
                          (ld0, ld0.n), (ldg, ldg.n)])
            cast_done = dve.n
            for dc in range(DC):
                V_(lambda v, dc=dc: v.tensor_mul(
                    WB[:, dc * 512:(dc + 1) * 512],
                    PR[:, dc * 512:(dc + 1) * 512], PR[:, dc * 512:(dc + 1) * 512]),
                   waits=[(dve, cast_done)])
            mul_done = dve.n
            for dc in range(DC):
                T(lambda t, dc=dc, ps=ps: t.matmul(
                    ps, ONEC[:, :], PR[:, dc * 512:(dc + 1) * 512],
                    start=(dc == 0), stop=(dc == DC - 1)),
                  waits=([(dve, cast_done), (ldg, ldg.n)] if dc == 0 else []),
                  incs=[pe] if dc == DC - 1 else [])
            for dc in range(DC):
                T(lambda t, dc=dc, ps2=ps2: t.matmul(
                    ps2, ONEC[:, :], WB[:, dc * 512:(dc + 1) * 512],
                    start=(dc == 0), stop=(dc == DC - 1)),
                  waits=([(dve, mul_done)] if dc == 0 else []),
                  incs=[pe] if dc == DC - 1 else [])
            mm_done = pe.n
            s1 = XT1[0:1, 0:512]; s2 = SCB[0:1, 0:512]; s3 = OS[0:1, 0:512]
            V_(lambda v, ps=ps, s1=s1: v.tensor_scalar_mul(s1, ps, 1.0 / D),
               waits=[(pe, mm_done), (dve, dve.n), (act, act.n)])
            V_(lambda v, s1=s1, s2=s2: v.tensor_mul(s2, s1, s1),
               waits=[(dve, dve.n)])
            V_(lambda v, ps2=ps2, s3=s3: v.tensor_scalar_mul(s3, ps2, 1.0 / D),
               waits=[(pe, mm_done), (dve, dve.n)])
            V_(lambda v, s2=s2, s3=s3: v.tensor_sub(s3, s3, s2),
               waits=[(dve, dve.n)])
            A_(lambda s, s2=s2, s3=s3: s.activation(s2, s3, AF.Sqrt, bias=CONSTS[0:1, 0:1]),
               waits=[(dve, dve.n)])
            V_(lambda v, s2=s2, s3=s3: v.reciprocal(s3, s2),
               waits=[(act, act.n), (dve, dve.n)])
            V_(lambda v, s3=s3, o=o: v.tensor_copy(SROW[0:1, o:o + 512], s3),
               waits=[(dve, dve.n)])
            V_(lambda v, s1=s1, s2=s2, s3=s3: v.tensor_mul(s2, s1, s3),
               waits=[(dve, dve.n)])
            V_(lambda v, s2=s2, o=o: v.tensor_scalar(
                SROW[0:1, 2048 + o:2048 + o + 512], s2, -1.0, None, ALU.mult),
               waits=[(dve, dve.n)])

    def emit_h(extra_gamma=False):
        for tb in range(TB):
            o = tb * 512
            pa = PB[6][:, 0:512]
            pb_ = PB[7][:, 0:512]
            T(lambda t, o=o, pa=pa: t.matmul(pa, ONEB[0:1, :], SROW[0:1, o:o + 512],
                                             start=True, stop=True),
              waits=[(dve, dve.n), (pe, pe.n)], incs=[pe])
            T(lambda t, o=o, pb_=pb_: t.matmul(pb_, ONEB[0:1, :], SROW[0:1, 2048 + o:2048 + o + 512],
                                               start=True, stop=True),
              waits=[], incs=[pe])
            bc_done = pe.n
            for dc in range(DC):
                V_(lambda v, dc=dc, o=o, pa=pa: v.tensor_tensor(
                    XT1[:, 0:512], Xc(dc, o, o + 512), pa, ALU.mult),
                   waits=[(pe, bc_done), (dve, dve.n)])
                if not extra_gamma:
                    V_(lambda v, dc=dc, o=o, pb_=pb_: v.tensor_tensor(
                        Hc(dc, o, o + 512), XT1[:, 0:512], pb_, ALU.add),
                       waits=[(dve, dve.n)])
                else:
                    V_(lambda v, dc=dc, o=o, pb_=pb_: v.tensor_tensor(
                        XT1[:, 0:512], XT1[:, 0:512], pb_, ALU.add),
                       waits=[(dve, dve.n)])
                    V_(lambda v, dc=dc, o=o: v.tensor_scalar(
                        Hc(dc, o, o + 512), XT1[:, 0:512],
                        GAM[:, dc:dc + 1], BET[:, dc:dc + 1], ALU.mult, ALU.add),
                       waits=[(dve, dve.n)])

    def emit_attn_a():
        act_a_gate = {}
        k_done = {}
        g2_done = {}
        for tb in range(TB):
            o = tb * 512
            pss = PB[2][0:1, 0:512]
            g2_dve = {}
            for sc in range(NKT + 2):
              if sc < NKT:
                wsb = WS[sc % 2]
                wsem = ws_s[sc % 2]
                dma_g(wsb[:, 0:1024].rearrange("p (c f) -> p c f", c=DC),
                      ktp[:, sc * 128:(sc + 1) * 128].rearrange("(c p) f -> p c f", p=128),
                      wsem, waits=[(pe, k_done.get((tb, sc - 2),
                                                  k_done.get((tb - 1, NKT - 2 + (sc % 2)), 0))),
                                   (wsem, max(0, wsem.n - 16))])
                ldw = wsem.n
                pscore = PB[sc % 2][:, 0:512]
                for dc in range(DC):
                    T(lambda t, dc=dc, wsb=wsb, pscore=pscore, o=o: t.matmul(
                        pscore, wsb[:, dc * 128:(dc + 1) * 128], Hc(dc, o, o + 512),
                        start=(dc == 0), stop=(dc == DC - 1)),
                      waits=([(ws_s[sc % 2], ldw)] +
                             ([(act, act_a_gate[sc - 2])] if sc >= 2 else
                              [(dve, dve.n), (act, act.n)]) if dc == 0 else []),
                      incs=[pe] if dc == DC - 1 else [])
                k_done[(tb, sc)] = pe.n
              sd = sc - 2
              if sd >= 0:
                T(lambda t, sd=sd, pss=pss: t.matmul(
                    pss, ONEC[:, :], PR[:, (sd % 2) * 512:(sd % 2) * 512 + 512],
                    start=(sd == 0), stop=(sd == NKT - 1)),
                  waits=[(dve, g2_dve[sd])], incs=[pe])
                g2_done[sd] = pe.n
              if sc < NKT:
                A_(lambda s, sc=sc, pscore=pscore: s.activation(
                    WB[:, sc * 512:(sc + 1) * 512], pscore, AF.Gelu),
                   waits=[(pe, k_done[(tb, sc)])])
                act_a_gate[sc] = act.n
                V_(lambda v, sc=sc: v.tensor_mul(
                    PR[:, (sc % 2) * 512:(sc % 2) * 512 + 512],
                    WB[:, sc * 512:(sc + 1) * 512], WB[:, sc * 512:(sc + 1) * 512]),
                   waits=[(act, act.n)] + ([(pe, g2_done[sc - 2])] if sc >= 2 else [(pe, pe.n)]))
                g2_dve[sc] = dve.n
            A_(lambda s, pss=pss: s.activation(XT1[0:1, 0:512], pss, AF.Sqrt),
               waits=[(pe, pe.n), (dve, dve.n)])
            V_(lambda v: v.reciprocal(OS[0:1, 0:512], XT1[0:1, 0:512]),
               waits=[(act, act.n), (dve, dve.n)])
            V_(lambda v: v.tensor_scalar_mul(XT1[0:1, 0:512], OS[0:1, 0:512], 64.0),
               waits=[(dve, dve.n)])
            T(lambda t: t.matmul(PB[2][:, 0:512], ONEF[0:1, :], XT1[0:1, 0:512],
                                 start=True, stop=True),
              waits=[(dve, dve.n)], incs=[pe])
            V_(lambda v: v.tensor_copy(SCB[:, :], PB[2][:, 0:512]),
               waits=[(pe, pe.n)])
            scb_done = dve.n
            ctx_sc = {}
            for sc in range(NKT):
                vsb = VS[sc % 2]
                vsem = kv_s[sc % 2]
                dma_g(vsb[:, 0:1024],
                      vp[sc * 128:(sc + 1) * 128, :], vsem,
                      waits=[(pe, ctx_sc.get((tb, sc - 2), ctx_sc.get((tb - 1, NKT - 2 + (sc % 2)), 0))),
                             (vsem, max(0, vsem.n - 16))])
                ldw = vsem.n
                for vc in range(DC):
                    T(lambda t, sc=sc, vc=vc, vsb=vsb, o=o: t.matmul(
                        PB[vc][:, 0:512], vsb[:, vc * 128:(vc + 1) * 128],
                        WB[:, sc * 512:(sc + 1) * 512],
                        start=(sc == 0), stop=(sc == NKT - 1)),
                      waits=([(kv_s[sc % 2], ldw)] if vc == 0 else []) +
                            ([(dve, scb_done)] if sc == 0 and vc == 0 else []),
                      incs=[pe] if vc == DC - 1 else [])
                ctx_sc[(tb, sc)] = pe.n
            ctx_done = pe.n
            for vc in range(DC):
                V_(lambda v, vc=vc: v.tensor_tensor(XT1[:, 0:512], PB[vc][:, 0:512],
                                                    SCB[:, :], ALU.mult),
                   waits=[(pe, ctx_done), (dve, dve.n), (act, act.n)])
                V_(lambda v, vc=vc, o=o: v.tensor_tensor(
                    Xc(vc, o, o + 512), Xc(vc, o, o + 512), XT1[:, 0:512], ALU.add),
                   waits=[(dve, dve.n)])

    moe_gather_done = [0, 0]

    def emit_ffn(app, stop_after_counts=False):
        h_act = act.n
        h_dve = dve.n
        lg_dve = {}
        for ti in range(NT):
            pl = PB[4 + ti % 2][:, 0:E]
            for dc in range(DC):
                T(lambda t, ti=ti, dc=dc, pl=pl: t.matmul(
                    pl, Hc(dc, ti * 128, (ti + 1) * 128), RWS[:, dc * E:(dc + 1) * E],
                    start=(dc == 0), stop=False),
                  waits=([(dve, dve.n), (ldg, ldg.n), (act, act.n)] if dc == 0 else []))
            T(lambda t, pl=pl: t.matmul(pl, ONEB[0:1, :], RBS[:, :],
                                        start=False, stop=True), incs=[pe])
            V_(lambda v, ti=ti, pl=pl: v.tensor_copy(LG[:, ti * E:(ti + 1) * E], pl),
               waits=[(pe, pe.n), (dve, dve.n)])
            lg_dve[ti] = dve.n
        qI1, qI2, qW1, qW2, qPL1, qPL2, qPG1, qPG2, qG1, qG2 = range(10)

        def rt(q):
            return RT[:, q * NT:(q + 1) * NT]

        init_ld0 = ld0.n

        def emit_topk_tile(ti):
            Lsl = LG[:, ti * E:(ti + 1) * E]
            OH1s = OH1[:, ti * E:(ti + 1) * E]
            OH2s = OH2[:, ti * E:(ti + 1) * E]
            t0 = RSCR[:, 0:E]; t1 = RSCR[:, E:2 * E]
            c0 = RSCR[:, 3 * E:3 * E + 1]; c1 = RSCR[:, 3 * E + 1:3 * E + 2]
            V_(lambda v, Lsl=Lsl, c0=c0: v.reduce_max(c0, Lsl, axis=AX.X),
               waits=[(ld0, init_ld0), (act, act.n)])
            V_(lambda v, c0=c0, c1=c1: v.tensor_scalar_mul(c1, c0, -1.0), waits=[(dve, dve.n)])
            A_(lambda s, Lsl=Lsl, t0=t0, c1=c1: s.activation(t0, Lsl, AF.Exp, bias=c1),
               waits=[(dve, dve.n)])
            V_(lambda v, t0=t0, c0=c0: v.reduce_sum(c0, t0, axis=AX.X),
               waits=[(act, act.n)])
            V_(lambda v, c0=c0, c1=c1: v.reciprocal(c1, c0), waits=[(dve, dve.n)])
            V_(lambda v, t0=t0, c1=c1: v.tensor_scalar_mul(t0, t0, c1), waits=[(dve, dve.n)])
            V_(lambda v, t0=t0, c0=c0: v.reduce_max(c0, t0, axis=AX.X), waits=[(dve, dve.n)])
            V_(lambda v, t0=t0, c0=c0, OH1s=OH1s: v.tensor_scalar(
                OH1s, t0, c0, None, ALU.is_ge), waits=[(dve, dve.n)])
            V_(lambda v, t1=t1, OH1s=OH1s: v.tensor_mul(t1, IOTA[:, :], OH1s), waits=[(dve, dve.n)])
            V_(lambda v, ti=ti, t1=t1: v.reduce_max(rt(qI1)[:, ti:ti + 1], t1, axis=AX.X), waits=[(dve, dve.n)])
            V_(lambda v, ti=ti, c0=c0: v.tensor_copy(rt(qW1)[:, ti:ti + 1], c0), waits=[(dve, dve.n)])
            V_(lambda v, t1=t1, OH1s=OH1s: v.tensor_scalar_mul(t1, OH1s, 1e9), waits=[(dve, dve.n)])
            V_(lambda v, t0=t0, t1=t1: v.tensor_sub(t0, t0, t1), waits=[(dve, dve.n)])
            V_(lambda v, t0=t0, c0=c0: v.reduce_max(c0, t0, axis=AX.X), waits=[(dve, dve.n)])
            V_(lambda v, t0=t0, c0=c0, OH2s=OH2s: v.tensor_scalar(
                OH2s, t0, c0, None, ALU.is_ge), waits=[(dve, dve.n)])
            V_(lambda v, t1=t1, OH2s=OH2s: v.tensor_mul(t1, IOTA[:, :], OH2s), waits=[(dve, dve.n)])
            V_(lambda v, ti=ti, t1=t1: v.reduce_max(rt(qI2)[:, ti:ti + 1], t1, axis=AX.X), waits=[(dve, dve.n)])
            V_(lambda v, ti=ti, c0=c0: v.tensor_copy(rt(qW2)[:, ti:ti + 1], c0), waits=[(dve, dve.n)])

        def emit_topk_finish():
            V_(lambda v: v.tensor_add(RSCR[:, 0:NT], rt(qW1), rt(qW2)), waits=[(dve, dve.n)])
            V_(lambda v: v.tensor_scalar_add(RSCR[:, 0:NT], RSCR[:, 0:NT], 1e-9), waits=[(dve, dve.n)])
            V_(lambda v: v.reciprocal(RSCR[:, NT:2 * NT], RSCR[:, 0:NT]), waits=[(dve, dve.n)])
            V_(lambda v: v.tensor_mul(rt(qW1), rt(qW1), RSCR[:, NT:2 * NT]), waits=[(dve, dve.n)])
            V_(lambda v: v.tensor_mul(rt(qW2), rt(qW2), RSCR[:, NT:2 * NT]), waits=[(dve, dve.n)])
            for ti in range(NT):
                V_(lambda v, ti=ti: v.tensor_add(CSX[:, ti * E:(ti + 1) * E],
                                                 OH1[:, ti * E:(ti + 1) * E],
                                                 OH2[:, ti * E:(ti + 1) * E]), waits=[(dve, dve.n)])

        def emit_counts(dve_gate):
            for ti in range(NT):
                T(lambda t, ti=ti: t.matmul(PB[6][0:1, ti * E:(ti + 1) * E], ONEF[:, 0:1],
                                            CSX[:, ti * E:(ti + 1) * E],
                                            start=True, stop=True),
                  waits=([(pool, pool.n), (ld0, ld0.n), (dve, dve_gate)] if ti == 0 else []),
                  incs=[pe] if ti == NT - 1 else [])
            V_(lambda v: v.tensor_copy(XT1[0:1, 0:NT * E], PB[6][0:1, 0:NT * E]),
               waits=[(pe, pe.n), (dve, dve.n)])
            dma_g(cnt_in[app].rearrange("t e -> (t e)").rearrange("(o x) -> o x", o=1),
                  XT1[0:1, 0:NT * E], st_g, waits=[(dve, dve.n)])

        if stop_after_counts:
            for ti in range(NT):
                emit_topk_tile(ti)
            emit_topk_finish()
            emit_counts(dve.n)
            dma_s(o_cnt.rearrange("t e -> (t e)").rearrange("(o x) -> o x", o=1),
                  XT1[0:1, 0:NT * E], st_s, waits=[(dve, dve.n)])
            return
        for dc in range(DC):
            dma_s(hfm_d[dc * 128:(dc + 1) * 128, :], Hc(dc), st_s,
                  waits=[(act, h_act), (dve, h_dve)])
        def prod_sl(fc, lo, hi):
            base = fc * 2048
            bufs = [(WB, 16384), (VTILE[0], 2048), (VTILE[1], 2048),
                    (VS[0], 1024), (VS[1], 1024)]
            off = base + lo
            for b, size in bufs:
                if off < size:
                    assert off + (hi - lo) <= size, (fc, lo, hi)
                    return b[:, off:off + (hi - lo)]
                off -= size
            raise AssertionError("prod overflow")

        sh_gate = pe.n
        sh_mm = {}
        sh_sil = {}
        sh_mul = {}
        for fc in range(FC):
            wsb = WS[fc % 2]
            wsem = ws_s[fc % 2]
            G_(lambda g, fc=fc, wsb=wsb: g.dma_start(
                out=wsb[:, 0:1024].rearrange("p (c f) -> p c f", c=DC),
                in_=swg[:, fc * 128:(fc + 1) * 128].rearrange("(c p) f -> p c f", p=128)),
               waits=[(pe, sh_mm.get(fc - 2, sh_gate)), (wsem, wsem.n)], incs=[(wsem, 16)])
            G_(lambda g, fc=fc, wsb=wsb: g.dma_start(
                out=wsb[:, 1024:2048].rearrange("p (c f) -> p c f", c=DC),
                in_=swu[:, fc * 128:(fc + 1) * 128].rearrange("(c p) f -> p c f", p=128)),
               incs=[(wsem, 16)])
            ldw = wsem.n
            for th in range(2):
                st = 2 * fc + th
                p = st % 2
                pg2 = PB2[p]
                pu2 = PB2[2 + p]
                for half in range(2):
                    dst = pg2 if half == 0 else pu2
                    for dc in range(DC):
                        for q in range(2):
                            T(lambda t, fc=fc, th=th, half=half, dc=dc, q=q, wsb=wsb, dst=dst: t.matmul(
                                dst[:, q * 512:(q + 1) * 512],
                                wsb[:, half * 1024 + dc * 128:half * 1024 + (dc + 1) * 128],
                                Hc(dc, th * 1024 + q * 512, th * 1024 + (q + 1) * 512),
                                start=(dc == 0), stop=(dc == DC - 1)),
                              waits=([(ws_s[fc % 2], ldw),
                                      (dve, sh_mul.get(st - 2, dve.n if st == 0 else 0)),
                                      (act, sh_sil.get(st - 2, act.n if st == 0 else 0))]
                                     if half == 0 and dc == 0 and q == 0 else []),
                              incs=[pe] if (half == 1 and dc == DC - 1 and q == 1) else [])
                mm_done = pe.n
                if th == 1:
                    sh_mm[fc] = pe.n
                A_(lambda s, th=th, pg2=pg2: s.activation(
                    PR[:, th * 1024:(th + 1) * 1024], pg2[:, 0:1024], AF.Silu),
                   waits=[(pe, mm_done)] +
                         ([(dve, sh_mul[st - 2])] if st >= 2 else []))
                sh_sil[st] = act.n
                V_(lambda v, fc=fc, th=th, pu2=pu2: v.tensor_tensor(
                    prod_sl(fc, th * 1024, (th + 1) * 1024),
                    PR[:, th * 1024:(th + 1) * 1024], pu2[:, 0:1024], ALU.mult),
                   waits=[(act, sh_sil[st]), (dve, dve.n)],
                   incs=[dve])
                sh_mul[st] = dve.n
            if 2 * fc < NT:
                emit_topk_tile(2 * fc)
                emit_topk_tile(2 * fc + 1)
        prod_dve = dve.n
        emit_topk_finish()
        emit_counts(dve.n)
        dma_s(CNTS[:, :], i_cntall, ld0)
        dma_s(CNTL[:, :], cnt_in[app], ld0, waits=[(st_g, st_g.n)])
        ag_ld = ld0.n
        T(lambda t: t.matmul(PB[6][0:16, 0:E], LSTR16[:, :], CNTL[:, :],
                             start=True, stop=True),
          waits=[(ld0, ag_ld), (dve, dve.n)], incs=[pe])
        T(lambda t: t.matmul(PB[7][0:16, 0:E], TMASK[:, :], CNTS[:, :],
                             start=True, stop=True),
          waits=[(ld0, ag_ld)], incs=[pe])
        off_pe = pe.n
        V_(lambda v: v.tensor_copy(OS[0:16, 0:E], PB[6][0:16, 0:E]),
           waits=[(pe, off_pe), (dve, dve.n)])
        V_(lambda v: v.tensor_copy(OS[0:16, E:2 * E], PB[7][0:16, 0:E]),
           waits=[(dve, dve.n)])
        dma_s(bounce[0:1, :].rearrange("o (t e) -> (o t) e", e=E), OS[0:16, 0:E],
              st_s, waits=[(dve, dve.n)])
        dma_s(bounce[1:2, :].rearrange("o (t e) -> (o t) e", e=E), OS[0:16, E:2 * E],
              st_s, waits=[(dve, dve.n)])
        dma_s(SCB[0:1, 0:256], bounce[0:1, :], ld0, waits=[(st_s, st_s.n)])
        dma_s(OS[0:1, 0:256], bounce[1:2, :], ld0, waits=[(st_s, st_s.n)])
        offrow_ld = ld0.n
        for ti in range(NT):
            pcs = PB[4][:, 0:E]
            plo = PB[5][:, 0:E]
            pgo = PB[5][:, E:2 * E]
            T(lambda t, ti=ti, pcs=pcs: t.matmul(pcs, LSTR[:, :], OH1[:, ti * E:(ti + 1) * E],
                                                 start=True, stop=False),
              waits=[(dve, dve.n), (ld0, offrow_ld)] if ti == 0 else [(dve, dve.n)])
            T(lambda t, ti=ti, pcs=pcs: t.matmul(pcs, LSTR[:, :], OH2[:, ti * E:(ti + 1) * E],
                                                 start=False, stop=True), incs=[pe])
            T(lambda t, ti=ti, plo=plo: t.matmul(plo, ONEF[0:1, :], SCB[0:1, ti * E:(ti + 1) * E],
                                                 start=True, stop=True), incs=[pe])
            T(lambda t, ti=ti, pgo=pgo: t.matmul(pgo, ONEF[0:1, :], OS[0:1, ti * E:(ti + 1) * E],
                                                 start=True, stop=True), incs=[pe])
            pos_pe = pe.n
            V_(lambda v, ti=ti, pcs=pcs: v.tensor_copy(CSX[:, ti * E:(ti + 1) * E], pcs),
               waits=[(pe, pos_pe), (dve, dve.n)])
            t0 = RSCR[:, 0:E]; t1 = RSCR[:, E:2 * E]
            V_(lambda v, ti=ti, plo=plo, t0=t0: v.tensor_tensor(
                t0, CSX[:, ti * E:(ti + 1) * E], plo, ALU.add), waits=[(dve, dve.n)])
            V_(lambda v, ti=ti, pgo=pgo, t1=t1: v.tensor_tensor(
                t1, CSX[:, ti * E:(ti + 1) * E], pgo, ALU.add), waits=[(dve, dve.n)])
            for (k, q_pl, q_pg) in ((0, qPL1, qPG1), (1, qPL2, qPG2)):
                OHk = (OH1 if k == 0 else OH2)[:, ti * E:(ti + 1) * E]
                t2 = RSCR[:, 2 * E:3 * E]
                V_(lambda v, t0=t0, t2=t2, OHk=OHk: v.tensor_mul(t2, t0, OHk),
                   waits=[(dve, dve.n)])
                V_(lambda v, ti=ti, t2=t2, q=q_pl: v.reduce_sum(rt(q)[:, ti:ti + 1], t2, axis=AX.X),
                   waits=[(dve, dve.n)])
                V_(lambda v, t1=t1, t2=t2, OHk=OHk: v.tensor_mul(t2, t1, OHk),
                   waits=[(dve, dve.n)])
                V_(lambda v, ti=ti, t2=t2, q=q_pg: v.reduce_sum(rt(q)[:, ti:ti + 1], t2, axis=AX.X),
                   waits=[(dve, dve.n)])
        W1s = RSCR[:, 0:NT]; W2s = RSCR[:, NT:2 * NT]
        V_(lambda v: v.tensor_scalar(W1s, rt(qPG1), CAP * 1.0, None, ALU.is_lt),
           waits=[(dve, dve.n), (pool, pool.n)])
        V_(lambda v: v.tensor_scalar(W2s, rt(qPG2), CAP * 1.0, None, ALU.is_lt),
           waits=[(dve, dve.n)])
        V_(lambda v: v.tensor_mul(rt(qG1), rt(qW1), W1s), waits=[(dve, dve.n)])
        V_(lambda v: v.tensor_mul(rt(qG2), rt(qW2), W2s), waits=[(dve, dve.n)])
        for (qi, qpl, ws_, col) in ((qI1, qPL1, W1s, 0), (qI2, qPL2, W2s, 1)):
            f0 = RSCR[:, 2 * NT:3 * NT]
            V_(lambda v, qi=qi, f0=f0: v.tensor_scalar(f0, rt(qi), CLOC * 1.0, None, ALU.mult),
               waits=[(dve, dve.n)])
            V_(lambda v, qpl=qpl, f0=f0: v.tensor_add(f0, f0, rt(qpl)), waits=[(dve, dve.n)])
            f1 = RSCR[:, 3 * NT:4 * NT]
            V_(lambda v, ws_=ws_, f1=f1: v.tensor_scalar(f1, ws_, -1e9, 1e9, ALU.mult, ALU.add),
               waits=[(dve, dve.n)])
            V_(lambda v, f0=f0, f1=f1: v.tensor_add(f1, f0, f1), waits=[(dve, dve.n)])
            V_(lambda v, f1=f1: v.tensor_scalar_min(f1, f1, CTOT * 1.0), waits=[(dve, dve.n)])
            V_(lambda v, f1=f1, col=col: v.tensor_copy(RTI[:, col * NT:(col + 1) * NT], f1),
               waits=[(dve, dve.n)])
            V_(lambda v, f1=f1, col=col: v.tensor_copy(RTI[:, (2 + col) * NT:(3 + col) * NT], f1),
               waits=[(dve, dve.n)])
        flat_dve = dve.n
        for ti in range(NT):
            par = ti % 2
            dma_s_tr(YS[:, par * 1024:(par + 1) * 1024],
                     hfm_d[:, ti * 128:(ti + 1) * 128], tl_s[par],
                     waits=[(st_s, st_s.n), (mg, mg.n)])
            tl_done = tl_s[par].n
            for k in range(2):
                G_(lambda g, ti=ti, k=k, par=par: g.indirect_dma_start(
                    out=exp_in, out_offset=IndirectOffsetOnAxis(RTI[:, k * NT + ti:k * NT + ti + 1], 0),
                    in_=YS[:, par * 1024:(par + 1) * 1024], in_offset=None),
                   waits=[(tl_s[par], tl_done), (dve, flat_dve),
                          (zi, init_stg), (mg, mg.n)],
                   incs=[(mg, 16)])
        scatter_done = mg.n
        dn_mm = {}
        sh_bankfree = {}
        for pair in range(4):
            for fc in range(FC):
                wsb = WS[fc % 2]
                wsem = ws_s[fc % 2]
                G_(lambda g, pair=pair, fc=fc, wsb=wsb: g.dma_start(
                    out=wsb[:, 2048:2304].rearrange("p (c f) -> p c f", c=2),
                    in_=swd[fc * 128:(fc + 1) * 128,
                            pair * 256:(pair + 1) * 256].rearrange("p (c f) -> p c f", c=2)),
                   waits=[(pe, dn_mm.get((pair, fc - 2),
                                         dn_mm.get((pair - 1, FC - 1 - (fc % 2)), sh_mm[FC - 1]))),
                          (wsem, wsem.n)],
                   incs=[(wsem, 16)])
                ldw = wsem.n
                for d2 in range(2):
                    for tq in range(4):
                        T(lambda t, pair=pair, fc=fc, d2=d2, tq=tq, wsb=wsb: t.matmul(
                            PB[d2 * 4 + tq][:, 0:512],
                            wsb[:, 2048 + d2 * 128:2048 + (d2 + 1) * 128],
                            prod_sl(fc, tq * 512, (tq + 1) * 512),
                            start=(fc == 0), stop=(fc == FC - 1)),
                          waits=([(ws_s[fc % 2], ldw)] if d2 == 0 and tq == 0 else []) +
                                ([(dve, sh_bankfree.get(pair - 1, prod_dve))]
                                 if fc == 0 and d2 == 0 and tq == 0 else []),
                          incs=[pe] if (d2 == 1 and tq == 3) else [])
                dn_mm[(pair, fc)] = pe.n
            down_done = pe.n
            for d2 in range(2):
                dc = pair * 2 + d2
                for tq in range(4):
                    o = tq * 512
                    pa = PB[d2 * 4 + tq][:, 0:512]
                    V_(lambda v, dc=dc, o=o, pa=pa: v.tensor_tensor(
                        Xc(dc, o, o + 512), Xc(dc, o, o + 512), pa, ALU.add),
                       waits=[(pe, down_done), (dve, dve.n), (st_g, st_g.n),
                              (st_s, st_s.n)])
            dn_mm[(pair, FC - 1)] = max(dn_mm[(pair, FC - 1)], 0)
            bank_free = dve.n
            for d2 in range(2):
                dc = pair * 2 + d2
                for tq in range(4):
                    o = tq * 512
                    V_(lambda v, dc=dc, o=o: v.tensor_tensor(
                        Xc(dc, o, o + 512), Xc(dc, o, o + 512), Hc(dc, o, o + 512), ALU.add),
                       waits=[(dve, dve.n)])
            sh_bankfree[pair] = bank_free
        def egu_slot(k):
            s = k % 3
            if s < 2:
                return WS[s][:, 0:1024], WS[s][:, 1024:2048], ws_s[s]
            return VTILE[0][:, 0:1024], VTILE[0][:, 1024:2048], ws3

        def ewd_slot(k):
            s = k % 3
            if s < 2:
                return WS[s][:, 2048:3072], ws_s[s]
            return VTILE[1][:, 0:1024], ws3

        gu_mm = {}
        dnx_mm = {}
        ex_act = {}
        ex_dve = {}
        ex_yhalf = {}
        for e in range(E):
            ebuf = KTILE[e % 2]
            esem = tl_s[e % 2]
            ein_gate = gu_mm.get((e - 2, FC - 1), 0)
            for dc in range(DC):
                dma_s_tr(ebuf[:, dc * CLOC:dc * CLOC + CLOC],
                         exp_in[e * CLOC:(e + 1) * CLOC, dc * 128:(dc + 1) * 128],
                         esem,
                         waits=([(mg, scatter_done), (pe, ein_gate)] if dc == 0 else []))
            ein_ld = esem.n
            for fc in range(FC):
                k = e * FC + fc
                gsl, usl, wsem = egu_slot(k)
                if k >= 3:
                    ep, fp = divmod(k - 3, FC)
                    gu_gate = gu_mm[(ep, fp)]
                else:
                    gu_gate = (sh_mm[FC - 1], sh_mm[FC - 2],
                               dn_mm[(3, FC - 1)])[k]
                G_(lambda g, e=e, fc=fc, gsl=gsl: g.dma_start(
                    out=gsl.rearrange("p (c f) -> p c f", c=DC),
                    in_=wg[e, :, fc * 128:(fc + 1) * 128].rearrange("(c p) f -> p c f", p=128)),
                   waits=[(pe, gu_gate), (wsem, wsem.n)], incs=[(wsem, 16)])
                G_(lambda g, e=e, fc=fc, usl=usl: g.dma_start(
                    out=usl.rearrange("p (c f) -> p c f", c=DC),
                    in_=wu[e, :, fc * 128:(fc + 1) * 128].rearrange("(c p) f -> p c f", p=128)),
                   incs=[(wsem, 16)])
                ldw = wsem.n
                pg = PB[fc % 2][:, 0:CLOC]
                pu = PB[2 + fc % 2][:, 0:CLOC]
                for dc in range(DC):
                    T(lambda t, dc=dc, gsl=gsl, pg=pg, ebuf=ebuf: t.matmul(
                        pg, gsl[:, dc * 128:(dc + 1) * 128],
                        ebuf[:, dc * CLOC:dc * CLOC + CLOC],
                        start=(dc == 0), stop=(dc == DC - 1)),
                      waits=([(wsem, ldw), (esem, ein_ld),
                              (act, ex_act.get((e, fc - 2), ex_act.get((e, fc - 1), act.n))),
                              (dve, ex_dve.get((e, fc - 2), ex_dve.get(
                                  (e, fc - 1), ex_yhalf.get(e - 1, dve.n))))]
                             if dc == 0 else []),
                      incs=[pe] if dc == DC - 1 else [])
                for dc in range(DC):
                    T(lambda t, dc=dc, usl=usl, pu=pu, ebuf=ebuf: t.matmul(
                        pu, usl[:, dc * 128:(dc + 1) * 128],
                        ebuf[:, dc * CLOC:dc * CLOC + CLOC],
                        start=(dc == 0), stop=(dc == DC - 1)),
                      incs=[pe] if dc == DC - 1 else [])
                gu_done = pe.n
                gu_mm[(e, fc)] = pe.n
                A_(lambda s, fc=fc, pg=pg: s.activation(
                    PR[:, (fc % 2) * 512:(fc % 2) * 512 + CLOC], pg, AF.Silu),
                   waits=[(pe, gu_done)] +
                         ([(dve, ex_dve[(e, fc - 2)])] if (e, fc - 2) in ex_dve else []))
                ex_act[(e, fc)] = act.n
                V_(lambda v, fc=fc, pu=pu: v.tensor_tensor(
                    WB[:, fc * CLOC:(fc + 1) * CLOC],
                    PR[:, (fc % 2) * 512:(fc % 2) * 512 + CLOC], pu, ALU.mult),
                   waits=[(act, act.n), (pe, gu_done), (dve, dve.n)], incs=[dve])
                ex_dve[(e, fc)] = dve.n
            eprod_dve = dve.n
            for fc in range(FC):
                k = e * FC + fc
                wdsl, wsem = ewd_slot(k)
                if k >= 3:
                    ep, fp = divmod(k - 3, FC)
                    dn_gate = dnx_mm[(ep, fp)]
                else:
                    dn_gate = dn_mm[(3, FC - 1)]
                G_(lambda g, e=e, fc=fc, wdsl=wdsl: g.dma_start(
                    out=wdsl,
                    in_=wd[e, fc * 128:(fc + 1) * 128, :]),
                   waits=[(pe, dn_gate), (wsem, wsem.n)], incs=[(wsem, 16)])
                ldw = wsem.n
                for sc in range(4):
                    rows = 128 if sc < 3 else CLOC - 384
                    for dh in range(2):
                        T(lambda t, fc=fc, sc=sc, dh=dh, rows=rows, wdsl=wdsl: t.matmul(
                            PB[sc * 2 + dh][0:rows, 0:512],
                            WB[:, fc * CLOC + sc * 128:fc * CLOC + sc * 128 + rows],
                            wdsl[:, dh * 512:(dh + 1) * 512],
                            start=(fc == 0), stop=(fc == FC - 1)),
                          waits=([(wsem, ldw)] if sc == 0 and dh == 0 else []) +
                                ([(dve, eprod_dve)] if fc == 0 and sc == 0 and dh == 0 else []),
                          incs=[pe] if (sc == 3 and dh == 1) else [])
                dnx_mm[(e, fc)] = pe.n
            y_done = pe.n
            for sc in range(4):
                rows = 128 if sc < 3 else CLOC - 384
                V_(lambda v, sc=sc, rows=rows: v.tensor_copy(
                    YS[0:rows, 0:1024], PB[sc * 2][0:rows, 0:512]),
                   waits=[(pe, y_done), (dve, dve.n), (tl_s[0], tl_s[0].n),
                          (tl_s[1], tl_s[1].n)]) if False else None
                V_(lambda v, sc=sc, rows=rows: v.tensor_copy(
                    YS[0:rows, 0:512], PB[sc * 2][0:rows, 0:512]),
                   waits=[(pe, y_done), (dve, dve.n), (st_g, st_g.n)])
                V_(lambda v, sc=sc, rows=rows: v.tensor_copy(
                    YS[0:rows, 512:1024], PB[sc * 2 + 1][0:rows, 0:512]),
                   waits=[(dve, dve.n)])
                dma_g(y_tok[e * CLOC + sc * 128:e * CLOC + sc * 128 + rows, :],
                      YS[0:rows, 0:1024], st_g, waits=[(dve, dve.n)])
                if sc == 1:
                    ex_yhalf[e] = dve.n
        prev_tp = 0
        for ti in range(NT):
            G_(lambda g, ti=ti: g.indirect_dma_start(
                out=PR[:, 0:1024], out_offset=None,
                in_=y_tok, in_offset=IndirectOffsetOnAxis(RTI[:, 2 * NT + ti:2 * NT + ti + 1], 0)),
               waits=[(st_g, st_g.n), (dve, dve.n), (act, act.n)], incs=[(mg, 16)])
            G_(lambda g, ti=ti: g.indirect_dma_start(
                out=PR[:, 1024:2048], out_offset=None,
                in_=y_tok, in_offset=IndirectOffsetOnAxis(RTI[:, 3 * NT + ti:3 * NT + ti + 1], 0)),
               incs=[(mg, 16)])
            g_done = mg.n
            V_(lambda v, ti=ti: v.tensor_scalar(OS[:, :], PR[:, 0:1024],
                                                rt(qG1)[:, ti:ti + 1], None, ALU.mult),
               waits=[(mg, g_done), (dve, dve.n), (pe, prev_tp)])
            V_(lambda v, ti=ti: v.scalar_tensor_tensor(
                OS[:, :], PR[:, 1024:2048], rt(qG2)[:, ti:ti + 1], OS[:, :],
                ALU.mult, ALU.add),
               waits=[(dve, dve.n)], incs=[dve])
            os_dve = dve.n
            for dc in range(DC):
                T(lambda t, ti=ti, dc=dc: t.transpose(
                    PB[dc][:, 0:128], OS[:, dc * 128:(dc + 1) * 128], IDF[:, :]),
                  waits=([(dve, os_dve)] if dc == 0 else []),
                  incs=[pe] if dc == DC - 1 else [])
            tp_done = pe.n
            prev_tp = tp_done
            for dc in range(DC):
                V_(lambda v, ti=ti, dc=dc: v.tensor_tensor(
                    Xc(dc, ti * 128, (ti + 1) * 128), Xc(dc, ti * 128, (ti + 1) * 128),
                    PB[dc][:, 0:128], ALU.add),
                   waits=([(pe, tp_done)] if dc == 0 else []) + [(dve, dve.n)])
        moe_gather_done[app] = mg.n

    def emit_qk_proj(feat_lo, bias_col0, dest):
        qk_mm = {}
        qk_act = {}
        base_pe = pe.n
        for oc8 in range(8):
            wsb = WS[oc8 % 2]
            wsem = ws_s[oc8 % 2]
            G_(lambda g, oc8=oc8, wsb=wsb: g.dma_start(
                out=wsb[:, 0:1024].rearrange("p (c f) -> p c f", c=DC),
                in_=wqkT[:, feat_lo + oc8 * 128:feat_lo + (oc8 + 1) * 128]
                    .rearrange("(c p) f -> p c f", p=128)),
               waits=[(pe, qk_mm.get(oc8 - 2, base_pe)), (wsem, wsem.n),
                      (act, act.n if oc8 == 0 else 0)], incs=[(wsem, 16)])
            ldw = wsem.n
            for tb in range(TB):
                pq = PB[(oc8 % 2) * 4 + tb][:, 0:512]
                for dc in range(DC):
                    T(lambda t, oc8=oc8, tb=tb, dc=dc, wsb=wsb, pq=pq: t.matmul(
                        pq, wsb[:, dc * 128:(dc + 1) * 128],
                        Hc(dc, tb * 512, (tb + 1) * 512),
                        start=(dc == 0), stop=(dc == DC - 1)),
                      waits=([(ws_s[oc8 % 2], ldw), (act, qk_act.get(oc8 - 2, act.n)),
                              (dve, dve.n if oc8 == 0 else 0)]
                             if tb == 0 and dc == 0 else []),
                      incs=[pe] if dc == DC - 1 else [])
            mm_done = pe.n
            qk_mm[oc8] = pe.n
            for tb in range(TB):
                A_(lambda s, oc8=oc8, tb=tb: s.activation(
                    dest[:, oc8 * 2048 + tb * 512:oc8 * 2048 + (tb + 1) * 512],
                    PB[(oc8 % 2) * 4 + tb][:, 0:512], AF.Identity,
                    bias=BQ[:, bias_col0 + oc8:bias_col0 + oc8 + 1]),
                   waits=([(pe, mm_done)] if tb == 0 else []))
            qk_act[oc8] = act.n

    def emit_attnb_kv():
        emit_qk_proj(1024, 8, WB)
        kt_act = act.n
        for oc8 in range(8):
            dma_s(o_kt[oc8 * 128:(oc8 + 1) * 128, :],
                  WB[:, oc8 * 2048:(oc8 + 1) * 2048], st_s, waits=[(act, kt_act)])
        for q in range(2):
            G_(lambda g, q=q: g.dma_start(
                out=KTILE[q][:, :].rearrange("p (c f) -> p c f", c=DC),
                in_=wqkT[:, 2048 + q * 512:2048 + (q + 1) * 512]
                    .rearrange("(c p) f -> p c f", p=128)),
               waits=[(pe, pe.n), (act, act.n), (dve, dve.n)], incs=[(kv_s[q], 16)])
        G_(lambda g: g.dma_start(out=PR[0:1, 0:1024], in_=bqv_row),
           waits=[(dve, dve.n)], incs=[(ldg, 16)])
        T(lambda t: t.matmul(PB[6][:, 0:512], ONEB[0:1, :], PR[0:1, 0:512],
                             start=True, stop=True),
          waits=[(ldg, ldg.n), (act, act.n), (dve, dve.n)], incs=[pe])
        T(lambda t: t.matmul(PB[7][:, 0:512], ONEB[0:1, :], PR[0:1, 512:1024],
                             start=True, stop=True), incs=[pe])
        bv_pe = pe.n
        V_(lambda v: v.tensor_copy(PR[:, 1024:1536], PB[6][:, 0:512]),
           waits=[(pe, bv_pe), (dve, dve.n)])
        V_(lambda v: v.tensor_copy(PR[:, 1536:2048], PB[7][:, 0:512]),
           waits=[(dve, dve.n)])
        bv_dve = dve.n
        vdve = {}
        for tt in range(NT):
            b0 = PB[(tt % 2) * 2][:, 0:512]
            b1 = PB[(tt % 2) * 2 + 1][:, 0:512]
            for dc in range(DC):
                T(lambda t, tt=tt, dc=dc, b0=b0: t.matmul(
                    b0, Hc(dc, tt * 128, (tt + 1) * 128),
                    KTILE[0][:, dc * 512:(dc + 1) * 512],
                    start=(dc == 0), stop=(dc == DC - 1)),
                  waits=(([(kv_s[0], kv_s[0].n), (kv_s[1], kv_s[1].n),
                           (dve, dve.n), (act, act.n)] if tt == 0 else
                          [(dve, vdve.get(tt - 2, 0))]) if dc == 0 else []),
                  incs=[pe] if dc == DC - 1 else [])
                T(lambda t, tt=tt, dc=dc, b1=b1: t.matmul(
                    b1, Hc(dc, tt * 128, (tt + 1) * 128),
                    KTILE[1][:, dc * 512:(dc + 1) * 512],
                    start=(dc == 0), stop=(dc == DC - 1)),
                  incs=[pe] if dc == DC - 1 else [])
            vp_done = pe.n
            V_(lambda v, tt=tt, b0=b0: v.tensor_tensor(YS[:, (tt % 2) * 1024:(tt % 2) * 1024 + 512], b0,
                                                PR[:, 1024:1536], ALU.add),
               waits=[(pe, vp_done), (dve, max(bv_dve, dve.n)), (st_s, st_s.n)])
            V_(lambda v, tt=tt, b1=b1: v.tensor_tensor(YS[:, (tt % 2) * 1024 + 512:(tt % 2) * 1024 + 1024], b1,
                                                PR[:, 1536:2048], ALU.add),
               waits=[(dve, dve.n)])
            vdve[tt] = dve.n
            dma_s(o_v[tt * 128:(tt + 1) * 128, :], YS[:, (tt % 2) * 1024:(tt % 2) * 1024 + 1024], st_s,
                  waits=[(dve, dve.n)])

    def emit_attnb_attn():
        emit_qk_proj(0, 0, WB)
        agk = i_agk
        agv = i_agv
        sc_last = {}
        ctx_last = {}
        wout_mm = {}
        for qb in range(4):
            for hp in range(8):
                par = hp % 2
                ksem = kv_s[par]
                gi = qb * 8 + hp
                kt_gate = sc_last.get(gi - 2, pe.n if gi < 2 else 0)
                v_gate = ctx_last.get(gi - 1, pe.n if gi < 1 else 0)
                for rank in range(8):
                    SY(lambda sy, qb=qb, hp=hp, rank=rank, par=par: sy.dma_start(
                        out=KTILE[par].rearrange("p (j r c) -> p j r c", j=4, r=8)[:, :, rank, :],
                        in_=agk[1024 * rank + 128 * hp:1024 * rank + 128 * (hp + 1),
                                512 * qb:512 * (qb + 1)].rearrange("p (j c) -> p j c", j=4)),
                       waits=([(pe, kt_gate)] if rank == 0 else []),
                       incs=[(ksem, 16)])
                for hh in range(2):
                    h = 2 * hp + hh
                    for rank in range(8):
                        SY(lambda sy, qb=qb, h=h, hh=hh, rank=rank: sy.dma_start(
                            out=VTILE[hh].rearrange("p (j r c) -> p j r c", j=4, r=8, c=65)[:, :, rank, 0:64],
                            in_=agv[2048 * rank + 512 * qb:2048 * rank + 512 * (qb + 1),
                                    64 * h:64 * (h + 1)].rearrange("(j p) c -> p j c", p=128)),
                           waits=(([(pe, v_gate)] +
                                   ([(dve, dve.n)] if gi == 0 else []))
                                  if hh == 0 and rank == 0 else []),
                           incs=[(ksem, 16)])
                ld_done = ksem.n
                act_kc = {}
                pe_ctx = {}
                sc_pe_k = {}
                for kc in range(NKT + 2):
                    if kc < NKT:
                        pA = PB2[kc % 2][:, 0:512]
                        pB_ = PB2[kc % 2][:, 512:1024]
                        wA = [(ksem, ld_done)] if kc == 0 else []
                        if kc >= 2:
                            wA = wA + [(act, act_kc[kc - 2])]
                        if kc <= 1:
                            wA = wA + [(act, act.n), (dve, dve.n)]
                        T(lambda t, qb=qb, hp=hp, kc=kc, par=par, pA=pA: t.matmul(
                            pA, KTILE[par][0:64, kc * 128:(kc + 1) * 128],
                            WB[0:64, hp * 2048 + qb * 512:hp * 2048 + (qb + 1) * 512],
                            start=True, stop=True, tile_position=(0, 0)),
                          waits=wA, incs=[pe])
                        T(lambda t, qb=qb, hp=hp, kc=kc, par=par, pB_=pB_: t.matmul(
                            pB_, KTILE[par][64:128, kc * 128:(kc + 1) * 128],
                            WB[64:128, hp * 2048 + qb * 512:hp * 2048 + (qb + 1) * 512],
                            start=True, stop=True, tile_position=(64, 0)),
                          incs=[pe])
                        sc_pe_k[kc] = pe.n
                    kd = kc - 2
                    if kd >= 0:
                        T(lambda t, kd=kd: t.matmul(
                            PB[4][0:65, 0:512], VTILE[0][:, kd * 65:(kd + 1) * 65],
                            PR[:, (kd % 2) * 1024:(kd % 2) * 1024 + 512],
                            start=(kd == 0), stop=(kd == NKT - 1)),
                          waits=[(act, act_kc[kd])] + ([(dve, dve.n)] if kd == 0 else []))
                        T(lambda t, kd=kd: t.matmul(
                            PB[5][0:65, 0:512], VTILE[1][:, kd * 65:(kd + 1) * 65],
                            PR[:, (kd % 2) * 1024 + 512:(kd % 2) * 1024 + 1024],
                            start=(kd == 0), stop=(kd == NKT - 1)),
                          incs=[pe])
                        pe_ctx[kd] = pe.n
                    if kc < NKT:
                        A_(lambda s, kc=kc: s.activation(
                            PR[:, (kc % 2) * 1024:(kc % 2) * 1024 + 1024],
                            PB2[kc % 2][:, 0:1024], AF.Exp,
                            bias=CONSTS[:, 1:2], scale=0.125),
                           waits=[(pe, sc_pe_k[kc])] +
                                 ([(pe, pe_ctx[kc - 2])] if kc >= 2 else []))
                        act_kc[kc] = act.n
                sc_pe = sc_pe_k[NKT - 1]
                ctx_pe = pe.n
                sc_last[qb * 8 + hp] = sc_pe
                ctx_last[qb * 8 + hp] = pe.n
                V_(lambda v: v.reciprocal(XT1[0:1, 0:512], PB[4][64:65, 0:512]),
                   waits=[(pe, ctx_pe), (dve, dve.n)])
                V_(lambda v: v.reciprocal(OS[0:1, 0:512], PB[5][64:65, 0:512]),
                   waits=[(dve, dve.n)])
                rc_dve = dve.n
                T(lambda t: t.matmul(PB[6][0:64, 0:512], ONEF[0:1, 0:64], XT1[0:1, 0:512],
                                     start=True, stop=True),
                  waits=[(dve, rc_dve)], incs=[pe])
                T(lambda t: t.matmul(PB[6][64:128, 0:512], ONEF[0:1, 0:64], OS[0:1, 0:512],
                                     start=True, stop=True), incs=[pe])
                bc_pe = pe.n
                V_(lambda v: v.tensor_copy(SCB[:, 0:512], PB[6][:, 0:512]),
                   waits=[(pe, bc_pe), (dve, dve.n)])
                V_(lambda v, qb=qb, hp=hp: v.tensor_tensor(
                    HB[0:64, hp * 2048 + qb * 512:hp * 2048 + (qb + 1) * 512],
                    PB[4][0:64, 0:512], SCB[0:64, 0:512], ALU.mult),
                   waits=[(dve, dve.n)])
                V_(lambda v, qb=qb, hp=hp: v.tensor_tensor(
                    HB[64:128, hp * 2048 + qb * 512:hp * 2048 + (qb + 1) * 512],
                    PB[5][0:64, 0:512], SCB[64:128, 0:512], ALU.mult),
                   waits=[(dve, dve.n)], incs=[dve])
            o = qb * 512
            T(lambda t, o=o: t.matmul(PB[6][:, 0:512], ONEB[0:1, :], SROW[0:1, o:o + 512],
                                      start=True, stop=True),
              waits=[(dve, dve.n), (act, act.n)], incs=[pe])
            T(lambda t, o=o: t.matmul(PB[7][:, 0:512], ONEB[0:1, :],
                                      SROW[0:1, 2048 + o:2048 + o + 512],
                                      start=True, stop=True), incs=[pe])
            ab_pe = pe.n
            for oc in range(8):
                wsb = WS[oc % 2]
                wsem = ws_s[oc % 2]
                G_(lambda g, oc=oc, wsb=wsb: g.dma_start(
                    out=wsb[:, 0:1024].rearrange("p (c f) -> p c f", c=DC),
                    in_=woutT[:, oc * 128:(oc + 1) * 128].rearrange("(c p) f -> p c f", p=128)),
                   waits=[(pe, wout_mm.get((qb, oc - 2),
                                           wout_mm.get((qb - 1, 6 + (oc % 2)), pe.n if qb == 0 and oc < 2 else 0))),
                          (wsem, wsem.n)], incs=[(wsem, 16)])
                ldw = wsem.n
                po = PB[oc % 4][:, 0:512]
                for dc in range(DC):
                    T(lambda t, oc=oc, dc=dc, o=o, wsb=wsb, po=po: t.matmul(
                        po, wsb[:, dc * 128:(dc + 1) * 128],
                        HB[:, dc * 2048 + o:dc * 2048 + o + 512],
                        start=(dc == 0), stop=(dc == DC - 1)),
                      waits=([(ws_s[oc % 2], ldw), (act, act.n)] if dc == 0 else []) +
                            ([(dve, dve.n)] if oc == 0 and dc == 0 else []),
                      incs=[pe] if dc == DC - 1 else [])
                mm_done = pe.n
                wout_mm[(qb, oc)] = pe.n
                A_(lambda s, oc=oc, po=po: s.activation(XT1[:, 0:512], po, AF.Identity,
                                                        bias=BO[:, oc:oc + 1]),
                   waits=[(pe, mm_done), (dve, dve.n)])
                V_(lambda v, oc=oc, o=o: v.tensor_tensor(
                    OS[:, 0:512], Xc(oc, o, o + 512), PB[6][:, 0:512], ALU.mult),
                   waits=[(pe, ab_pe), (dve, dve.n), (act, act.n)])
                V_(lambda v: v.tensor_tensor(OS[:, 0:512], OS[:, 0:512],
                                             PB[7][:, 0:512], ALU.add),
                   waits=[(dve, dve.n)])
                V_(lambda v: v.tensor_tensor(OS[:, 0:512], OS[:, 0:512],
                                             XT1[:, 0:512], ALU.add),
                   waits=[(dve, dve.n)])
                V_(lambda v, oc=oc, o=o: v.tensor_tensor(
                    Xc(oc, o, o + 512), Xc(oc, o, o + 512), OS[:, 0:512], ALU.add),
                   waits=[(dve, dve.n)], incs=[dve])

    def emit_x_out():
        for dc in range(DC):
            dma_s(x_out[dc * 128:(dc + 1) * 128, :], Xc(dc), st_s,
                  waits=[(dve, dve.n), (pe, pe.n), (act, act.n)])
        SY(lambda s: s.nop(), waits=[(st_s, st_s.n), (st_g, st_g.n)])

    def emit_out():
        for ti in range(NT):
            for dc in range(DC):
                T(lambda t, ti=ti, dc=dc: t.transpose(
                    PB[dc][:, 0:128], Xc(dc, ti * 128, ti * 128 + 128), IDF[:, :]),
                  waits=[(ld0, ld0.n), (dve, dve.n)], incs=[pe])
            tp_done = pe.n
            for dc in range(DC):
                V_(lambda v, dc=dc: v.tensor_copy(OS[:, dc * 128:(dc + 1) * 128],
                                                  PB[dc][:, 0:128]),
                   waits=[(pe, tp_done), (st_s, st_s.n)], incs=[dve])
            dma_s(out_tok[ti * 128:(ti + 1) * 128, :], OS[:, :], st_s,
                  waits=[(dve, dve.n)])
        SY(lambda s: s.nop(), waits=[(st_s, st_s.n)])

    if mode in ("L1", "D_attn_a"):
        emit_ln()
        emit_h()
        emit_attn_a()
        if mode == "L1":
            emit_ln()
            emit_h()
            emit_ffn(0, stop_after_counts=True)
            emit_x_out()
        else:
            emit_out()
    elif mode == "L2":
        emit_ln()
        emit_h()
        emit_ffn(0)
        emit_ln()
        emit_h(extra_gamma=True)
        emit_attnb_kv()
        emit_x_out()
    elif mode == "L3":
        emit_ln()
        emit_h(extra_gamma=True)
        emit_attnb_attn()
        emit_ln()
        emit_h()
        emit_ffn(1, stop_after_counts=True)
        emit_x_out()
    elif mode == "L4":
        emit_ln()
        emit_h()
        emit_ffn(1)
        emit_out()
    else:
        raise ValueError(mode)

    P.finalize()
    es.close()
    return nc


def _prep_inputs(inputs):
    x = np.ascontiguousarray(inputs["L0_attn_a_inputs"].reshape(16384, 1024))
    ktp = np.ascontiguousarray(inputs["L0_attn_a_key_param_tokens"].T)
    vp = np.ascontiguousarray(inputs["L0_attn_a_value_param_tokens"])
    wqkT = np.ascontiguousarray(inputs["L0_attn_b_W_qkv"].T)
    woutT = np.ascontiguousarray(inputs["L0_attn_b_W_out"].T)
    shared = dict(
        ktp=ktp, vp=vp,
        gamma=inputs["L0_attn_b_ln_gamma"].reshape(8, 128).T,
        beta=inputs["L0_attn_b_ln_beta"].reshape(8, 128).T,
        wqkT=wqkT, bqkv=inputs["L0_attn_b_b_qkv"].reshape(24, 128).T,
        woutT=woutT, bout=inputs["L0_attn_b_b_out"].reshape(8, 128).T,
        rw=inputs["L0_ffn_router_w"], rb=inputs["L0_ffn_router_b"].reshape(1, 16),
        bqv_row=inputs["L0_attn_b_b_qkv"][2048:].reshape(1, 1024),
        swg=inputs["L0_ffn_sw_gate"], swu=inputs["L0_ffn_sw_up"],
        swd=inputs["L0_ffn_sw_down"],
        wg=inputs["L0_ffn_w_gate"], wu=inputs["L0_ffn_w_up"],
        wd=inputs["L0_ffn_w_down"],
    )
    shared = {k: np.ascontiguousarray(v, dtype=np.float32) for k, v in shared.items()}
    xt = x.reshape(128, 128, 1024)
    in_maps = []
    for c in range(NCORES):
        xc = np.ascontiguousarray(xt[c::8].reshape(2048, 1024).T)
        r = np.arange(128)
        gtile_of_row = (r % 16) * 8 + (r // 16)
        gi = 8 * np.arange(16) + c
        tm = (gtile_of_row[:, None] < gi[None, :]).astype(np.float32)
        m = dict(shared)
        m["x_fm"] = xc
        m["tilemask"] = tm
        in_maps.append(m)
    return in_maps


_PROG_CACHE = {}


def _prog(mode):
    if mode not in _PROG_CACHE:
        _PROG_CACHE[mode] = build_program(mode)
    return _PROG_CACHE[mode]


def kernel(**inputs):
    base_maps = _prep_inputs(inputs)
    cores = list(range(NCORES))

    r1 = run_bass_kernel_spmd(_prog("L1"), base_maps, core_ids=cores)
    cnt_all1 = np.concatenate([r1.results[c]["o_cnt"] for c in cores], axis=0)
    maps2 = []
    for c in cores:
        m = dict(base_maps[c])
        m["x_fm"] = r1.results[c]["x_out"]
        m["i_cntall"] = cnt_all1
        maps2.append(m)
    r2 = run_bass_kernel_spmd(_prog("L2"), maps2, core_ids=cores)
    agk = np.concatenate([r2.results[c]["o_kt"] for c in cores], axis=0)
    agv = np.concatenate([r2.results[c]["o_v"] for c in cores], axis=0)
    maps3 = []
    for c in cores:
        m = dict(base_maps[c])
        m["x_fm"] = r2.results[c]["x_out"]
        m["i_agk"] = agk
        m["i_agv"] = agv
        maps3.append(m)
    r3 = run_bass_kernel_spmd(_prog("L3"), maps3, core_ids=cores)
    cnt_all2 = np.concatenate([r3.results[c]["o_cnt"] for c in cores], axis=0)
    maps4 = []
    for c in cores:
        m = dict(base_maps[c])
        m["x_fm"] = r3.results[c]["x_out"]
        m["i_cntall"] = cnt_all2
        maps4.append(m)
    r4 = run_bass_kernel_spmd(_prog("L4"), maps4, core_ids=cores)
    out = np.empty((128, 128, 1024), np.float32)
    for c in cores:
        out[c::8] = r4.results[c]["out_tok"].reshape(16, 128, 1024)
    return out.reshape(4, 4096, 1024)
